# version 11
# baseline (speedup 1.0000x reference)
import contextlib
import numpy as np

import concourse.bass as bass
import concourse.mybir as mybir
from concourse.bass_utils import run_bass_kernel_spmd

F32 = mybir.dt.float32
F16 = mybir.dt.float16
BF16 = mybir.dt.bfloat16
U16 = mybir.dt.uint16
I32 = mybir.dt.int32
U8 = mybir.dt.uint8
AF = mybir.ActivationFunctionType
ALU = mybir.AluOpType

NC = 8
T_OWN = 1024
HALO = 512
T_ALL = T_OWN + HALO
D = 2048
NT = T_ALL // 128
H, HK, DH = 16, 4, 128
IFF = 5632
CAP = 20.0
EPS_LN = 1e-5
EPS_RMS = 1e-6
SEQ = 8192

ENGS = ("sync", "gpsimd", "vector", "scalar", "tensor")
OR_ON_POOL = True


class Sem:
    def __init__(self, name):
        self.name = name
        self.h = None
        self.n = 0


class Prog:
    def __init__(self):
        self.ops = {e: [] for e in ENGS}
        self.lastw = {e: {} for e in ENGS}

    def emit(self, eng, fn, waits=(), inc=None, incn=1):
        ws = []
        for s, v in waits:
            if v <= 0:
                continue
            assert v <= s.n, f"[{eng}] waits {s.name}>={v} but only {s.n} emitted"
            if self.lastw[eng].get(s.name, -1) < v:
                ws.append((s, v))
                self.lastw[eng][s.name] = v
        if inc is not None:
            inc.n += incn
        self.ops[eng].append((ws, fn, inc, incn))

    def dma(self, eng, out, in_, sem, waits=(), **kw):
        def fn(e, o=out, i=in_, kw=kw):
            return e.dma_start(out=o, in_=i, **kw)
        self.emit(eng, fn, waits, sem, 16)

    def snap(self, sems):
        return {s: s.n for s in sems}


class _StopBuild(Exception):
    pass


def build_program(debug_taps=False, stop_after=None):
    nc = bass.Bass("TRN2", target_bir_lowering=False)

    x_in = nc.dram_tensor("x", [T_ALL, D], F32, kind="ExternalInput").ap()
    wq = nc.dram_tensor("wq", [D, D], F32, kind="ExternalInput").ap()
    wk = nc.dram_tensor("wk", [HK * DH, D], F32, kind="ExternalInput").ap()
    wv = nc.dram_tensor("wv", [HK * DH, D], F32, kind="ExternalInput").ap()
    wo = nc.dram_tensor("wo", [D, D], F32, kind="ExternalInput").ap()
    gup = nc.dram_tensor("gup", [2 * IFF, D // 4], U16, kind="ExternalInput").ap()
    gus = nc.dram_tensor("gus", [16, 2 * IFF], F16, kind="ExternalInput").ap()
    dnp = nc.dram_tensor("dnp", [D, IFF // 4], U16, kind="ExternalInput").ap()
    dns = nc.dram_tensor("dns", [44, D], F16, kind="ExternalInput").ap()
    a_cos = nc.dram_tensor("a_cos", [T_ALL, 256], F16, kind="ExternalInput").ap()
    a_sin = nc.dram_tensor("a_sin", [T_ALL, 256], F16, kind="ExternalInput").ap()
    a_msk = nc.dram_tensor("a_msk", [128, 1792], BF16, kind="ExternalInput").ap()
    a_id = nc.dram_tensor("a_id", [128, 128], F16, kind="ExternalInput").ap()
    a_one = nc.dram_tensor("a_one", [128, 128], BF16, kind="ExternalInput").ap()
    a_eps = nc.dram_tensor("a_eps", [128, 2], F32, kind="ExternalInput").ap()

    y_out = nc.dram_tensor("y", [T_OWN, D], F32, kind="ExternalOutput").ap()

    whbm = nc.dram_tensor("whbm", [2 * D + 2 * HK * DH, D], F16, kind="Internal").ap()
    gus_rep = nc.dram_tensor("gus_rep", [128, 4 * 2 * IFF], F16, kind="Internal").ap()
    dns_rep = nc.dram_tensor("dns_rep", [128, 11 * D], F16, kind="Internal").ap()
    WQ0, WK0, WV0, WO0 = 0, D, D + HK * DH, D + 2 * HK * DH

    taps = {}
    if debug_taps:
        def T(name, shape, dt):
            taps[name] = nc.dram_tensor(name, shape, dt, kind="ExternalOutput").ap()
        T("t_xnnat", [128, 16 * T_ALL], F16)
        T("t_xnpi", [128, 16 * T_OWN], F16)
        T("t_qfm", [128, H * T_OWN], F16)
        T("t_kfm", [128, HK * T_ALL], F16)
        T("t_vtm", [128, NT * 512], BF16)
        T("t_attn", [128, H * T_OWN], F16)
        T("t_h", [128, 44 * T_OWN], BF16)
        T("t_y", [128, 8 * D], BF16)
        T("t_stats", [128, 12 * 12], F32)

    K = 1024
    PERM0 = 0
    G1 = 4 * K
    G2 = 52 * K
    G3 = 84 * K
    G4 = 96 * K
    G5 = 128 * K
    G6 = 160 * K
    G7 = 172 * K
    G9 = 184 * K
    ATOT = 205 * K

    prog = Prog()
    s_ld = Sem("s_ld")
    s_gld = Sem("s_gld")
    s_pe = Sem("s_pe")
    s_dve = Sem("s_dve")
    s_act = Sem("s_act")
    s_pool = Sem("s_pool")
    s_out = Sem("s_out")
    s_gout = Sem("s_gout")
    s_ld2 = Sem("s_ld2")
    SEMS = [s_ld, s_gld, s_pe, s_dve, s_act, s_pool, s_out, s_gout, s_ld2]

    es = contextlib.ExitStack()
    with es:
        arena_t = es.enter_context(nc.sbuf_tensor("arena", [128, ATOT], U8))
        arena = arena_t[:]
        st_nat = es.enter_context(nc.sbuf_tensor("st_nat", [128, 64], F32))[:]

        def view(off, dtype, ncols):
            esz = {F32: 4, F16: 2, BF16: 2, U16: 2, I32: 4, U8: 1}[dtype]
            return arena[:, off:off + ncols * esz].bitcast(dtype)

        v_mask = view(PERM0, BF16, 1792)
        v_id = view(PERM0 + 3584, F16, 128)
        v_one = view(PERM0 + 3840, BF16, 128)
        MDP = v_mask[:, 0:128]
        MDC = v_mask[:, 128:256]
        MQ0 = v_mask[:, 256:1536]

        xn_nat = view(G1, F16, 16 * T_ALL)
        attn_fm = view(G1, F16, H * T_OWN)
        xn_pi = view(G2, F16, 16 * T_OWN)
        y_bf = view(G2, BF16, 8 * D)
        cs_cos = view(G3, F16, NT * 256)
        cs_sin = view(G3 + 6 * K, F16, NT * 256)
        wT = [view(G4, F16, 16 * 512), view(G4 + 16 * K, F16, 16 * 512)]
        p_T = view(G4, BF16, 8 * 512)
        stmp = view(G4 + 8 * K, F32, 512)
        ttmp = view(G4 + 10 * K, F32, 512)
        ssum = view(G4 + 12 * K, F32, 512)
        recip = view(G4 + 14 * K, F32, 512)
        dec_g = view(G4, F16, 16 * 512)
        dec_u = view(G4 + 16 * K, F16, 16 * 512)
        wdec = [view(G4, BF16, 16 * 512), view(G4 + 16 * K, BF16, 16 * 512)]
        ln_scratch = view(G5, F32, 2048)
        xn16 = [view(G5 + 8 * K, F16, 2048), view(G5 + 12 * K, F16, 2048)]
        q_fm = view(G5, F16, H * T_OWN)
        gtmp = view(G5, F16, 8 * 512)
        scl_g = view(G5 + 8 * K, F16, 2048)
        scl_u = view(G5 + 12 * K, F16, 2048)
        pkT_g = view(G5 + 16 * K, U16, 2048)
        pkT_u = view(G5 + 20 * K, U16, 2048)
        pkT_d = view(G5, U16, 11 * 512)
        dscl = view(G5 + 11 * K, BF16, 11 * 512)
        dscl_f16 = view(G5 + 11 * K, F16, 11 * 512)
        k_fm = view(G6, F16, HK * T_ALL)
        v_tm = view(G7, BF16, NT * 512)
        xbuf = view(G9, F32, 2048)
        qtmp = [view(G9 + 8 * K, F16, 512), view(G9 + 9 * K, F16, 512)]
        rsc_a = view(G9 + 10 * K, F16, 256)
        rsc_b = view(G9 + 10 * K + 512, F16, 256)
        rout = [view(G9 + 12 * K, F16, 512), view(G9 + 13 * K, F16, 512)]
        obuf = [view(G9 + 14 * K, F32, 512), view(G9 + 16 * K, F32, 512)]
        st_m = st_nat[:, 0:16]
        v_eps = view(G9 + 19 * K, F32, 2)
        st_sq = view(G9 + 18 * K + 64, F32, 16)
        st_rms = st_nat[:, 32:40]
        dk1 = [view(G9 + 14 * K, I32, 256), view(G9 + 15 * K, I32, 256)]
        dk2 = [view(G9 + 16 * K, I32, 256), view(G9 + 17 * K, I32, 256)]

        h_offs = ([G1 + i * 2 * K for i in range(24)]
                  + [G3 + i * 2 * K for i in range(6)]
                  + [G6 + i * 2 * K for i in range(6)]
                  + [G7 + i * 2 * K for i in range(6)]
                  + [G5 + 24 * K, G5 + 26 * K])
        assert len(h_offs) == 44

        def h_chunk(c):
            return view(h_offs[c], BF16, T_OWN)

        pb = [es.enter_context(nc.psum_tensor(f"pb{i}", [128, 512], F32)) for i in range(8)]
        tsl = []
        for b in range(2):
            v16 = pb[b][:].bitcast(F16)
            for s in range(4):
                tsl.append(v16[:, s * 128:(s + 1) * 128])

        for s in SEMS:
            s.h = es.enter_context(nc.semaphore(s.name))
        blk = es.enter_context(nc.Block())

        E = prog.emit
        DMA = prog.dma

        def _cut(name):
            if stop_after == name:
                prog.emit("sync", lambda e: None,
                          [(s_out, s_out.n), (s_gout, s_gout.n),
                           (s_ld, s_ld.n), (s_gld, s_gld.n),
                           (s_pe, s_pe.n), (s_dve, s_dve.n),
                           (s_act, s_act.n), (s_pool, s_pool.n)])
                raise _StopBuild()

        try:
            DMA("sync", v_mask, a_msk, s_ld)
            DMA("sync", v_id, a_id, s_ld)
            DMA("sync", v_one, a_one, s_ld)
            DMA("sync", v_eps, a_eps, s_ld)
            DMA("sync", cs_cos.rearrange("p (t c) -> p t c", c=256),
                a_cos.rearrange("(t p) c -> p t c", p=128), s_ld)
            DMA("sync", cs_sin.rearrange("p (t c) -> p t c", c=256),
                a_sin.rearrange("(t p) c -> p t c", p=128), s_ld)
            DMA("gpsimd", whbm[WQ0:WQ0 + D, :], wq, s_gld)
            DMA("gpsimd", whbm[WK0:WK0 + HK * DH, :], wk, s_gld)
            DMA("gpsimd", whbm[WV0:WV0 + HK * DH, :], wv, s_gld)
            DMA("gpsimd", whbm[WO0:WO0 + D, :], wo, s_gld)
            gld_wq, gld_wk, gld_wv, gld_wo = 16, 32, 48, 64
            src_g = gus.rearrange("(mb g) o -> mb g o", g=4)
            src_g = src_g.unsqueeze(2).broadcast_to([4, 4, 32, 2 * IFF])
            DMA("gpsimd", gus_rep.rearrange("(g r) (mb o) -> mb g r o", r=32, mb=4),
                src_g, s_gld)
            gld_gs = s_gld.n
            src_d = dns.rearrange("(mb g) o -> mb g o", g=4)
            src_d = src_d.unsqueeze(2).broadcast_to([11, 4, 32, D])
            DMA("gpsimd", dns_rep.rearrange("(g r) (mb o) -> mb g r o", r=32, mb=11),
                src_d, s_gld)
            gld_ds = s_gld.n

            E("vector", lambda e: e.memset(st_nat, 0.0), [], s_dve)

            tb_pend = []
            tb_gdst = [None]
            tb_group = [0]
            tb_evacd = []

            def tb_flush():
                if not tb_pend:
                    return
                g = tb_group[0]
                tb_group[0] += 1
                bank = g % 2
                n = len(tb_pend)
                grp_waits = []
                if g >= 2:
                    grp_waits.append((s_dve, tb_evacd[g - 2]))
                for idx, (src_ap, ew) in enumerate(tb_pend):
                    slot = pb[bank][:].bitcast(F16)[:, idx * 128:(idx + 1) * 128]
                    w = list(ew) + (grp_waits if idx == 0 else [])
                    E("tensor", (lambda e, s=src_ap, o=slot: e.transpose(o, s, v_id)),
                      w, s_pe if idx == n - 1 else None)
                pev = s_pe.n
                psrc = pb[bank][:].bitcast(F16)[:, 0:n * 128].rearrange(
                    "p (n c) -> p n c", c=128)
                E("vector", (lambda e, o=tb_gdst[0], s=psrc: e.tensor_copy(o, s)),
                  [(s_pe, pev)], s_dve)
                tb_evacd.append(s_dve.n)
                tb_pend.clear()
                tb_gdst[0] = None

            def pe_transpose_group(srcs, gdst, extra_waits=()):
                assert not tb_pend
                for i, s in enumerate(srcs):
                    tb_pend.append((s, extra_waits if i == 0 else ()))
                tb_gdst[0] = gdst
                tb_flush()

            ln_pe_done = []
            xbufs = [xbuf, xbuf]
            stp = [st_nat[:, 0:16], st_nat[:, 0:16]]
            xread = [0, 0]
            for ti in range(NT):
                xb = xbufs[ti % 2]
                stm = stp[ti % 2]
                waits = [(s_dve, s_dve.n), (s_act, s_act.n)] if ti > 0 else []
                DMA("sync", xb, x_in[ti * 128:(ti + 1) * 128, :], s_ld, waits)
                ldv = s_ld.n
                E("vector", lambda e, stm=stm, xb=xb: e.reduce_sum(stm[:, 0:1], xb, axis=mybir.AxisListType.X),
                  [(s_ld, ldv)], s_dve)
                redv = s_dve.n
                E("vector", (lambda e, xb=xb: e.tensor_tensor(out=ln_scratch, in0=xb,
                                                      in1=xb, op=ALU.mult)),
                  [(s_ld, ldv)], None)
                E("vector", lambda e, stm=stm, xb=xb: e.reduce_sum(stm[:, 12:13], ln_scratch,
                                                   axis=mybir.AxisListType.X),
                  [], s_dve)
                sqv = s_dve.n
                E("vector", lambda e, stm=stm, xb=xb: e.tensor_scalar(out=stm[:, 2:3], in0=stm[:, 0:1],
                                                      scalar1=1.0 / D, scalar2=None, op0=ALU.mult),
                  [(s_dve, redv)], s_dve)
                v1 = s_dve.n
                E("vector", lambda e, stm=stm, xb=xb: e.tensor_tensor(out=stm[:, 3:4], in0=stm[:, 2:3],
                                                      in1=stm[:, 2:3], op=ALU.mult),
                  [(s_dve, v1)], s_dve)
                v2 = s_dve.n
                E("vector", lambda e, stm=stm, xb=xb: e.tensor_scalar(out=stm[:, 4:5], in0=stm[:, 12:13],
                                                      scalar1=1.0 / D, scalar2=EPS_LN,
                                                      op0=ALU.mult, op1=ALU.add),
                  [(s_dve, sqv)], s_dve)
                v3 = s_dve.n
                E("vector", lambda e, stm=stm, xb=xb: e.tensor_tensor(out=stm[:, 5:6], in0=stm[:, 4:5],
                                                      in1=stm[:, 3:4], op=ALU.subtract),
                  [(s_dve, v2), (s_dve, v3)], s_dve)
                dvev = s_dve.n
                E("scalar", lambda e, stm=stm: e.activation(stm[:, 6:7], stm[:, 5:6], AF.Ln),
                  [(s_dve, dvev)], None)
                E("scalar", lambda e, stm=stm: e.activation(stm[:, 7:8], stm[:, 6:7], AF.Exp,
                                                   scale=-0.5), [], s_act)
                actv = s_act.n
                E("vector", lambda e, stm=stm, xb=xb: e.tensor_copy(stm[:, 8:9], stm[:, 7:8]),
                  [(s_act, actv)], s_dve)
                rcv = s_dve.n
                if debug_taps:
                    E("vector", (lambda e, o=view(G9 + 20 * K, F32, 144)[:, ti * 12:(ti + 1) * 12]:
                                 e.tensor_copy(o, stm[:, 0:12])), [], None)
                xw = [(s_dve, rcv)]
                if ti >= 2:
                    xw.append((s_pe, ln_pe_done[ti - 2]))
                xnb = xn16[ti % 2]
                mb_ = stm[:, 2:3].to_broadcast([128, D])
                rb_ = stm[:, 8:9].to_broadcast([128, D])
                E("vector", (lambda e, mb_=mb_, xb=xb: e.tensor_tensor(
                    out=xb, in0=xb, in1=mb_, op=ALU.subtract)),
                  xw, None)
                E("vector", (lambda e, xnb=xnb, rb_=rb_, xb=xb: e.tensor_tensor(
                    out=xnb, in0=xb, in1=rb_, op=ALU.mult)),
                  [], s_dve)
                xnv = s_dve.n
                xn_nat3 = xn_nat.rearrange("p (c t) -> p c t", t=T_ALL)
                for h8 in range(2):
                    srcs = [xnb[:, c * 128:(c + 1) * 128] for c in range(8 * h8, 8 * h8 + 8)]
                    gdst = xn_nat3[:, 8 * h8:8 * h8 + 8, ti * 128:(ti + 1) * 128]
                    pe_transpose_group(srcs, gdst,
                                       extra_waits=([(s_dve, xnv)] if h8 == 0 else ()))
                if ti >= 4:
                    to = ti - 4
                    xn_pi3 = xn_pi.rearrange("p (c t) -> p c t", t=T_OWN)
                    for h8 in range(2):
                        srcs = []
                        for c in range(8 * h8, 8 * h8 + 8):
                            mb, j = c // 4, c % 4
                            srcs.append(xnb[:, 512 * mb + j: 512 * mb + j + 509: 4])
                        gdst = xn_pi3[:, 8 * h8:8 * h8 + 8, to * 128:(to + 1) * 128]
                        pe_transpose_group(srcs, gdst)
                xread[ti % 2] = s_dve.n
                ln_pe_done.append(s_pe.n)

            ln_end = prog.snap(SEMS)
            if debug_taps:
                DMA("sync", taps["t_xnnat"], xn_nat, s_out, [(s_dve, s_dve.n)])
                DMA("sync", taps["t_xnpi"], xn_pi, s_out, [(s_dve, s_dve.n)])
                DMA("sync", taps["t_stats"], view(G9 + 20 * K, F32, 144), s_out, [])
            _cut("ln")

            ogs = [("q", WQ0 + 512 * g, True, g) for g in range(4)]
            ogs.append(("k", WK0, True, 0))
            ogs.append(("v", WV0, False, 0))

            wt_loaded = []
            wt_lastuse = []

            def load_wT(i, row0):
                cast_gate = gld_wv if row0 >= WV0 else (gld_wk if row0 >= WK0 else gld_wq)
                waits = [(s_gld, cast_gate)]
                if i == 0:
                    pass
                if i >= 2:
                    waits.append((s_pe, wt_lastuse[i - 2]))
                buf = wT[i % 2]
                for c in range(16):
                    DMA("sync", buf[:, c * 512:(c + 1) * 512],
                        whbm[row0:row0 + 512, c * 128:(c + 1) * 128], s_ld,
                        waits if c == 0 else [], transpose=True)
                wt_loaded.append((s_ld.n, s_ld2.n))

            load_wT(0, ogs[0][1])
            mm_banks = (2, 3)
            bank_free = {2: ln_end[s_dve], 3: ln_end[s_dve]}
            rout_lastread = [0, 0]

            pend_tr = [None]

            def flush_pend_tr():
                if pend_tr[0] is None:
                    return
                srcs, gdst, rv, par = pend_tr[0]
                pend_tr[0] = None
                pe_transpose_group(srcs, gdst, extra_waits=[(s_dve, rv)])
                rout_lastread[par] = s_pe.n

            for i, (kind, row0, dorope, gidx) in enumerate(ogs):
                if i + 1 < len(ogs):
                    load_wT(i + 1, ogs[i + 1][1])
                buf = wT[i % 2]
                tlist = list(range(4, 12)) if kind == "q" else list(range(NT))
                for tj, ti in enumerate(tlist):
                    bank = mm_banks[tj % 2]
                    bw = [(s_ld, wt_loaded[i][0]), (s_ld2, wt_loaded[i][1]),
                          (s_dve, bank_free[bank]),
                          (s_dve, ln_end[s_dve]), (s_act, ln_end[s_act])]
                    for c in range(16):
                        lhs = xn_nat[:, c * T_ALL + ti * 128: c * T_ALL + (ti + 1) * 128]
                        rhs = buf[:, c * 512:(c + 1) * 512]
                        E("tensor", (lambda e, o=pb[bank][:], l=lhs, r=rhs,
                                     st=(c == 0), sp=(c == 15):
                                     e.matmul(o, lhsT=l, rhs=r, start=st, stop=sp)),
                          bw if c == 0 else [], s_pe if c == 15 else None)
                    pev = s_pe.n
                    flush_pend_tr()
                    if kind == "v":
                        dst = v_tm[:, ti * 512:(ti + 1) * 512]
                        E("vector", (lambda e, o=dst, s=pb[bank][:]: e.tensor_copy(o, s)),
                          [(s_pe, pev)], s_dve)
                        bank_free[bank] = s_dve.n
                    else:
                        qt = qtmp[tj % 2]
                        E("vector", (lambda e, o=qt, s=pb[bank][:]: e.tensor_copy(o, s)),
                          [(s_pe, pev)], s_dve)
                        bank_free[bank] = s_dve.n
                        cosb = cs_cos[:, ti * 256:(ti + 1) * 256].rearrange("p (h c) -> p h c", h=4)
                        sinb = cs_sin[:, ti * 256:(ti + 1) * 256].rearrange("p (h c) -> p h c", h=4)
                        q3 = qt.rearrange("p (h c two) -> p h c two", h=4, two=2)
                        ro = rout[tj % 2]
                        r3 = ro.rearrange("p (h c two) -> p h c two", h=4, two=2)
                        s3a = rsc_a.rearrange("p (h c) -> p h c", h=4)
                        s3b = rsc_b.rearrange("p (h c) -> p h c", h=4)
                        rw = [(s_pe, rout_lastread[tj % 2])]
                        E("vector", (lambda e, a=q3, b=cosb, o=s3a: e.tensor_tensor(
                            out=o, in0=a[:, :, :, 0], in1=b, op=ALU.mult)), rw, None)
                        E("vector", (lambda e, a=q3, b=sinb, o=s3b: e.tensor_tensor(
                            out=o, in0=a[:, :, :, 1], in1=b, op=ALU.mult)), [], None)
                        E("vector", (lambda e, a=s3a, b=s3b, o=r3: e.tensor_tensor(
                            out=o[:, :, :, 0], in0=a, in1=b, op=ALU.subtract)), [], None)
                        E("vector", (lambda e, a=q3, b=sinb, o=s3a: e.tensor_tensor(
                            out=o, in0=a[:, :, :, 0], in1=b, op=ALU.mult)), [], None)
                        E("vector", (lambda e, a=q3, b=cosb, o=s3b: e.tensor_tensor(
                            out=o, in0=a[:, :, :, 1], in1=b, op=ALU.mult)), [], None)
                        E("vector", (lambda e, a=s3a, b=s3b, o=r3: e.tensor_tensor(
                            out=o[:, :, :, 1], in0=a, in1=b, op=ALU.add)), [], s_dve)
                        rv = s_dve.n
                        srcs = [ro[:, hh * 128:(hh + 1) * 128] for hh in range(4)]
                        if kind == "q":
                            q_fm3 = q_fm.rearrange("p (c t) -> p c t", t=T_OWN)
                            gdst = q_fm3[:, 4 * gidx:4 * gidx + 4,
                                         (ti - 4) * 128:(ti - 3) * 128]
                        else:
                            k_fm3 = k_fm.rearrange("p (c t) -> p c t", t=T_ALL)
                            gdst = k_fm3[:, 0:4, ti * 128:(ti + 1) * 128]
                        pend_tr[0] = (srcs, gdst, rv, tj % 2)
                wt_lastuse.append(s_pe.n)
            flush_pend_tr()

            qkv_end = prog.snap(SEMS)
            if debug_taps:
                DMA("sync", taps["t_qfm"], q_fm, s_out, [(s_dve, s_dve.n)])
                DMA("sync", taps["t_kfm"], k_fm, s_out, [(s_dve, s_dve.n)])
                DMA("sync", taps["t_vtm"], v_tm, s_out, [(s_dve, s_dve.n)])
            _cut("qkv")

            p_T2 = [view(G4, BF16, 8 * 512), view(G4 + 8 * K, BF16, 8 * 512)]
            stmp2 = [view(G4 + 16 * K, F32, 512), view(G4 + 18 * K, F32, 512)]
            ttmp2 = [view(G4 + 20 * K, F32, 512), view(G4 + 22 * K, F32, 512)]
            ssum2 = [view(G4 + 24 * K, F32, 512), view(G4 + 26 * K, F32, 512)]
            recip2 = [view(G4 + 28 * K, F32, 512), view(G4 + 30 * K, F32, 512)]
            SC_SCALE = 1.0 / (CAP * float(np.sqrt(DH)))
            sb_free = {b: qkv_end[s_dve] for b in range(8)}
            pvread = [qkv_end[s_pe], qkv_end[s_pe]]
            sums_free = [qkv_end[s_dve], qkv_end[s_dve]]
            pv_free = [qkv_end[s_dve], qkv_end[s_dve]]
            last_tanh = [0, 0]
            mask_v = {}
            pv_stop = {}
            it_list = [(qb, hh) for qb in range(2) for hh in range(H)]

            def chunks_of(qb):
                return ([(qb * 4 + ci, 0, ci) for ci in (3, 2, 1, 0)]
                        + [(qb * 4 + 4 + ci, 1, ci) for ci in range(4)])

            def emit_front(it):
                qb, hh = it_list[it]
                par = it % 2
                kvh = hh // 4
                qbase = hh * T_OWN + qb * 512
                for cidx, (tkt, kind, ci) in enumerate(chunks_of(qb)):
                    if kind == 0:
                        c0, nq = 0, 128 * (ci + 1)
                    else:
                        c0, nq = 128 * ci, 512 - 128 * ci
                    sbk = 4 * par + cidx % 2
                    lhs = k_fm[:, kvh * T_ALL + tkt * 128: kvh * T_ALL + (tkt + 1) * 128]
                    rhs = q_fm[:, qbase + c0: qbase + c0 + nq]
                    E("tensor", (lambda e, o=pb[sbk][:, 0:nq], l=lhs, r=rhs:
                                 e.matmul(o, lhsT=l, rhs=r, start=True, stop=True,
                                          skip_group_check=True)),
                      [(s_dve, sb_free[sbk])], s_pe)
                    pev = s_pe.n
                    E("vector", (lambda e, o=stmp2[par][:, 0:nq], s=pb[sbk][:, 0:nq]:
                                 e.tensor_copy(o, s)),
                      [(s_pe, pev), (s_act, last_tanh[par])], s_dve)
                    sb_free[sbk] = s_dve.n
                    dvev = s_dve.n
                    tw = [(s_dve, dvev)]
                    if cidx == 0:
                        tw.append((s_pe, pvread[par]))
                    E("scalar", (lambda e, n=nq, par=par: e.activation(
                        ttmp2[par][:, 0:n], stmp2[par][:, 0:n], AF.Tanh, scale=SC_SCALE)),
                      tw, s_act)
                    last_tanh[par] = s_act.n
                    pslice = p_T2[par][:, cidx * 512: cidx * 512 + nq]
                    E("scalar", (lambda e, o=pslice, n=nq, par=par: e.activation(
                        o, ttmp2[par][:, 0:n], AF.Exp, scale=CAP)), [], s_act)
                    actv = s_act.n
                    if kind == 0 and qb == 0:
                        mseg = MQ0[:, 64 * ci * (ci + 1): 64 * ci * (ci + 1) + nq]
                        E("vector", (lambda e, o=pslice, m=mseg: e.tensor_tensor(
                            out=o, in0=o, in1=m, op=ALU.mult)), [(s_act, actv)], s_dve)
                    elif kind == 0:
                        dseg = pslice[:, nq - 128:nq]
                        E("vector", (lambda e, o=dseg: e.tensor_tensor(
                            out=o, in0=o, in1=MDP, op=ALU.mult)), [(s_act, actv)], s_dve)
                    else:
                        dseg = pslice[:, 0:128]
                        E("vector", (lambda e, o=dseg: e.tensor_tensor(
                            out=o, in0=o, in1=MDC, op=ALU.mult)), [(s_act, actv)], s_dve)
                    mask_v[(it, cidx)] = s_dve.n

            def emit_back(it):
                qb, hh = it_list[it]
                par = it % 2
                kvh = hh // 4
                qbase = hh * T_OWN + qb * 512
                bsum, bpv = 2 + 4 * par, 3 + 4 * par
                for cidx, (tkt, kind, ci) in enumerate(chunks_of(qb)):
                    if kind == 0:
                        c0, nq = 0, 128 * (ci + 1)
                    else:
                        c0, nq = 128 * ci, 512 - 128 * ci
                    pslice = p_T2[par][:, cidx * 512: cidx * 512 + nq]
                    stw = [(s_dve, mask_v[(it, cidx)])]
                    if cidx == 0:
                        stw += [(s_dve, sums_free[par]), (s_dve, pv_free[par])]
                    E("tensor", (lambda e, r=pslice, st=(cidx == 0), c0=c0, nq=nq, b=bsum:
                                 e.matmul(pb[b][:, c0:c0 + nq], lhsT=v_one, rhs=r,
                                          start=st, stop=False, skip_group_check=True)),
                      stw, None)
                    vsl = v_tm[:, tkt * 512 + kvh * 128: tkt * 512 + (kvh + 1) * 128]
                    E("tensor", (lambda e, r=pslice, l=vsl, st=(cidx == 0), c0=c0, nq=nq, b=bpv:
                                 e.matmul(pb[b][:, c0:c0 + nq], lhsT=l, rhs=r,
                                          start=st, stop=(cidx == 7), skip_group_check=True)),
                      [], s_pe if cidx == 7 else None)
                pv_stop[it] = s_pe.n
                pvread[par] = s_pe.n
                E("vector", (lambda e, par=par, b=bsum: e.tensor_copy(ssum2[par], pb[b][:])),
                  [(s_pe, pv_stop[it])], s_dve)
                sums_free[par] = s_dve.n
                dvev = s_dve.n
                E("scalar", (lambda e, par=par: e.activation(recip2[par], ssum2[par], AF.Ln)),
                  [(s_dve, dvev)], None)
                E("scalar", (lambda e, par=par: e.activation(recip2[par], recip2[par],
                                                             AF.Exp, scale=-1.0)), [], s_act)
                actv = s_act.n
                dst = attn_fm[:, qbase: qbase + 512]
                E("vector", (lambda e, o=dst, par=par, b=bpv: e.tensor_tensor(
                    out=o, in0=pb[b][:], in1=recip2[par], op=ALU.mult)),
                  [(s_act, actv)], s_dve)
                pv_free[par] = s_dve.n

            for it in range(len(it_list) + 1):
                if it < len(it_list):
                    emit_front(it)
                if it >= 1:
                    emit_back(it - 1)

            attn_end = prog.snap(SEMS)
            if debug_taps:
                DMA("sync", taps["t_attn"], attn_fm, s_out, [(s_dve, s_dve.n)])
            _cut("attn")

            wt_loaded2 = []
            wt_lastuse2 = []

            def load_wT2(i):
                waits = [(s_gld, gld_wo)]
                if i == 0:
                    waits += [(s_dve, attn_end[s_dve]), (s_act, attn_end[s_act]),
                              (s_pe, attn_end[s_pe])]
                if i >= 2:
                    waits.append((s_pe, wt_lastuse2[i - 2]))
                buf = wT[i % 2]
                for c in range(16):
                    DMA("sync", buf[:, c * 512:(c + 1) * 512],
                        whbm[WO0 + 512 * i:WO0 + 512 * i + 512, c * 128:(c + 1) * 128],
                        s_ld, waits if c == 0 else [], transpose=True)
                wt_loaded2.append((s_ld.n, s_ld2.n))

            load_wT2(0)
            bank_free2 = {2: attn_end[s_dve], 3: attn_end[s_dve]}
            obuf_ofree = [0, 0]

            for og in range(4):
                if og + 1 < 4:
                    load_wT2(og + 1)
                buf = wT[og % 2]
                for ti in range(8):
                    DMA("sync", xbuf[:, 0:512],
                        x_in[HALO + ti * 128: HALO + (ti + 1) * 128, og * 512:(og + 1) * 512],
                        s_ld, [(s_dve, s_dve.n)])
                    xldv = s_ld.n
                    bank = mm_banks[ti % 2]
                    bw = [(s_ld, wt_loaded2[og][0]), (s_ld2, wt_loaded2[og][1]),
                          (s_dve, bank_free2[bank])]
                    for c in range(16):
                        lhs = attn_fm[:, c * T_OWN + ti * 128: c * T_OWN + (ti + 1) * 128]
                        rhs = buf[:, c * 512:(c + 1) * 512]
                        E("tensor", (lambda e, o=pb[bank][:], l=lhs, r=rhs,
                                     st=(c == 0), sp=(c == 15):
                                     e.matmul(o, lhsT=l, rhs=r, start=st, stop=sp)),
                          bw if c == 0 else [], s_pe if c == 15 else None)
                    pev = s_pe.n
                    ob = obuf[ti % 2]
                    E("vector", (lambda e, o=ob, b=pb[bank][:]: e.tensor_tensor(
                        out=o, in0=b, in1=xbuf[:, 0:512], op=ALU.add)),
                      [(s_pe, pev), (s_ld, xldv), (s_out, obuf_ofree[ti % 2])], s_dve)
                    bank_free2[bank] = s_dve.n
                    dvev = s_dve.n
                    DMA("sync", y_out[ti * 128:(ti + 1) * 128, og * 512:(og + 1) * 512],
                        ob, s_out, [(s_dve, dvev)])
                    obuf_ofree[ti % 2] = s_out.n
                wt_lastuse2.append(s_pe.n)
            out_plain_all = s_out.n
            wo_end = prog.snap(SEMS)
            _cut("wo")

            def emit_decode_planes(items):
                done = [None] * len(items)
                orv = [0] * len(items)

                def front(i):
                    src32, sc, dst, dtype, j, w0 = items[i]
                    par = i % 2
                    s1, s2 = dk1[par], dk2[par]
                    if dtype == F16:
                        sh_m, msk_m = 9 - 4 * j, 0x0E000E00
                    else:
                        sh_m, msk_m = 6 - 4 * j, 0x01C001C0
                    op_m = ALU.logical_shift_left if sh_m >= 0 else ALU.logical_shift_right
                    E("vector", (lambda e, s=src32, o=s1, sh=abs(sh_m), m=msk_m, op=op_m:
                                 e.tensor_scalar(out=o, in0=s, scalar1=sh, scalar2=m,
                                                 op0=op, op1=ALU.bitwise_and)), list(w0), None)
                    E("vector", (lambda e, s=src32, o=s2, sh=12 - 4 * j:
                                 e.tensor_scalar(out=o, in0=s, scalar1=sh,
                                                 scalar2=-2147450880,
                                                 op0=ALU.logical_shift_left,
                                                 op1=ALU.bitwise_and)),
                      [], s_dve)
                    t2v = s_dve.n
                    if OR_ON_POOL:
                        E("gpsimd", (lambda e, a=s1, b=s2: e.tensor_tensor(
                            out=a, in0=a, in1=b, op=ALU.add)), [(s_dve, t2v)], s_pool)
                        orv[i] = s_pool.n
                    else:
                        E("vector", (lambda e, a=s1, b=s2: e.tensor_tensor(
                            out=a, in0=a, in1=b, op=ALU.bitwise_or)), [], None)

                def back(i):
                    src32, sc, dst, dtype, j, w0 = items[i]
                    par = i % 2
                    s1 = dk1[par]
                    w = [(s_pool, orv[i])] if OR_ON_POOL else []
                    E("vector", (lambda e, o=dst, a=s1.bitcast(dtype), b=sc:
                                 e.tensor_tensor(out=o, in0=a, in1=b, op=ALU.mult)),
                      w, s_dve)
                    done[i] = s_dve.n

                for i in range(len(items)):
                    front(i)
                    if i >= 1:
                        back(i - 1)
                back(len(items) - 1)
                return done

            def emit_decode(pkbuf, sclbuf, decbuf, nmc, dtype, waits_first):
                items = []
                for mc in range(nmc):
                    src32 = pkbuf[:, mc * 512:(mc + 1) * 512].bitcast(I32)
                    sc = sclbuf[:, mc * 512:(mc + 1) * 512]
                    for j in range(4):
                        pl = 4 * mc + j
                        dst = decbuf[:, pl * 512:(pl + 1) * 512]
                        items.append((src32, sc, dst, dtype, j,
                                      list(waits_first) if pl == 0 else []))
                return emit_decode_planes(items)

            h_mult_done = wo_end[s_dve]
            pk_war_dve = wo_end[s_dve]
            g5_war_dve = wo_end[s_dve]
            prev_up_read = wo_end[s_dve]

            for og in range(11):
                pkw = [(s_dve, pk_war_dve)]
                if og == 0:
                    pkw += [(s_out, out_plain_all), (s_pe, attn_end[s_pe])]
                for mc in range(4):
                    DMA("sync", pkT_g[:, mc * 512:(mc + 1) * 512],
                        gup[512 * og:512 * (og + 1), mc * 128:(mc + 1) * 128], s_ld,
                        pkw if mc == 0 else [], transpose=True)
                for mc in range(4):
                    DMA("sync", pkT_u[:, mc * 512:(mc + 1) * 512],
                        gup[IFF + 512 * og:IFF + 512 * (og + 1), mc * 128:(mc + 1) * 128],
                        s_ld, [], transpose=True)
                pk_ld = s_ld.n
                pk_ld2 = s_ld2.n
                for half, (sbuf_, col0) in enumerate([(scl_g, 512 * og), (scl_u, IFF + 512 * og)]):
                    srcs = gus_rep.rearrange("p (mb o) -> p mb o", mb=4)[:, :, col0:col0 + 512]
                    DMA("sync", sbuf_.rearrange("p (mb o) -> p mb o", mb=4), srcs, s_ld,
                        ([(s_dve, pk_war_dve), (s_gld, gld_gs)] if half == 0 else []))
                scl_ld = s_ld.n
                E("vector", lambda e: e.tensor_scalar(out=scl_g, in0=scl_g, scalar1=16384.0,
                                                      scalar2=None, op0=ALU.mult),
                  [(s_ld, scl_ld)], s_dve)
                E("vector", lambda e: e.tensor_scalar(out=scl_u, in0=scl_u, scalar1=16384.0,
                                                      scalar2=None, op0=ALU.mult), [], s_dve)
                scl_rdy = s_dve.n
                dg = emit_decode(pkT_g, scl_g, dec_g, 4, F16,
                                 [(s_ld, pk_ld), (s_dve, scl_rdy), (s_dve, g5_war_dve)])
                du = emit_decode(pkT_u, scl_u, dec_u, 4, F16, [(s_ld2, pk_ld2)])
                pk_war_dve = du[-1]
                gate_stops = [0] * 8
                for jh in range(4):
                    for th in range(2):
                        b8 = jh * 2 + th
                        bk = pb[b8]
                        for kc in range(16):
                            lhs = dec_g[:, kc * 512 + jh: kc * 512 + jh + 509: 4]
                            rhs = xn_pi[:, kc * T_OWN + th * 512: kc * T_OWN + (th + 1) * 512]
                            wts = [(s_dve, dg[kc])]
                            if jh == 0 and th == 0 and kc == 0:
                                wts.append((s_dve, h_mult_done))
                            E("tensor", (lambda e, o=bk[:], l=lhs, r=rhs,
                                         st=(kc == 0), sp=(kc == 15):
                                         e.matmul(o, lhsT=l, rhs=r, start=st, stop=sp)),
                              wts, s_pe if kc == 15 else None)
                        gate_stops[b8] = s_pe.n
                gpe_b = gate_stops
                gev_b = [0] * 8
                silv_b = [0] * 8
                for t8 in range(8):
                    E("vector", (lambda e, o=gtmp[:, t8 * 512:(t8 + 1) * 512], s=pb[t8][:]:
                                 e.tensor_copy(o, s)), [(s_pe, gpe_b[t8])], s_dve)
                    gev_b[t8] = s_dve.n
                    E("scalar", (lambda e, o=gtmp[:, t8 * 512:(t8 + 1) * 512]:
                                 e.activation(o, o, AF.Silu)), [(s_dve, gev_b[t8])], s_act)
                    silv_b[t8] = s_act.n
                g5_war_dve = gev_b[7]
                upe_b = [0] * 8
                for jh in range(4):
                    for th in range(2):
                        b8 = jh * 2 + th
                        bk = pb[b8]
                        for kc in range(16):
                            lhs = dec_u[:, kc * 512 + jh: kc * 512 + jh + 509: 4]
                            rhs = xn_pi[:, kc * T_OWN + th * 512: kc * T_OWN + (th + 1) * 512]
                            wts = [(s_dve, du[kc])]
                            if kc == 0:
                                wts.append((s_dve, gev_b[b8]))
                            E("tensor", (lambda e, o=bk[:], l=lhs, r=rhs,
                                         st=(kc == 0), sp=(kc == 15):
                                         e.matmul(o, lhsT=l, rhs=r, start=st, stop=sp)),
                              wts, s_pe if kc == 15 else None)
                        upe_b[b8] = s_pe.n
                for jh in range(4):
                    for th in range(2):
                        b8 = jh * 2 + th
                        hc = h_chunk(4 * og + jh)
                        dst = hc[:, th * 512:(th + 1) * 512]
                        sil = gtmp[:, b8 * 512:(b8 + 1) * 512]
                        E("vector", (lambda e, o=dst, a=sil, b=pb[b8][:]:
                                     e.tensor_tensor(out=o, in0=a, in1=b, op=ALU.mult)),
                          [(s_pe, upe_b[b8]), (s_act, silv_b[b8])], s_dve)
                h_mult_done = s_dve.n

            h_done = s_dve.n
            if debug_taps:
                for c in range(44):
                    DMA("sync", taps["t_h"][:, c * T_OWN:(c + 1) * T_OWN], h_chunk(c),
                        s_out, [(s_dve, h_done)] if c == 0 else [])
            _cut("gate")

            dgroups = [(0, 4), (4, 4), (8, 3)]
            wdec_lastmm = [wo_end[s_pe], wo_end[s_pe]]
            down_bank_free = [h_done] * 8
            pk_war_d = h_done
            ydone = 0

            for nck in range(4):
                pw = [(s_dve, pk_war_d)]
                if nck == 0:
                    pw.append((s_act, s_act.n))
                for mc in range(11):
                    DMA("sync", pkT_d[:, mc * 512:(mc + 1) * 512],
                        dnp[512 * nck:512 * (nck + 1), mc * 128:(mc + 1) * 128],
                        s_ld, pw if mc == 0 else [], transpose=True)
                pk_ld = s_ld.n
                pk_ld2d = s_ld2.n
                srcs = dns_rep.rearrange("p (mb o) -> p mb o", mb=11)[:, :, 512 * nck:512 * (nck + 1)]
                DMA("sync", dscl_f16.rearrange("p (mb o) -> p mb o", mb=11), srcs, s_ld,
                    [(s_dve, pk_war_d), (s_gld, gld_ds)])
                dsl = s_ld.n
                E("vector", lambda e: e.tensor_scalar(out=dscl, in0=dscl_f16,
                                                      scalar1=float(2.0 ** 126),
                                                      scalar2=None, op0=ALU.mult),
                  [(s_ld, dsl)], s_dve)
                dscl_rdy = s_dve.n
                grp_done_pe = []
                for gi, (mb0, nmb) in enumerate(dgroups):
                    gbuf = wdec[gi % 2]
                    ditems = []
                    for mci in range(nmb):
                        mc = mb0 + mci
                        src32 = pkT_d[:, mc * 512:(mc + 1) * 512].bitcast(I32)
                        sc = dscl[:, mc * 512:(mc + 1) * 512]
                        for j in range(4):
                            pl = 4 * mci + j
                            dst = gbuf[:, pl * 512:(pl + 1) * 512]
                            w0 = []
                            if pl == 0:
                                w0 = [(s_ld, pk_ld), (s_ld2, pk_ld2d), (s_dve, dscl_rdy),
                                      (s_pe, wdec_lastmm[gi % 2])]
                            ditems.append((src32, sc, dst, BF16, j, w0))
                    done = emit_decode_planes(ditems)
                    pk_war_d = done[-1]
                    for ti in range(8):
                        for pli in range(4 * nmb):
                            mc = mb0 + pli // 4
                            j = pli % 4
                            hc = h_chunk(4 * mc + j)
                            lhs = hc[:, ti * 128:(ti + 1) * 128]
                            rhs = gbuf[:, pli * 512:(pli + 1) * 512]
                            st = (gi == 0 and pli == 0)
                            sp = (gi == 2 and pli == 4 * nmb - 1)
                            last_of_grp = (ti == 7 and pli == 4 * nmb - 1)
                            wts = []
                            if pli == 0:
                                wts = [(s_dve, done[-1]), (s_dve, h_done)]
                                if gi == 0 and ti < 8:
                                    wts.append((s_dve, down_bank_free[ti]))
                            E("tensor", (lambda e, o=pb[ti][:], l=lhs, r=rhs, st=st, sp=sp:
                                         e.matmul(o, lhsT=l, rhs=r, start=st, stop=sp,
                                                  skip_group_check=True)),
                              wts, s_pe if (sp or last_of_grp) else None)
                    wdec_lastmm[gi % 2] = s_pe.n
                dpe = s_pe.n
                for ti in range(8):
                    dst = y_bf[:, ti * D + nck * 512: ti * D + (nck + 1) * 512]
                    E("vector", (lambda e, o=dst, s=pb[ti][:]: e.tensor_copy(o, s)),
                      [(s_pe, dpe)], s_dve)
                    down_bank_free[ti] = s_dve.n
                ydone = s_dve.n

            if debug_taps:
                DMA("sync", taps["t_y"], y_bf, s_out, [(s_dve, ydone)])

            obuf_gfree = [0, 0]
            rms_read = 0
            for ti in range(8):
                E("vector", (lambda e, i=y_bf[:, ti * D:(ti + 1) * D]:
                             e.tensor_tensor(out=xbuf, in0=i, in1=i, op=ALU.mult)),
                  [(s_dve, ydone), (s_dve, rms_read)], None)
                E("vector", lambda e: e.reduce_sum(st_rms[:, 7:8], xbuf,
                                                   axis=mybir.AxisListType.X),
                  [], s_dve)
                sqv = s_dve.n
                E("vector", lambda e: e.tensor_scalar(out=st_rms[:, 1:2], in0=st_rms[:, 7:8],
                                                      scalar1=1.0 / D, scalar2=EPS_RMS,
                                                      op0=ALU.mult, op1=ALU.add),
                  [(s_dve, sqv)], s_dve)
                rms_read = s_dve.n
                E("scalar", lambda e: e.activation(st_rms[:, 2:3], st_rms[:, 1:2], AF.Ln),
                  [(s_dve, rms_read)], None)
                E("scalar", lambda e: e.activation(st_rms[:, 3:4], st_rms[:, 2:3], AF.Exp,
                                                   scale=-0.5), [], s_act)
                rsv = s_act.n
                E("vector", lambda e: e.tensor_copy(st_rms[:, 4:5], st_rms[:, 3:4]),
                  [(s_act, rsv)], s_dve)
                rcv2 = s_dve.n
                rb2 = st_rms[:, 4:5].to_broadcast([128, 512])
                for ncx in range(4):
                    par = ncx % 2
                    ob = obuf[par]
                    E("vector", (lambda e, o=ob, rb2=rb2, i=y_bf[:, ti * D + ncx * 512:
                                                        ti * D + (ncx + 1) * 512]:
                                 e.tensor_tensor(out=o, in0=i, in1=rb2, op=ALU.mult)),
                      [(s_gout, obuf_gfree[par]), (s_dve, rcv2)], s_dve)
                    dv2 = s_dve.n
                    DMA("gpsimd", y_out[ti * 128:(ti + 1) * 128, ncx * 512:(ncx + 1) * 512],
                        ob, s_gout, [(s_dve, dv2), (s_out, out_plain_all)],
                        accum_op=ALU.add)
                    obuf_gfree[par] = s_gout.n

            prog.emit("sync", lambda e: None,
                      [(s_out, s_out.n), (s_gout, s_gout.n),
                       (s_ld, s_ld.n), (s_gld, s_gld.n), (s_ld2, s_ld2.n)])

        except _StopBuild:
            pass

        def replay(engobj, oplist):
            for ws, fn, inc, incn in oplist:
                for s, v in ws:
                    engobj.wait_ge(s.h, v)
                ins = fn(engobj)
                if inc is not None:
                    ins.then_inc(inc.h, incn)

        @blk.sync
        def _(e):
            replay(e, prog.ops["sync"])

        @blk.gpsimd
        def _(e):
            replay(e, prog.ops["gpsimd"])

        @blk.vector
        def _(e):
            replay(e, prog.ops["vector"])

        @blk.scalar
        def _(e):
            replay(e, prog.ops["scalar"])

        @blk.tensor
        def _(e):
            replay(e, prog.ops["tensor"])

    nins = {e: len(prog.ops[e]) for e in ENGS}
    return nc, taps, nins


def _aux_arrays(core):
    import ml_dtypes
    pos = np.arange(core * T_OWN - HALO, (core + 1) * T_OWN, dtype=np.float64)
    pos = np.maximum(pos, 0.0)
    inv = 1.0 / (10000.0 ** (np.arange(0, DH, 2, dtype=np.float64) / DH))
    ang = pos[:, None] * inv[None, :]
    cos4 = np.tile(np.cos(ang), (1, 4)).astype(np.float16)
    sin4 = np.tile(np.sin(ang), (1, 4)).astype(np.float16)
    p = np.arange(128)[:, None]
    a = np.arange(128)[None, :]
    mdp = (p >= a).astype(np.float32)
    mdc = (p <= a).astype(np.float32)
    mq0 = np.zeros((128, 1280), dtype=np.float32)
    if core > 0:
        col = 0
        for ci in range(4):
            nq = 128 * (ci + 1)
            seg = np.ones((128, nq), dtype=np.float32)
            seg[:, nq - 128:nq] = mdp
            mq0[:, col:col + nq] = seg
            col += nq
    msk = np.zeros((128, 1792), dtype=np.float32)
    msk[:, 0:128] = mdp
    msk[:, 128:256] = mdc
    msk[:, 256:1536] = mq0
    return (cos4, sin4, msk.astype(ml_dtypes.bfloat16),
            np.eye(128, dtype=np.float16),
            np.ones((128, 128), dtype=ml_dtypes.bfloat16))


def make_in_maps(inputs):
    x = np.asarray(inputs["L0_attn_x"], dtype=np.float32).reshape(SEQ, D)
    wq_ = np.ascontiguousarray(np.asarray(inputs["L0_attn_wq"], dtype=np.float32))
    wk_ = np.ascontiguousarray(np.asarray(inputs["L0_attn_wk"], dtype=np.float32))
    wv_ = np.ascontiguousarray(np.asarray(inputs["L0_attn_wv"], dtype=np.float32))
    wo_ = np.ascontiguousarray(np.asarray(inputs["L0_attn_wo"], dtype=np.float32))
    gup_ = np.ascontiguousarray(np.asarray(inputs["L0_ffn_gate_up_packed"],
                                           dtype=np.int32)).view(np.uint16)
    dnp_ = np.ascontiguousarray(np.asarray(inputs["L0_ffn_down_packed"],
                                           dtype=np.int32)).view(np.uint16)
    gus_ = np.ascontiguousarray(np.asarray(inputs["L0_ffn_gate_up_scales"],
                                           dtype=np.float16))
    dns_ = np.ascontiguousarray(np.asarray(inputs["L0_ffn_down_scales"],
                                           dtype=np.float16))
    in_maps = []
    for c in range(NC):
        xs = np.zeros((T_ALL, D), dtype=np.float32)
        if c == 0:
            xs[HALO:] = x[0:T_OWN]
        else:
            lo = c * T_OWN - HALO
            xs[:] = x[lo:lo + T_ALL]
        cos4, sin4, msk, ide, one = _aux_arrays(c)
        eps = np.zeros((128, 2), dtype=np.float32)
        eps[:, 0] = EPS_LN
        eps[:, 1] = EPS_RMS
        in_maps.append(dict(x=xs, wq=wq_, wk=wk_, wv=wv_, wo=wo_,
                            gup=gup_, gus=gus_, dnp=dnp_, dns=dns_,
                            a_cos=cos4, a_sin=sin4, a_msk=msk, a_id=ide, a_one=one,
                            a_eps=eps))
    return in_maps


def kernel(**inputs):
    nc, _, _ = build_program(debug_taps=False)
    in_maps = make_in_maps(inputs)
    res = run_bass_kernel_spmd(nc, in_maps, core_ids=list(range(NC)))
    out = np.concatenate([r["y"] for r in res.results], axis=0)
    return out.reshape(1, SEQ, D).astype(np.float32)


if __name__ == "__main__":
    nc, taps, nins = build_program(debug_taps=False)
    print("instruction counts:", nins, "total:", sum(nins.values()))


# version 12
# speedup vs baseline: 1.1650x; 1.1650x over previous
import contextlib
import numpy as np

import concourse.bass as bass
import concourse.mybir as mybir
from concourse.bass_utils import run_bass_kernel_spmd

F32 = mybir.dt.float32
F16 = mybir.dt.float16
BF16 = mybir.dt.bfloat16
U16 = mybir.dt.uint16
I32 = mybir.dt.int32
U8 = mybir.dt.uint8
AF = mybir.ActivationFunctionType
ALU = mybir.AluOpType

NC = 8
T_OWN = 1024
HALO = 512
T_ALL = T_OWN + HALO
D = 2048
NT = T_ALL // 128
H, HK, DH = 16, 4, 128
IFF = 5632
CAP = 20.0
EPS_LN = 1e-5
EPS_RMS = 1e-6
SEQ = 8192

ENGS = ("sync", "gpsimd", "vector", "scalar", "tensor")
OR_ON_POOL = True


class Sem:
    def __init__(self, name):
        self.name = name
        self.h = None
        self.n = 0


class Prog:
    def __init__(self):
        self.ops = {e: [] for e in ENGS}
        self.lastw = {e: {} for e in ENGS}

    def emit(self, eng, fn, waits=(), inc=None, incn=1):
        ws = []
        for s, v in waits:
            if v <= 0:
                continue
            assert v <= s.n, f"[{eng}] waits {s.name}>={v} but only {s.n} emitted"
            if self.lastw[eng].get(s.name, -1) < v:
                ws.append((s, v))
                self.lastw[eng][s.name] = v
        if inc is not None:
            inc.n += incn
        self.ops[eng].append((ws, fn, inc, incn))

    def dma(self, eng, out, in_, sem, waits=(), **kw):
        def fn(e, o=out, i=in_, kw=kw):
            return e.dma_start(out=o, in_=i, **kw)
        self.emit(eng, fn, waits, sem, 16)

    def snap(self, sems):
        return {s: s.n for s in sems}


class _StopBuild(Exception):
    pass


def build_program(debug_taps=False, stop_after=None):
    nc = bass.Bass("TRN2", target_bir_lowering=False)

    x_in = nc.dram_tensor("x", [T_ALL, D], F32, kind="ExternalInput").ap()
    wq = nc.dram_tensor("wq", [D, D], F32, kind="ExternalInput").ap()
    wk = nc.dram_tensor("wk", [HK * DH, D], F32, kind="ExternalInput").ap()
    wv = nc.dram_tensor("wv", [HK * DH, D], F32, kind="ExternalInput").ap()
    wo = nc.dram_tensor("wo", [D, D], F32, kind="ExternalInput").ap()
    gup = nc.dram_tensor("gup", [2 * IFF, D // 4], U16, kind="ExternalInput").ap()
    gus = nc.dram_tensor("gus", [16, 2 * IFF], F16, kind="ExternalInput").ap()
    dnp = nc.dram_tensor("dnp", [D, IFF // 4], U16, kind="ExternalInput").ap()
    dns = nc.dram_tensor("dns", [44, D], F16, kind="ExternalInput").ap()
    a_cos = nc.dram_tensor("a_cos", [T_ALL, 256], F16, kind="ExternalInput").ap()
    a_sin = nc.dram_tensor("a_sin", [T_ALL, 256], F16, kind="ExternalInput").ap()
    a_msk = nc.dram_tensor("a_msk", [128, 1792], BF16, kind="ExternalInput").ap()
    a_id = nc.dram_tensor("a_id", [128, 128], F16, kind="ExternalInput").ap()
    a_one = nc.dram_tensor("a_one", [128, 128], BF16, kind="ExternalInput").ap()
    a_eps = nc.dram_tensor("a_eps", [128, 2], F32, kind="ExternalInput").ap()

    y_out = nc.dram_tensor("y", [T_OWN, D], F32, kind="ExternalOutput").ap()

    whbm = nc.dram_tensor("whbm", [2 * D + 2 * HK * DH, D], F16, kind="Internal").ap()
    gus_rep = nc.dram_tensor("gus_rep", [128, 4 * 2 * IFF], F16, kind="Internal").ap()
    dns_rep = nc.dram_tensor("dns_rep", [128, 11 * D], F16, kind="Internal").ap()
    WQ0, WK0, WV0, WO0 = 0, D, D + HK * DH, D + 2 * HK * DH

    taps = {}
    if debug_taps:
        def T(name, shape, dt):
            taps[name] = nc.dram_tensor(name, shape, dt, kind="ExternalOutput").ap()
        T("t_xnnat", [128, 16 * T_ALL], F16)
        T("t_xnpi", [128, 16 * T_OWN], F16)
        T("t_qfm", [128, H * T_OWN], F16)
        T("t_kfm", [128, HK * T_ALL], F16)
        T("t_vtm", [128, NT * 512], BF16)
        T("t_attn", [128, H * T_OWN], F16)
        T("t_h", [128, 44 * T_OWN], BF16)
        T("t_y", [128, 8 * D], BF16)
        T("t_stats", [128, 12 * 12], F32)

    K = 1024
    PERM0 = 0
    G1 = 4 * K
    G2 = 52 * K
    G3 = 84 * K
    G4 = 96 * K
    G5 = 128 * K
    G6 = 160 * K
    G7 = 172 * K
    G9 = 184 * K
    ATOT = 205 * K

    prog = Prog()
    s_ld = Sem("s_ld")
    s_gld = Sem("s_gld")
    s_pe = Sem("s_pe")
    s_dve = Sem("s_dve")
    s_act = Sem("s_act")
    s_pool = Sem("s_pool")
    s_out = Sem("s_out")
    s_gout = Sem("s_gout")
    s_ld2 = Sem("s_ld2")
    SEMS = [s_ld, s_gld, s_pe, s_dve, s_act, s_pool, s_out, s_gout, s_ld2]

    es = contextlib.ExitStack()
    with es:
        arena_t = es.enter_context(nc.sbuf_tensor("arena", [128, ATOT], U8))
        arena = arena_t[:]
        st_nat = es.enter_context(nc.sbuf_tensor("st_nat", [128, 64], F32))[:]

        def view(off, dtype, ncols):
            esz = {F32: 4, F16: 2, BF16: 2, U16: 2, I32: 4, U8: 1}[dtype]
            return arena[:, off:off + ncols * esz].bitcast(dtype)

        v_mask = view(PERM0, BF16, 1792)
        v_id = view(PERM0 + 3584, F16, 128)
        v_one = view(PERM0 + 3840, BF16, 128)
        MDP = v_mask[:, 0:128]
        MDC = v_mask[:, 128:256]
        MQ0 = v_mask[:, 256:1536]

        xn_nat = view(G1, F16, 16 * T_ALL)
        attn_fm = view(G1, F16, H * T_OWN)
        xn_pi = view(G2, F16, 16 * T_OWN)
        y_bf = view(G2, BF16, 8 * D)
        cs_cos = view(G3, F16, NT * 256)
        cs_sin = view(G3 + 6 * K, F16, NT * 256)
        wT = [view(G4, F16, 16 * 512), view(G4 + 16 * K, F16, 16 * 512)]
        p_T = view(G4, BF16, 8 * 512)
        stmp = view(G4 + 8 * K, F32, 512)
        ttmp = view(G4 + 10 * K, F32, 512)
        ssum = view(G4 + 12 * K, F32, 512)
        recip = view(G4 + 14 * K, F32, 512)
        dec_g = view(G4, F16, 16 * 512)
        dec_u = view(G4 + 16 * K, F16, 16 * 512)
        wdec = [view(G4, BF16, 16 * 512), view(G4 + 16 * K, BF16, 16 * 512)]
        ln_scratch = view(G5, F32, 2048)
        xn16 = [view(G5 + 8 * K, F16, 2048), view(G5 + 12 * K, F16, 2048)]
        q_fm = view(G5, F16, H * T_OWN)
        gtmp = view(G5, F16, 8 * 512)
        scl_g = view(G5 + 8 * K, F16, 2048)
        scl_u = view(G5 + 12 * K, F16, 2048)
        pkT_g = view(G5 + 16 * K, U16, 2048)
        pkT_u = view(G5 + 20 * K, U16, 2048)
        pkT_d = view(G5, U16, 11 * 512)
        dscl = view(G5 + 11 * K, BF16, 11 * 512)
        dscl_f16 = view(G5 + 11 * K, F16, 11 * 512)
        k_fm = view(G6, F16, HK * T_ALL)
        v_tm = view(G7, BF16, NT * 512)
        xbuf = view(G9, F32, 2048)
        qtmp = [view(G9 + 8 * K, F16, 512), view(G9 + 9 * K, F16, 512)]
        rsc_a = view(G9 + 10 * K, F16, 256)
        rsc_b = view(G9 + 10 * K + 512, F16, 256)
        rout = [view(G9 + 12 * K, F16, 512), view(G9 + 13 * K, F16, 512)]
        obuf = [view(G9 + 14 * K, F32, 512), view(G9 + 16 * K, F32, 512)]
        st_m = st_nat[:, 0:16]
        v_eps = view(G9 + 19 * K, F32, 2)
        st_sq = view(G9 + 18 * K + 64, F32, 16)
        st_rms = st_nat[:, 32:40]
        dk1 = [view(G9 + 14 * K, I32, 256), view(G9 + 15 * K, I32, 256)]
        dk2 = [view(G9 + 16 * K, I32, 256), view(G9 + 17 * K, I32, 256)]

        h_offs = ([G1 + i * 2 * K for i in range(24)]
                  + [G3 + i * 2 * K for i in range(6)]
                  + [G6 + i * 2 * K for i in range(6)]
                  + [G7 + i * 2 * K for i in range(6)]
                  + [G5 + 24 * K, G5 + 26 * K])
        assert len(h_offs) == 44

        def h_chunk(c):
            return view(h_offs[c], BF16, T_OWN)

        pb = [es.enter_context(nc.psum_tensor(f"pb{i}", [128, 512], F32)) for i in range(8)]
        tsl = []
        for b in range(2):
            v16 = pb[b][:].bitcast(F16)
            for s in range(4):
                tsl.append(v16[:, s * 128:(s + 1) * 128])

        for s in SEMS:
            s.h = es.enter_context(nc.semaphore(s.name))
        blk = es.enter_context(nc.Block())

        E = prog.emit
        DMA = prog.dma

        def _cut(name):
            if stop_after == name:
                prog.emit("sync", lambda e: None,
                          [(s_out, s_out.n), (s_gout, s_gout.n),
                           (s_ld, s_ld.n), (s_gld, s_gld.n),
                           (s_pe, s_pe.n), (s_dve, s_dve.n),
                           (s_act, s_act.n), (s_pool, s_pool.n)])
                raise _StopBuild()

        try:
            DMA("sync", v_mask, a_msk, s_ld)
            DMA("sync", v_id, a_id, s_ld)
            DMA("sync", v_one, a_one, s_ld)
            DMA("sync", v_eps, a_eps, s_ld)
            DMA("sync", cs_cos.rearrange("p (t c) -> p t c", c=256),
                a_cos.rearrange("(t p) c -> p t c", p=128), s_ld)
            DMA("sync", cs_sin.rearrange("p (t c) -> p t c", c=256),
                a_sin.rearrange("(t p) c -> p t c", p=128), s_ld)
            DMA("gpsimd", whbm[WQ0:WQ0 + D, :], wq, s_gld)
            DMA("gpsimd", whbm[WK0:WK0 + HK * DH, :], wk, s_gld)
            DMA("gpsimd", whbm[WV0:WV0 + HK * DH, :], wv, s_gld)
            DMA("gpsimd", whbm[WO0:WO0 + D, :], wo, s_gld)
            gld_wq, gld_wk, gld_wv, gld_wo = 16, 32, 48, 64
            src_g = gus.rearrange("(mb g) o -> mb g o", g=4)
            src_g = src_g.unsqueeze(2).broadcast_to([4, 4, 32, 2 * IFF])
            DMA("gpsimd", gus_rep.rearrange("(g r) (mb o) -> mb g r o", r=32, mb=4),
                src_g, s_gld)
            gld_gs = s_gld.n
            src_d = dns.rearrange("(mb g) o -> mb g o", g=4)
            src_d = src_d.unsqueeze(2).broadcast_to([11, 4, 32, D])
            DMA("gpsimd", dns_rep.rearrange("(g r) (mb o) -> mb g r o", r=32, mb=11),
                src_d, s_gld)
            gld_ds = s_gld.n

            E("vector", lambda e: e.memset(st_nat, 0.0), [], s_dve)
            memset_v = s_dve.n

            tb_pend = []
            tb_gdst = [None]
            tb_group = [0]
            tb_evacd = []

            def tb_flush():
                if not tb_pend:
                    return
                g = tb_group[0]
                tb_group[0] += 1
                bank = g % 2
                n = len(tb_pend)
                grp_waits = []
                if g >= 2:
                    grp_waits.append((s_dve, tb_evacd[g - 2]))
                for idx, (src_ap, ew) in enumerate(tb_pend):
                    slot = pb[bank][:].bitcast(F16)[:, idx * 128:(idx + 1) * 128]
                    w = list(ew) + (grp_waits if idx == 0 else [])
                    E("tensor", (lambda e, s=src_ap, o=slot: e.transpose(o, s, v_id)),
                      w, s_pe if idx == n - 1 else None)
                pev = s_pe.n
                psrc = pb[bank][:].bitcast(F16)[:, 0:n * 128].rearrange(
                    "p (n c) -> p n c", c=128)
                E("vector", (lambda e, o=tb_gdst[0], s=psrc: e.tensor_copy(o, s)),
                  [(s_pe, pev)], s_dve)
                tb_evacd.append(s_dve.n)
                tb_pend.clear()
                tb_gdst[0] = None

            def pe_transpose_group(srcs, gdst, extra_waits=()):
                assert not tb_pend
                for i, s in enumerate(srcs):
                    tb_pend.append((s, extra_waits if i == 0 else ()))
                tb_gdst[0] = gdst
                tb_flush()

            ln_pe_done = []
            xbufs = [xbuf, xbuf]
            stp = [st_nat[:, 0:16], st_nat[:, 0:16]]
            xread = [0, 0]
            for ti in range(NT):
                xb = xbufs[ti % 2]
                stm = stp[ti % 2]
                waits = [(s_dve, s_dve.n), (s_act, s_act.n)] if ti > 0 else []
                DMA("sync", xb, x_in[ti * 128:(ti + 1) * 128, :], s_ld, waits)
                ldv = s_ld.n
                E("vector", lambda e, stm=stm, xb=xb: e.reduce_sum(stm[:, 0:1], xb, axis=mybir.AxisListType.X),
                  [(s_ld, ldv)] + ([(s_dve, memset_v)] if ti == 0 else []), s_dve)
                redv = s_dve.n
                E("vector", (lambda e, xb=xb: e.tensor_tensor(out=ln_scratch, in0=xb,
                                                      in1=xb, op=ALU.mult)),
                  [(s_ld, ldv)], None)
                E("vector", lambda e, stm=stm, xb=xb: e.reduce_sum(stm[:, 12:13], ln_scratch,
                                                   axis=mybir.AxisListType.X),
                  [], s_dve)
                sqv = s_dve.n
                E("vector", lambda e, stm=stm, xb=xb: e.tensor_scalar(out=stm[:, 2:3], in0=stm[:, 0:1],
                                                      scalar1=1.0 / D, scalar2=None, op0=ALU.mult),
                  [(s_dve, redv)], s_dve)
                v1 = s_dve.n
                E("vector", lambda e, stm=stm, xb=xb: e.tensor_tensor(out=stm[:, 3:4], in0=stm[:, 2:3],
                                                      in1=stm[:, 2:3], op=ALU.mult),
                  [(s_dve, v1)], s_dve)
                v2 = s_dve.n
                E("vector", lambda e, stm=stm, xb=xb: e.tensor_scalar(out=stm[:, 4:5], in0=stm[:, 12:13],
                                                      scalar1=1.0 / D, scalar2=EPS_LN,
                                                      op0=ALU.mult, op1=ALU.add),
                  [(s_dve, sqv)], s_dve)
                v3 = s_dve.n
                E("vector", lambda e, stm=stm, xb=xb: e.tensor_tensor(out=stm[:, 5:6], in0=stm[:, 4:5],
                                                      in1=stm[:, 3:4], op=ALU.subtract),
                  [(s_dve, v2), (s_dve, v3)], s_dve)
                dvev = s_dve.n
                E("scalar", lambda e, stm=stm: e.activation(stm[:, 6:7], stm[:, 5:6], AF.Ln),
                  [(s_dve, dvev)], None)
                E("scalar", lambda e, stm=stm: e.activation(stm[:, 7:8], stm[:, 6:7], AF.Exp,
                                                   scale=-0.5), [], s_act)
                actv = s_act.n
                E("vector", lambda e, stm=stm, xb=xb: e.tensor_copy(stm[:, 8:9], stm[:, 7:8]),
                  [(s_act, actv)], s_dve)
                rcv = s_dve.n
                if debug_taps:
                    E("vector", (lambda e, o=view(G9 + 20 * K, F32, 144)[:, ti * 12:(ti + 1) * 12]:
                                 e.tensor_copy(o, stm[:, 0:12])), [], None)
                xw = [(s_dve, rcv)]
                if ti >= 2:
                    xw.append((s_pe, ln_pe_done[ti - 2]))
                xnb = xn16[ti % 2]
                mb_ = stm[:, 2:3].to_broadcast([128, D])
                rb_ = stm[:, 8:9].to_broadcast([128, D])
                E("vector", (lambda e, mb_=mb_, xb=xb: e.tensor_tensor(
                    out=xb, in0=xb, in1=mb_, op=ALU.subtract)),
                  xw, None)
                E("vector", (lambda e, xnb=xnb, rb_=rb_, xb=xb: e.tensor_tensor(
                    out=xnb, in0=xb, in1=rb_, op=ALU.mult)),
                  [], s_dve)
                xnv = s_dve.n
                xn_nat3 = xn_nat.rearrange("p (c t) -> p c t", t=T_ALL)
                for h8 in range(2):
                    srcs = [xnb[:, c * 128:(c + 1) * 128] for c in range(8 * h8, 8 * h8 + 8)]
                    gdst = xn_nat3[:, 8 * h8:8 * h8 + 8, ti * 128:(ti + 1) * 128]
                    pe_transpose_group(srcs, gdst,
                                       extra_waits=([(s_dve, xnv)] if h8 == 0 else ()))
                if ti >= 4:
                    to = ti - 4
                    xn_pi3 = xn_pi.rearrange("p (c t) -> p c t", t=T_OWN)
                    for h8 in range(2):
                        srcs = []
                        for c in range(8 * h8, 8 * h8 + 8):
                            mb, j = c // 4, c % 4
                            srcs.append(xnb[:, 512 * mb + j: 512 * mb + j + 509: 4])
                        gdst = xn_pi3[:, 8 * h8:8 * h8 + 8, to * 128:(to + 1) * 128]
                        pe_transpose_group(srcs, gdst)
                xread[ti % 2] = s_dve.n
                ln_pe_done.append(s_pe.n)

            ln_end = prog.snap(SEMS)
            if debug_taps:
                DMA("sync", taps["t_xnnat"], xn_nat, s_out, [(s_dve, s_dve.n)])
                DMA("sync", taps["t_xnpi"], xn_pi, s_out, [(s_dve, s_dve.n)])
                DMA("sync", taps["t_stats"], view(G9 + 20 * K, F32, 144), s_out, [])
            _cut("ln")

            ogs = [("q", WQ0 + 512 * g, True, g) for g in range(4)]
            ogs.append(("k", WK0, True, 0))
            ogs.append(("v", WV0, False, 0))

            wt_loaded = []
            wt_lastuse = []

            def load_wT(i, row0):
                cast_gate = gld_wv if row0 >= WV0 else (gld_wk if row0 >= WK0 else gld_wq)
                waits = [(s_gld, cast_gate)]
                if i == 0:
                    pass
                if i >= 2:
                    waits.append((s_pe, wt_lastuse[i - 2]))
                buf = wT[i % 2]
                for c in range(16):
                    DMA("sync", buf[:, c * 512:(c + 1) * 512],
                        whbm[row0:row0 + 512, c * 128:(c + 1) * 128], s_ld,
                        waits if c == 0 else [], transpose=True)
                wt_loaded.append((s_ld.n, s_ld2.n))

            load_wT(0, ogs[0][1])
            mm_banks = (2, 3)
            bank_free = {2: ln_end[s_dve], 3: ln_end[s_dve]}
            rout_lastread = [0, 0]

            pend_tr = [None]

            def flush_pend_tr():
                if pend_tr[0] is None:
                    return
                srcs, gdst, rv, par = pend_tr[0]
                pend_tr[0] = None
                pe_transpose_group(srcs, gdst, extra_waits=[(s_dve, rv)])
                rout_lastread[par] = s_pe.n

            for i, (kind, row0, dorope, gidx) in enumerate(ogs):
                if i + 1 < len(ogs):
                    load_wT(i + 1, ogs[i + 1][1])
                buf = wT[i % 2]
                tlist = list(range(4, 12)) if kind == "q" else list(range(NT))
                for tj, ti in enumerate(tlist):
                    bank = mm_banks[tj % 2]
                    bw = [(s_ld, wt_loaded[i][0]), (s_ld2, wt_loaded[i][1]),
                          (s_dve, bank_free[bank]),
                          (s_dve, ln_end[s_dve]), (s_act, ln_end[s_act])]
                    for c in range(16):
                        lhs = xn_nat[:, c * T_ALL + ti * 128: c * T_ALL + (ti + 1) * 128]
                        rhs = buf[:, c * 512:(c + 1) * 512]
                        E("tensor", (lambda e, o=pb[bank][:], l=lhs, r=rhs,
                                     st=(c == 0), sp=(c == 15):
                                     e.matmul(o, lhsT=l, rhs=r, start=st, stop=sp)),
                          bw if c == 0 else [], s_pe if c == 15 else None)
                    pev = s_pe.n
                    flush_pend_tr()
                    if kind == "v":
                        dst = v_tm[:, ti * 512:(ti + 1) * 512]
                        E("vector", (lambda e, o=dst, s=pb[bank][:]: e.tensor_copy(o, s)),
                          [(s_pe, pev)], s_dve)
                        bank_free[bank] = s_dve.n
                    else:
                        qt = qtmp[tj % 2]
                        E("vector", (lambda e, o=qt, s=pb[bank][:]: e.tensor_copy(o, s)),
                          [(s_pe, pev)], s_dve)
                        bank_free[bank] = s_dve.n
                        cosb = cs_cos[:, ti * 256:(ti + 1) * 256].rearrange("p (h c) -> p h c", h=4)
                        sinb = cs_sin[:, ti * 256:(ti + 1) * 256].rearrange("p (h c) -> p h c", h=4)
                        q3 = qt.rearrange("p (h c two) -> p h c two", h=4, two=2)
                        ro = rout[tj % 2]
                        r3 = ro.rearrange("p (h c two) -> p h c two", h=4, two=2)
                        s3a = rsc_a.rearrange("p (h c) -> p h c", h=4)
                        s3b = rsc_b.rearrange("p (h c) -> p h c", h=4)
                        rw = [(s_pe, rout_lastread[tj % 2])]
                        E("vector", (lambda e, a=q3, b=cosb, o=s3a: e.tensor_tensor(
                            out=o, in0=a[:, :, :, 0], in1=b, op=ALU.mult)), rw, None)
                        E("vector", (lambda e, a=q3, b=sinb, o=s3b: e.tensor_tensor(
                            out=o, in0=a[:, :, :, 1], in1=b, op=ALU.mult)), [], None)
                        E("vector", (lambda e, a=s3a, b=s3b, o=r3: e.tensor_tensor(
                            out=o[:, :, :, 0], in0=a, in1=b, op=ALU.subtract)), [], None)
                        E("vector", (lambda e, a=q3, b=sinb, o=s3a: e.tensor_tensor(
                            out=o, in0=a[:, :, :, 0], in1=b, op=ALU.mult)), [], None)
                        E("vector", (lambda e, a=q3, b=cosb, o=s3b: e.tensor_tensor(
                            out=o, in0=a[:, :, :, 1], in1=b, op=ALU.mult)), [], None)
                        E("vector", (lambda e, a=s3a, b=s3b, o=r3: e.tensor_tensor(
                            out=o[:, :, :, 1], in0=a, in1=b, op=ALU.add)), [], s_dve)
                        rv = s_dve.n
                        srcs = [ro[:, hh * 128:(hh + 1) * 128] for hh in range(4)]
                        if kind == "q":
                            q_fm3 = q_fm.rearrange("p (c t) -> p c t", t=T_OWN)
                            gdst = q_fm3[:, 4 * gidx:4 * gidx + 4,
                                         (ti - 4) * 128:(ti - 3) * 128]
                        else:
                            k_fm3 = k_fm.rearrange("p (c t) -> p c t", t=T_ALL)
                            gdst = k_fm3[:, 0:4, ti * 128:(ti + 1) * 128]
                        pend_tr[0] = (srcs, gdst, rv, tj % 2)
                wt_lastuse.append(s_pe.n)
            flush_pend_tr()

            qkv_end = prog.snap(SEMS)
            if debug_taps:
                DMA("sync", taps["t_qfm"], q_fm, s_out, [(s_dve, s_dve.n)])
                DMA("sync", taps["t_kfm"], k_fm, s_out, [(s_dve, s_dve.n)])
                DMA("sync", taps["t_vtm"], v_tm, s_out, [(s_dve, s_dve.n)])
            _cut("qkv")

            p_T2 = [view(G4, BF16, 8 * 512), view(G4 + 8 * K, BF16, 8 * 512)]
            stmp2 = [view(G4 + 16 * K, F32, 512), view(G4 + 18 * K, F32, 512)]
            ttmp2 = [view(G4 + 20 * K, F32, 512), view(G4 + 22 * K, F32, 512)]
            ssum2 = [view(G4 + 24 * K, F32, 512), view(G4 + 26 * K, F32, 512)]
            recip2 = [view(G4 + 28 * K, F32, 512), view(G4 + 30 * K, F32, 512)]
            SC_SCALE = 1.0 / (CAP * float(np.sqrt(DH)))
            sb_free = {b: qkv_end[s_dve] for b in range(8)}
            pvread = [qkv_end[s_pe], qkv_end[s_pe]]
            sums_free = [qkv_end[s_dve], qkv_end[s_dve]]
            pv_free = [qkv_end[s_dve], qkv_end[s_dve]]
            last_tanh = [0, 0]
            mask_v = {}
            pv_stop = {}
            it_list = [(qb, hh) for qb in range(2) for hh in range(H)]

            def chunks_of(qb):
                return ([(qb * 4 + ci, 0, ci) for ci in (3, 2, 1, 0)]
                        + [(qb * 4 + 4 + ci, 1, ci) for ci in range(4)])

            def emit_front(it):
                qb, hh = it_list[it]
                par = it % 2
                kvh = hh // 4
                qbase = hh * T_OWN + qb * 512
                for cidx, (tkt, kind, ci) in enumerate(chunks_of(qb)):
                    if kind == 0:
                        c0, nq = 0, 128 * (ci + 1)
                    else:
                        c0, nq = 128 * ci, 512 - 128 * ci
                    sbk = 4 * par + cidx % 2
                    lhs = k_fm[:, kvh * T_ALL + tkt * 128: kvh * T_ALL + (tkt + 1) * 128]
                    rhs = q_fm[:, qbase + c0: qbase + c0 + nq]
                    E("tensor", (lambda e, o=pb[sbk][:, 0:nq], l=lhs, r=rhs:
                                 e.matmul(o, lhsT=l, rhs=r, start=True, stop=True,
                                          skip_group_check=True)),
                      [(s_dve, sb_free[sbk])], s_pe)
                    pev = s_pe.n
                    E("vector", (lambda e, o=stmp2[par][:, 0:nq], s=pb[sbk][:, 0:nq]:
                                 e.tensor_copy(o, s)),
                      [(s_pe, pev), (s_act, last_tanh[par])], s_dve)
                    sb_free[sbk] = s_dve.n
                    dvev = s_dve.n
                    tw = [(s_dve, dvev)]
                    if cidx == 0:
                        tw.append((s_pe, pvread[par]))
                    E("scalar", (lambda e, n=nq, par=par: e.activation(
                        ttmp2[par][:, 0:n], stmp2[par][:, 0:n], AF.Tanh, scale=SC_SCALE)),
                      tw, s_act)
                    last_tanh[par] = s_act.n
                    pslice = p_T2[par][:, cidx * 512: cidx * 512 + nq]
                    E("scalar", (lambda e, o=pslice, n=nq, par=par: e.activation(
                        o, ttmp2[par][:, 0:n], AF.Exp, scale=CAP)), [], s_act)
                    actv = s_act.n
                    if kind == 0 and qb == 0:
                        mseg = MQ0[:, 64 * ci * (ci + 1): 64 * ci * (ci + 1) + nq]
                        E("vector", (lambda e, o=pslice, m=mseg: e.tensor_tensor(
                            out=o, in0=o, in1=m, op=ALU.mult)), [(s_act, actv)], s_dve)
                    elif kind == 0:
                        dseg = pslice[:, nq - 128:nq]
                        E("vector", (lambda e, o=dseg: e.tensor_tensor(
                            out=o, in0=o, in1=MDP, op=ALU.mult)), [(s_act, actv)], s_dve)
                    else:
                        dseg = pslice[:, 0:128]
                        E("vector", (lambda e, o=dseg: e.tensor_tensor(
                            out=o, in0=o, in1=MDC, op=ALU.mult)), [(s_act, actv)], s_dve)
                    mask_v[(it, cidx)] = s_dve.n

            def emit_back(it):
                qb, hh = it_list[it]
                par = it % 2
                kvh = hh // 4
                qbase = hh * T_OWN + qb * 512
                bsum, bpv = 2 + 4 * par, 3 + 4 * par
                for cidx, (tkt, kind, ci) in enumerate(chunks_of(qb)):
                    if kind == 0:
                        c0, nq = 0, 128 * (ci + 1)
                    else:
                        c0, nq = 128 * ci, 512 - 128 * ci
                    pslice = p_T2[par][:, cidx * 512: cidx * 512 + nq]
                    stw = [(s_dve, mask_v[(it, cidx)])]
                    if cidx == 0:
                        stw += [(s_dve, sums_free[par]), (s_dve, pv_free[par])]
                    E("tensor", (lambda e, r=pslice, st=(cidx == 0), c0=c0, nq=nq, b=bsum:
                                 e.matmul(pb[b][:, c0:c0 + nq], lhsT=v_one, rhs=r,
                                          start=st, stop=False, skip_group_check=True)),
                      stw, None)
                    vsl = v_tm[:, tkt * 512 + kvh * 128: tkt * 512 + (kvh + 1) * 128]
                    E("tensor", (lambda e, r=pslice, l=vsl, st=(cidx == 0), c0=c0, nq=nq, b=bpv:
                                 e.matmul(pb[b][:, c0:c0 + nq], lhsT=l, rhs=r,
                                          start=st, stop=(cidx == 7), skip_group_check=True)),
                      [], s_pe if cidx == 7 else None)
                pv_stop[it] = s_pe.n
                pvread[par] = s_pe.n
                E("vector", (lambda e, par=par, b=bsum: e.tensor_copy(ssum2[par], pb[b][:])),
                  [(s_pe, pv_stop[it])], s_dve)
                sums_free[par] = s_dve.n
                dvev = s_dve.n
                E("scalar", (lambda e, par=par: e.activation(recip2[par], ssum2[par], AF.Ln)),
                  [(s_dve, dvev)], None)
                E("scalar", (lambda e, par=par: e.activation(recip2[par], recip2[par],
                                                             AF.Exp, scale=-1.0)), [], s_act)
                actv = s_act.n
                dst = attn_fm[:, qbase: qbase + 512]
                E("vector", (lambda e, o=dst, par=par, b=bpv: e.tensor_tensor(
                    out=o, in0=pb[b][:], in1=recip2[par], op=ALU.mult)),
                  [(s_act, actv)], s_dve)
                pv_free[par] = s_dve.n

            for it in range(len(it_list) + 1):
                if it < len(it_list):
                    emit_front(it)
                if it >= 1:
                    emit_back(it - 1)

            attn_end = prog.snap(SEMS)
            if debug_taps:
                DMA("sync", taps["t_attn"], attn_fm, s_out, [(s_dve, s_dve.n)])
            _cut("attn")

            wt_loaded2 = []
            wt_lastuse2 = []

            def load_wT2(i):
                waits = [(s_gld, gld_wo)]
                if i == 0:
                    waits += [(s_dve, attn_end[s_dve]), (s_act, attn_end[s_act]),
                              (s_pe, attn_end[s_pe])]
                if i >= 2:
                    waits.append((s_pe, wt_lastuse2[i - 2]))
                buf = wT[i % 2]
                for c in range(16):
                    DMA("sync", buf[:, c * 512:(c + 1) * 512],
                        whbm[WO0 + 512 * i:WO0 + 512 * i + 512, c * 128:(c + 1) * 128],
                        s_ld, waits if c == 0 else [], transpose=True)
                wt_loaded2.append((s_ld.n, s_ld2.n))

            load_wT2(0)
            bank_free2 = {2: attn_end[s_dve], 3: attn_end[s_dve]}
            obuf_ofree = [0, 0]

            for og in range(4):
                if og + 1 < 4:
                    load_wT2(og + 1)
                buf = wT[og % 2]
                for ti in range(8):
                    DMA("sync", xbuf[:, 0:512],
                        x_in[HALO + ti * 128: HALO + (ti + 1) * 128, og * 512:(og + 1) * 512],
                        s_ld, [(s_dve, s_dve.n)])
                    xldv = s_ld.n
                    bank = mm_banks[ti % 2]
                    bw = [(s_ld, wt_loaded2[og][0]), (s_ld2, wt_loaded2[og][1]),
                          (s_dve, bank_free2[bank])]
                    for c in range(16):
                        lhs = attn_fm[:, c * T_OWN + ti * 128: c * T_OWN + (ti + 1) * 128]
                        rhs = buf[:, c * 512:(c + 1) * 512]
                        E("tensor", (lambda e, o=pb[bank][:], l=lhs, r=rhs,
                                     st=(c == 0), sp=(c == 15):
                                     e.matmul(o, lhsT=l, rhs=r, start=st, stop=sp)),
                          bw if c == 0 else [], s_pe if c == 15 else None)
                    pev = s_pe.n
                    ob = obuf[ti % 2]
                    E("vector", (lambda e, o=ob, b=pb[bank][:]: e.tensor_tensor(
                        out=o, in0=b, in1=xbuf[:, 0:512], op=ALU.add)),
                      [(s_pe, pev), (s_ld, xldv), (s_out, obuf_ofree[ti % 2])], s_dve)
                    bank_free2[bank] = s_dve.n
                    dvev = s_dve.n
                    DMA("sync", y_out[ti * 128:(ti + 1) * 128, og * 512:(og + 1) * 512],
                        ob, s_out, [(s_dve, dvev)])
                    obuf_ofree[ti % 2] = s_out.n
                wt_lastuse2.append(s_pe.n)
            out_plain_all = s_out.n
            wo_end = prog.snap(SEMS)
            _cut("wo")

            def emit_decode_planes(items):
                done = [None] * len(items)
                orv = [0] * len(items)

                def front(i):
                    src32, sc, dst, dtype, j, w0 = items[i]
                    par = i % 2
                    s1, s2 = dk1[par], dk2[par]
                    if dtype == F16:
                        sh_m, msk_m = 9 - 4 * j, 0x0E000E00
                    else:
                        sh_m, msk_m = 6 - 4 * j, 0x01C001C0
                    op_m = ALU.logical_shift_left if sh_m >= 0 else ALU.logical_shift_right
                    E("vector", (lambda e, s=src32, o=s1, sh=abs(sh_m), m=msk_m, op=op_m:
                                 e.tensor_scalar(out=o, in0=s, scalar1=sh, scalar2=m,
                                                 op0=op, op1=ALU.bitwise_and)), list(w0), None)
                    E("vector", (lambda e, s=src32, o=s2, sh=12 - 4 * j:
                                 e.tensor_scalar(out=o, in0=s, scalar1=sh,
                                                 scalar2=-2147450880,
                                                 op0=ALU.logical_shift_left,
                                                 op1=ALU.bitwise_and)),
                      [], s_dve)
                    t2v = s_dve.n
                    if OR_ON_POOL:
                        E("gpsimd", (lambda e, a=s1, b=s2: e.tensor_tensor(
                            out=a, in0=a, in1=b, op=ALU.add)), [(s_dve, t2v)], s_pool)
                        orv[i] = s_pool.n
                    else:
                        E("vector", (lambda e, a=s1, b=s2: e.tensor_tensor(
                            out=a, in0=a, in1=b, op=ALU.bitwise_or)), [], None)

                def back(i):
                    src32, sc, dst, dtype, j, w0 = items[i]
                    par = i % 2
                    s1 = dk1[par]
                    w = [(s_pool, orv[i])] if OR_ON_POOL else []
                    E("vector", (lambda e, o=dst, a=s1.bitcast(dtype), b=sc:
                                 e.tensor_tensor(out=o, in0=a, in1=b, op=ALU.mult)),
                      w, s_dve)
                    done[i] = s_dve.n

                for i in range(len(items)):
                    front(i)
                    if i >= 1:
                        back(i - 1)
                back(len(items) - 1)
                return done

            def emit_decode(pkbuf, sclbuf, decbuf, nmc, dtype, waits_first):
                items = []
                for mc in range(nmc):
                    src32 = pkbuf[:, mc * 512:(mc + 1) * 512].bitcast(I32)
                    sc = sclbuf[:, mc * 512:(mc + 1) * 512]
                    for j in range(4):
                        pl = 4 * mc + j
                        dst = decbuf[:, pl * 512:(pl + 1) * 512]
                        items.append((src32, sc, dst, dtype, j,
                                      list(waits_first) if pl == 0 else []))
                return emit_decode_planes(items)

            h_mult_done = wo_end[s_dve]
            pk_war_dve = wo_end[s_dve]
            g5_war_dve = wo_end[s_dve]
            prev_up_read = wo_end[s_dve]

            for og in range(11):
                pkw = [(s_dve, pk_war_dve)]
                if og == 0:
                    pkw += [(s_out, out_plain_all), (s_pe, attn_end[s_pe])]
                for mc in range(4):
                    DMA("sync", pkT_g[:, mc * 512:(mc + 1) * 512],
                        gup[512 * og:512 * (og + 1), mc * 128:(mc + 1) * 128], s_ld,
                        pkw if mc == 0 else [], transpose=True)
                for mc in range(4):
                    DMA("sync", pkT_u[:, mc * 512:(mc + 1) * 512],
                        gup[IFF + 512 * og:IFF + 512 * (og + 1), mc * 128:(mc + 1) * 128],
                        s_ld, [], transpose=True)
                pk_ld = s_ld.n
                pk_ld2 = s_ld2.n
                for half, (sbuf_, col0) in enumerate([(scl_g, 512 * og), (scl_u, IFF + 512 * og)]):
                    srcs = gus_rep.rearrange("p (mb o) -> p mb o", mb=4)[:, :, col0:col0 + 512]
                    DMA("sync", sbuf_.rearrange("p (mb o) -> p mb o", mb=4), srcs, s_ld,
                        ([(s_dve, pk_war_dve), (s_gld, gld_gs)] if half == 0 else []))
                scl_ld = s_ld.n
                E("vector", lambda e: e.tensor_scalar(out=scl_g, in0=scl_g, scalar1=16384.0,
                                                      scalar2=None, op0=ALU.mult),
                  [(s_ld, scl_ld)], s_dve)
                E("vector", lambda e: e.tensor_scalar(out=scl_u, in0=scl_u, scalar1=16384.0,
                                                      scalar2=None, op0=ALU.mult), [], s_dve)
                scl_rdy = s_dve.n
                dg = emit_decode(pkT_g, scl_g, dec_g, 4, F16,
                                 [(s_ld, pk_ld), (s_dve, scl_rdy), (s_dve, g5_war_dve)])
                du = emit_decode(pkT_u, scl_u, dec_u, 4, F16, [(s_ld2, pk_ld2)])
                pk_war_dve = du[-1]
                gate_stops = [0] * 8
                for jh in range(4):
                    for th in range(2):
                        b8 = jh * 2 + th
                        bk = pb[b8]
                        for kc in range(16):
                            lhs = dec_g[:, kc * 512 + jh: kc * 512 + jh + 509: 4]
                            rhs = xn_pi[:, kc * T_OWN + th * 512: kc * T_OWN + (th + 1) * 512]
                            wts = [(s_dve, dg[kc])]
                            if jh == 0 and th == 0 and kc == 0:
                                wts.append((s_dve, h_mult_done))
                            E("tensor", (lambda e, o=bk[:], l=lhs, r=rhs,
                                         st=(kc == 0), sp=(kc == 15):
                                         e.matmul(o, lhsT=l, rhs=r, start=st, stop=sp)),
                              wts, s_pe if kc == 15 else None)
                        gate_stops[b8] = s_pe.n
                gpe_b = gate_stops
                gev_b = [0] * 8
                silv_b = [0] * 8
                for t8 in range(8):
                    E("vector", (lambda e, o=gtmp[:, t8 * 512:(t8 + 1) * 512], s=pb[t8][:]:
                                 e.tensor_copy(o, s)), [(s_pe, gpe_b[t8])], s_dve)
                    gev_b[t8] = s_dve.n
                    E("scalar", (lambda e, o=gtmp[:, t8 * 512:(t8 + 1) * 512]:
                                 e.activation(o, o, AF.Silu)), [(s_dve, gev_b[t8])], s_act)
                    silv_b[t8] = s_act.n
                g5_war_dve = gev_b[7]
                upe_b = [0] * 8
                for jh in range(4):
                    for th in range(2):
                        b8 = jh * 2 + th
                        bk = pb[b8]
                        for kc in range(16):
                            lhs = dec_u[:, kc * 512 + jh: kc * 512 + jh + 509: 4]
                            rhs = xn_pi[:, kc * T_OWN + th * 512: kc * T_OWN + (th + 1) * 512]
                            wts = [(s_dve, du[kc])]
                            if kc == 0:
                                wts.append((s_dve, gev_b[b8]))
                            E("tensor", (lambda e, o=bk[:], l=lhs, r=rhs,
                                         st=(kc == 0), sp=(kc == 15):
                                         e.matmul(o, lhsT=l, rhs=r, start=st, stop=sp)),
                              wts, s_pe if kc == 15 else None)
                        upe_b[b8] = s_pe.n
                for jh in range(4):
                    for th in range(2):
                        b8 = jh * 2 + th
                        hc = h_chunk(4 * og + jh)
                        dst = hc[:, th * 512:(th + 1) * 512]
                        sil = gtmp[:, b8 * 512:(b8 + 1) * 512]
                        E("vector", (lambda e, o=dst, a=sil, b=pb[b8][:]:
                                     e.tensor_tensor(out=o, in0=a, in1=b, op=ALU.mult)),
                          [(s_pe, upe_b[b8]), (s_act, silv_b[b8])], s_dve)
                h_mult_done = s_dve.n

            h_done = s_dve.n
            if debug_taps:
                for c in range(44):
                    DMA("sync", taps["t_h"][:, c * T_OWN:(c + 1) * T_OWN], h_chunk(c),
                        s_out, [(s_dve, h_done)] if c == 0 else [])
            _cut("gate")

            dgroups = [(0, 4), (4, 4), (8, 3)]
            wdec_lastmm = [wo_end[s_pe], wo_end[s_pe]]
            down_bank_free = [h_done] * 8
            pk_war_d = h_done
            ydone = 0

            for nck in range(4):
                pw = [(s_dve, pk_war_d)]
                if nck == 0:
                    pw.append((s_act, s_act.n))
                for mc in range(11):
                    DMA("sync", pkT_d[:, mc * 512:(mc + 1) * 512],
                        dnp[512 * nck:512 * (nck + 1), mc * 128:(mc + 1) * 128],
                        s_ld, pw if mc == 0 else [], transpose=True)
                pk_ld = s_ld.n
                pk_ld2d = s_ld2.n
                srcs = dns_rep.rearrange("p (mb o) -> p mb o", mb=11)[:, :, 512 * nck:512 * (nck + 1)]
                DMA("sync", dscl_f16.rearrange("p (mb o) -> p mb o", mb=11), srcs, s_ld,
                    [(s_dve, pk_war_d), (s_gld, gld_ds)])
                dsl = s_ld.n
                E("vector", lambda e: e.tensor_scalar(out=dscl, in0=dscl_f16,
                                                      scalar1=float(2.0 ** 126),
                                                      scalar2=None, op0=ALU.mult),
                  [(s_ld, dsl)], s_dve)
                dscl_rdy = s_dve.n
                grp_done_pe = []
                for gi, (mb0, nmb) in enumerate(dgroups):
                    gbuf = wdec[gi % 2]
                    ditems = []
                    for mci in range(nmb):
                        mc = mb0 + mci
                        src32 = pkT_d[:, mc * 512:(mc + 1) * 512].bitcast(I32)
                        sc = dscl[:, mc * 512:(mc + 1) * 512]
                        for j in range(4):
                            pl = 4 * mci + j
                            dst = gbuf[:, pl * 512:(pl + 1) * 512]
                            w0 = []
                            if pl == 0:
                                w0 = [(s_ld, pk_ld), (s_ld2, pk_ld2d), (s_dve, dscl_rdy),
                                      (s_pe, wdec_lastmm[gi % 2])]
                            ditems.append((src32, sc, dst, BF16, j, w0))
                    done = emit_decode_planes(ditems)
                    pk_war_d = done[-1]
                    for ti in range(8):
                        for pli in range(4 * nmb):
                            mc = mb0 + pli // 4
                            j = pli % 4
                            hc = h_chunk(4 * mc + j)
                            lhs = hc[:, ti * 128:(ti + 1) * 128]
                            rhs = gbuf[:, pli * 512:(pli + 1) * 512]
                            st = (gi == 0 and pli == 0)
                            sp = (gi == 2 and pli == 4 * nmb - 1)
                            last_of_grp = (ti == 7 and pli == 4 * nmb - 1)
                            wts = []
                            if pli == 0:
                                wts = [(s_dve, done[-1]), (s_dve, h_done)]
                                if gi == 0 and ti < 8:
                                    wts.append((s_dve, down_bank_free[ti]))
                            E("tensor", (lambda e, o=pb[ti][:], l=lhs, r=rhs, st=st, sp=sp:
                                         e.matmul(o, lhsT=l, rhs=r, start=st, stop=sp,
                                                  skip_group_check=True)),
                              wts, s_pe if (sp or last_of_grp) else None)
                    wdec_lastmm[gi % 2] = s_pe.n
                dpe = s_pe.n
                for ti in range(8):
                    dst = y_bf[:, ti * D + nck * 512: ti * D + (nck + 1) * 512]
                    E("vector", (lambda e, o=dst, s=pb[ti][:]: e.tensor_copy(o, s)),
                      [(s_pe, dpe)], s_dve)
                    down_bank_free[ti] = s_dve.n
                ydone = s_dve.n

            if debug_taps:
                DMA("sync", taps["t_y"], y_bf, s_out, [(s_dve, ydone)])

            obuf_gfree = [0, 0]
            rms_read = 0
            for ti in range(8):
                E("vector", (lambda e, i=y_bf[:, ti * D:(ti + 1) * D]:
                             e.tensor_tensor(out=xbuf, in0=i, in1=i, op=ALU.mult)),
                  [(s_dve, ydone), (s_dve, rms_read)], None)
                E("vector", lambda e: e.reduce_sum(st_rms[:, 7:8], xbuf,
                                                   axis=mybir.AxisListType.X),
                  [], s_dve)
                sqv = s_dve.n
                E("vector", lambda e: e.tensor_scalar(out=st_rms[:, 1:2], in0=st_rms[:, 7:8],
                                                      scalar1=1.0 / D, scalar2=EPS_RMS,
                                                      op0=ALU.mult, op1=ALU.add),
                  [(s_dve, sqv)], s_dve)
                rms_read = s_dve.n
                E("scalar", lambda e: e.activation(st_rms[:, 2:3], st_rms[:, 1:2], AF.Ln),
                  [(s_dve, rms_read)], None)
                E("scalar", lambda e: e.activation(st_rms[:, 3:4], st_rms[:, 2:3], AF.Exp,
                                                   scale=-0.5), [], s_act)
                rsv = s_act.n
                E("vector", lambda e: e.tensor_copy(st_rms[:, 4:5], st_rms[:, 3:4]),
                  [(s_act, rsv)], s_dve)
                rcv2 = s_dve.n
                rb2 = st_rms[:, 4:5].to_broadcast([128, 512])
                for ncx in range(4):
                    par = ncx % 2
                    ob = obuf[par]
                    E("vector", (lambda e, o=ob, rb2=rb2, i=y_bf[:, ti * D + ncx * 512:
                                                        ti * D + (ncx + 1) * 512]:
                                 e.tensor_tensor(out=o, in0=i, in1=rb2, op=ALU.mult)),
                      [(s_gout, obuf_gfree[par]), (s_dve, rcv2)], s_dve)
                    dv2 = s_dve.n
                    DMA("gpsimd", y_out[ti * 128:(ti + 1) * 128, ncx * 512:(ncx + 1) * 512],
                        ob, s_gout, [(s_dve, dv2), (s_out, out_plain_all)],
                        accum_op=ALU.add)
                    obuf_gfree[par] = s_gout.n

            prog.emit("sync", lambda e: None,
                      [(s_out, s_out.n), (s_gout, s_gout.n),
                       (s_ld, s_ld.n), (s_gld, s_gld.n), (s_ld2, s_ld2.n)])

        except _StopBuild:
            pass

        def replay(engobj, oplist):
            for ws, fn, inc, incn in oplist:
                for s, v in ws:
                    engobj.wait_ge(s.h, v)
                ins = fn(engobj)
                if inc is not None:
                    ins.then_inc(inc.h, incn)

        @blk.sync
        def _(e):
            replay(e, prog.ops["sync"])

        @blk.gpsimd
        def _(e):
            replay(e, prog.ops["gpsimd"])

        @blk.vector
        def _(e):
            replay(e, prog.ops["vector"])

        @blk.scalar
        def _(e):
            replay(e, prog.ops["scalar"])

        @blk.tensor
        def _(e):
            replay(e, prog.ops["tensor"])

    nins = {e: len(prog.ops[e]) for e in ENGS}
    return nc, taps, nins


def _aux_arrays(core):
    import ml_dtypes
    pos = np.arange(core * T_OWN - HALO, (core + 1) * T_OWN, dtype=np.float64)
    pos = np.maximum(pos, 0.0)
    inv = 1.0 / (10000.0 ** (np.arange(0, DH, 2, dtype=np.float64) / DH))
    ang = pos[:, None] * inv[None, :]
    cos4 = np.tile(np.cos(ang), (1, 4)).astype(np.float16)
    sin4 = np.tile(np.sin(ang), (1, 4)).astype(np.float16)
    p = np.arange(128)[:, None]
    a = np.arange(128)[None, :]
    mdp = (p >= a).astype(np.float32)
    mdc = (p <= a).astype(np.float32)
    mq0 = np.zeros((128, 1280), dtype=np.float32)
    if core > 0:
        col = 0
        for ci in range(4):
            nq = 128 * (ci + 1)
            seg = np.ones((128, nq), dtype=np.float32)
            seg[:, nq - 128:nq] = mdp
            mq0[:, col:col + nq] = seg
            col += nq
    msk = np.zeros((128, 1792), dtype=np.float32)
    msk[:, 0:128] = mdp
    msk[:, 128:256] = mdc
    msk[:, 256:1536] = mq0
    return (cos4, sin4, msk.astype(ml_dtypes.bfloat16),
            np.eye(128, dtype=np.float16),
            np.ones((128, 128), dtype=ml_dtypes.bfloat16))


def make_in_maps(inputs):
    x = np.asarray(inputs["L0_attn_x"], dtype=np.float32).reshape(SEQ, D)
    wq_ = np.ascontiguousarray(np.asarray(inputs["L0_attn_wq"], dtype=np.float32))
    wk_ = np.ascontiguousarray(np.asarray(inputs["L0_attn_wk"], dtype=np.float32))
    wv_ = np.ascontiguousarray(np.asarray(inputs["L0_attn_wv"], dtype=np.float32))
    wo_ = np.ascontiguousarray(np.asarray(inputs["L0_attn_wo"], dtype=np.float32))
    gup_ = np.ascontiguousarray(np.asarray(inputs["L0_ffn_gate_up_packed"],
                                           dtype=np.int32)).view(np.uint16)
    dnp_ = np.ascontiguousarray(np.asarray(inputs["L0_ffn_down_packed"],
                                           dtype=np.int32)).view(np.uint16)
    gus_ = np.ascontiguousarray(np.asarray(inputs["L0_ffn_gate_up_scales"],
                                           dtype=np.float16))
    dns_ = np.ascontiguousarray(np.asarray(inputs["L0_ffn_down_scales"],
                                           dtype=np.float16))
    in_maps = []
    for c in range(NC):
        xs = np.zeros((T_ALL, D), dtype=np.float32)
        if c == 0:
            xs[HALO:] = x[0:T_OWN]
        else:
            lo = c * T_OWN - HALO
            xs[:] = x[lo:lo + T_ALL]
        cos4, sin4, msk, ide, one = _aux_arrays(c)
        eps = np.zeros((128, 2), dtype=np.float32)
        eps[:, 0] = EPS_LN
        eps[:, 1] = EPS_RMS
        in_maps.append(dict(x=xs, wq=wq_, wk=wk_, wv=wv_, wo=wo_,
                            gup=gup_, gus=gus_, dnp=dnp_, dns=dns_,
                            a_cos=cos4, a_sin=sin4, a_msk=msk, a_id=ide, a_one=one,
                            a_eps=eps))
    return in_maps


def kernel(**inputs):
    nc, _, _ = build_program(debug_taps=False)
    in_maps = make_in_maps(inputs)
    res = run_bass_kernel_spmd(nc, in_maps, core_ids=list(range(NC)))
    out = np.concatenate([r["y"] for r in res.results], axis=0)
    return out.reshape(1, SEQ, D).astype(np.float32)


if __name__ == "__main__":
    nc, taps, nins = build_program(debug_taps=False)
    print("instruction counts:", nins, "total:", sum(nins.values()))


# version 13
# speedup vs baseline: 1.1813x; 1.0140x over previous
import contextlib
import numpy as np

import concourse.bass as bass
import concourse.mybir as mybir
from concourse.bass_utils import run_bass_kernel_spmd

F32 = mybir.dt.float32
F16 = mybir.dt.float16
BF16 = mybir.dt.bfloat16
U16 = mybir.dt.uint16
I32 = mybir.dt.int32
U8 = mybir.dt.uint8
AF = mybir.ActivationFunctionType
ALU = mybir.AluOpType

NC = 8
T_OWN = 1024
HALO = 512
T_ALL = T_OWN + HALO
D = 2048
NT = T_ALL // 128
H, HK, DH = 16, 4, 128
IFF = 5632
CAP = 20.0
EPS_LN = 1e-5
EPS_RMS = 1e-6
SEQ = 8192

ENGS = ("sync", "gpsimd", "vector", "scalar", "tensor")
OR_ON_POOL = True


class Sem:
    def __init__(self, name):
        self.name = name
        self.h = None
        self.n = 0


class Prog:
    def __init__(self):
        self.ops = {e: [] for e in ENGS}
        self.lastw = {e: {} for e in ENGS}

    def emit(self, eng, fn, waits=(), inc=None, incn=1):
        ws = []
        for s, v in waits:
            if v <= 0:
                continue
            assert v <= s.n, f"[{eng}] waits {s.name}>={v} but only {s.n} emitted"
            if self.lastw[eng].get(s.name, -1) < v:
                ws.append((s, v))
                self.lastw[eng][s.name] = v
        if inc is not None:
            inc.n += incn
        self.ops[eng].append((ws, fn, inc, incn))

    def dma(self, eng, out, in_, sem, waits=(), **kw):
        def fn(e, o=out, i=in_, kw=kw):
            return e.dma_start(out=o, in_=i, **kw)
        self.emit(eng, fn, waits, sem, 16)

    def snap(self, sems):
        return {s: s.n for s in sems}


class _StopBuild(Exception):
    pass


def build_program(debug_taps=False, stop_after=None):
    nc = bass.Bass("TRN2", target_bir_lowering=False)

    x_in = nc.dram_tensor("x", [T_ALL, D], F32, kind="ExternalInput").ap()
    wq = nc.dram_tensor("wq", [D, D], F32, kind="ExternalInput").ap()
    wk = nc.dram_tensor("wk", [HK * DH, D], F32, kind="ExternalInput").ap()
    wv = nc.dram_tensor("wv", [HK * DH, D], F32, kind="ExternalInput").ap()
    wo = nc.dram_tensor("wo", [D, D], F32, kind="ExternalInput").ap()
    gup = nc.dram_tensor("gup", [2 * IFF, D // 4], U16, kind="ExternalInput").ap()
    gus = nc.dram_tensor("gus", [16, 2 * IFF], F16, kind="ExternalInput").ap()
    dnp = nc.dram_tensor("dnp", [D, IFF // 4], U16, kind="ExternalInput").ap()
    dns = nc.dram_tensor("dns", [44, D], F16, kind="ExternalInput").ap()
    a_cos = nc.dram_tensor("a_cos", [T_ALL, 256], F16, kind="ExternalInput").ap()
    a_sin = nc.dram_tensor("a_sin", [T_ALL, 256], F16, kind="ExternalInput").ap()
    a_msk = nc.dram_tensor("a_msk", [128, 1792], BF16, kind="ExternalInput").ap()
    a_id = nc.dram_tensor("a_id", [128, 128], F16, kind="ExternalInput").ap()
    a_one = nc.dram_tensor("a_one", [128, 128], BF16, kind="ExternalInput").ap()
    a_eps = nc.dram_tensor("a_eps", [128, 2], F32, kind="ExternalInput").ap()

    y_out = nc.dram_tensor("y", [T_OWN, D], F32, kind="ExternalOutput").ap()

    whbm = nc.dram_tensor("whbm", [2 * D + 2 * HK * DH, D], F16, kind="Internal").ap()
    gus_rep = nc.dram_tensor("gus_rep", [128, 4 * 2 * IFF], F16, kind="Internal").ap()
    dns_rep = nc.dram_tensor("dns_rep", [128, 11 * D], F16, kind="Internal").ap()
    WQ0, WK0, WV0, WO0 = 0, D, D + HK * DH, D + 2 * HK * DH

    taps = {}
    if debug_taps:
        def T(name, shape, dt):
            taps[name] = nc.dram_tensor(name, shape, dt, kind="ExternalOutput").ap()
        T("t_xnnat", [128, 16 * T_ALL], F16)
        T("t_xnpi", [128, 16 * T_OWN], F16)
        T("t_qfm", [128, H * T_OWN], F16)
        T("t_kfm", [128, HK * T_ALL], F16)
        T("t_vtm", [128, NT * 512], BF16)
        T("t_attn", [128, H * T_OWN], F16)
        T("t_h", [128, 44 * T_OWN], BF16)
        T("t_y", [128, 8 * D], BF16)
        T("t_stats", [128, 12 * 12], F32)

    K = 1024
    PERM0 = 0
    G1 = 4 * K
    G2 = 52 * K
    G3 = 84 * K
    G4 = 96 * K
    G5 = 128 * K
    G6 = 160 * K
    G7 = 172 * K
    G9 = 184 * K
    ATOT = 205 * K

    prog = Prog()
    s_ld = Sem("s_ld")
    s_gld = Sem("s_gld")
    s_pe = Sem("s_pe")
    s_dve = Sem("s_dve")
    s_act = Sem("s_act")
    s_pool = Sem("s_pool")
    s_out = Sem("s_out")
    s_gout = Sem("s_gout")
    s_ld2 = Sem("s_ld2")
    SEMS = [s_ld, s_gld, s_pe, s_dve, s_act, s_pool, s_out, s_gout, s_ld2]

    es = contextlib.ExitStack()
    with es:
        arena_t = es.enter_context(nc.sbuf_tensor("arena", [128, ATOT], U8))
        arena = arena_t[:]
        st_nat = es.enter_context(nc.sbuf_tensor("st_nat", [128, 64], F32))[:]

        def view(off, dtype, ncols):
            esz = {F32: 4, F16: 2, BF16: 2, U16: 2, I32: 4, U8: 1}[dtype]
            return arena[:, off:off + ncols * esz].bitcast(dtype)

        v_mask = view(PERM0, BF16, 1792)
        v_id = view(PERM0 + 3584, F16, 128)
        v_one = view(PERM0 + 3840, BF16, 128)
        MDP = v_mask[:, 0:128]
        MDC = v_mask[:, 128:256]
        MQ0 = v_mask[:, 256:1536]

        xn_nat = view(G1, F16, 16 * T_ALL)
        attn_fm = view(G1, F16, H * T_OWN)
        xn_pi = view(G2, F16, 16 * T_OWN)
        y_bf = view(G2, BF16, 8 * D)
        cs_cos = view(G3, F16, NT * 256)
        cs_sin = view(G3 + 6 * K, F16, NT * 256)
        wT = [view(G4, F16, 16 * 512), view(G4 + 16 * K, F16, 16 * 512)]
        p_T = view(G4, BF16, 8 * 512)
        stmp = view(G4 + 8 * K, F32, 512)
        ttmp = view(G4 + 10 * K, F32, 512)
        ssum = view(G4 + 12 * K, F32, 512)
        recip = view(G4 + 14 * K, F32, 512)
        dec_g = view(G4, F16, 16 * 512)
        dec_u = view(G4 + 16 * K, F16, 16 * 512)
        wdec = [view(G4, BF16, 16 * 512), view(G4 + 16 * K, BF16, 16 * 512)]
        ln_scratch = view(G5, F32, 2048)
        xn16 = [view(G5 + 8 * K, F16, 2048), view(G5 + 12 * K, F16, 2048)]
        q_fm = view(G5, F16, H * T_OWN)
        gtmp = view(G5, F16, 8 * 512)
        scl_g = view(G5 + 8 * K, F16, 2048)
        scl_u = view(G5 + 12 * K, F16, 2048)
        pkT_g = view(G5 + 16 * K, U16, 2048)
        pkT_u = view(G5 + 20 * K, U16, 2048)
        pkT_d = view(G5, U16, 11 * 512)
        dscl = view(G5 + 11 * K, BF16, 11 * 512)
        dscl_f16 = view(G5 + 11 * K, F16, 11 * 512)
        k_fm = view(G6, F16, HK * T_ALL)
        v_tm = view(G7, BF16, NT * 512)
        xbuf = view(G9, F32, 2048)
        qtmp = [view(G9 + 8 * K, F16, 512), view(G9 + 9 * K, F16, 512)]
        rsc_a = view(G9 + 10 * K, F16, 256)
        rsc_b = view(G9 + 10 * K + 512, F16, 256)
        rout = [view(G9 + 12 * K, F16, 512), view(G9 + 13 * K, F16, 512)]
        obuf = [view(G9 + 14 * K, F32, 512), view(G9 + 16 * K, F32, 512)]
        st_m = st_nat[:, 0:16]
        v_eps = view(G9 + 19 * K, F32, 2)
        st_sq = view(G9 + 18 * K + 64, F32, 16)
        st_rms = st_nat[:, 32:40]
        dk1 = [view(G9 + 14 * K, I32, 256), view(G9 + 15 * K, I32, 256)]
        dk2 = [view(G9 + 16 * K, I32, 256), view(G9 + 17 * K, I32, 256)]

        h_offs = ([G1 + i * 2 * K for i in range(24)]
                  + [G3 + i * 2 * K for i in range(6)]
                  + [G6 + i * 2 * K for i in range(6)]
                  + [G7 + i * 2 * K for i in range(6)]
                  + [G5 + 24 * K, G5 + 26 * K])
        assert len(h_offs) == 44

        def h_chunk(c):
            return view(h_offs[c], BF16, T_OWN)

        pb = [es.enter_context(nc.psum_tensor(f"pb{i}", [128, 512], F32)) for i in range(8)]
        tsl = []
        for b in range(2):
            v16 = pb[b][:].bitcast(F16)
            for s in range(4):
                tsl.append(v16[:, s * 128:(s + 1) * 128])

        for s in SEMS:
            s.h = es.enter_context(nc.semaphore(s.name))
        blk = es.enter_context(nc.Block())

        E = prog.emit
        DMA = prog.dma

        def _cut(name):
            if stop_after == name:
                prog.emit("sync", lambda e: None,
                          [(s_out, s_out.n), (s_gout, s_gout.n),
                           (s_ld, s_ld.n), (s_gld, s_gld.n),
                           (s_pe, s_pe.n), (s_dve, s_dve.n),
                           (s_act, s_act.n), (s_pool, s_pool.n)])
                raise _StopBuild()

        try:
            DMA("sync", v_mask, a_msk, s_ld)
            DMA("sync", v_id, a_id, s_ld)
            DMA("sync", v_one, a_one, s_ld)
            DMA("sync", v_eps, a_eps, s_ld)
            DMA("sync", cs_cos.rearrange("p (t c) -> p t c", c=256),
                a_cos.rearrange("(t p) c -> p t c", p=128), s_ld)
            DMA("sync", cs_sin.rearrange("p (t c) -> p t c", c=256),
                a_sin.rearrange("(t p) c -> p t c", p=128), s_ld)
            DMA("gpsimd", whbm[WQ0:WQ0 + D, :], wq, s_gld)
            DMA("gpsimd", whbm[WK0:WK0 + HK * DH, :], wk, s_gld)
            DMA("gpsimd", whbm[WV0:WV0 + HK * DH, :], wv, s_gld)
            DMA("gpsimd", whbm[WO0:WO0 + D, :], wo, s_gld)
            gld_wq, gld_wk, gld_wv, gld_wo = 16, 32, 48, 64
            src_g = gus.rearrange("(mb g) o -> mb g o", g=4)
            src_g = src_g.unsqueeze(2).broadcast_to([4, 4, 32, 2 * IFF])
            DMA("gpsimd", gus_rep.rearrange("(g r) (mb o) -> mb g r o", r=32, mb=4),
                src_g, s_gld)
            gld_gs = s_gld.n
            src_d = dns.rearrange("(mb g) o -> mb g o", g=4)
            src_d = src_d.unsqueeze(2).broadcast_to([11, 4, 32, D])
            DMA("gpsimd", dns_rep.rearrange("(g r) (mb o) -> mb g r o", r=32, mb=11),
                src_d, s_gld)
            gld_ds = s_gld.n

            E("vector", lambda e: e.memset(st_nat, 0.0), [], s_dve)
            memset_v = s_dve.n

            tb_pend = []
            tb_gdst = [None]
            tb_group = [0]
            tb_evacd = []

            def tb_flush():
                if not tb_pend:
                    return
                g = tb_group[0]
                tb_group[0] += 1
                bank = g % 2
                n = len(tb_pend)
                grp_waits = []
                if g >= 2:
                    grp_waits.append((s_dve, tb_evacd[g - 2]))
                for idx, (src_ap, ew) in enumerate(tb_pend):
                    slot = pb[bank][:].bitcast(F16)[:, idx * 128:(idx + 1) * 128]
                    w = list(ew) + (grp_waits if idx == 0 else [])
                    E("tensor", (lambda e, s=src_ap, o=slot: e.transpose(o, s, v_id)),
                      w, s_pe if idx == n - 1 else None)
                pev = s_pe.n
                psrc = pb[bank][:].bitcast(F16)[:, 0:n * 128].rearrange(
                    "p (n c) -> p n c", c=128)
                E("vector", (lambda e, o=tb_gdst[0], s=psrc: e.tensor_copy(o, s)),
                  [(s_pe, pev)], s_dve)
                tb_evacd.append(s_dve.n)
                tb_pend.clear()
                tb_gdst[0] = None

            def pe_transpose_group(srcs, gdst, extra_waits=()):
                assert not tb_pend
                for i, s in enumerate(srcs):
                    tb_pend.append((s, extra_waits if i == 0 else ()))
                tb_gdst[0] = gdst
                tb_flush()

            ln_pe_done = []
            xbufs = [xbuf, xbuf]
            stp = [st_nat[:, 0:16], st_nat[:, 0:16]]
            xread = [0, 0]
            for ti in range(NT):
                xb = xbufs[ti % 2]
                stm = stp[ti % 2]
                waits = [(s_dve, s_dve.n), (s_act, s_act.n)] if ti > 0 else []
                DMA("sync", xb, x_in[ti * 128:(ti + 1) * 128, :], s_ld, waits)
                ldv = s_ld.n
                E("vector", lambda e, stm=stm, xb=xb: e.reduce_sum(stm[:, 0:1], xb, axis=mybir.AxisListType.X),
                  [(s_ld, ldv)] + ([(s_dve, memset_v)] if ti == 0 else []), s_dve)
                redv = s_dve.n
                E("vector", (lambda e, xb=xb: e.tensor_tensor(out=ln_scratch, in0=xb,
                                                      in1=xb, op=ALU.mult)),
                  [(s_ld, ldv)], None)
                E("vector", lambda e, stm=stm, xb=xb: e.reduce_sum(stm[:, 12:13], ln_scratch,
                                                   axis=mybir.AxisListType.X),
                  [], s_dve)
                sqv = s_dve.n
                E("vector", lambda e, stm=stm, xb=xb: e.tensor_scalar(out=stm[:, 2:3], in0=stm[:, 0:1],
                                                      scalar1=1.0 / D, scalar2=None, op0=ALU.mult),
                  [(s_dve, redv)], s_dve)
                v1 = s_dve.n
                E("vector", lambda e, stm=stm, xb=xb: e.tensor_tensor(out=stm[:, 3:4], in0=stm[:, 2:3],
                                                      in1=stm[:, 2:3], op=ALU.mult),
                  [(s_dve, v1)], s_dve)
                v2 = s_dve.n
                E("vector", lambda e, stm=stm, xb=xb: e.tensor_scalar(out=stm[:, 4:5], in0=stm[:, 12:13],
                                                      scalar1=1.0 / D, scalar2=EPS_LN,
                                                      op0=ALU.mult, op1=ALU.add),
                  [(s_dve, sqv)], s_dve)
                v3 = s_dve.n
                E("vector", lambda e, stm=stm, xb=xb: e.tensor_tensor(out=stm[:, 5:6], in0=stm[:, 4:5],
                                                      in1=stm[:, 3:4], op=ALU.subtract),
                  [(s_dve, v2), (s_dve, v3)], s_dve)
                dvev = s_dve.n
                E("scalar", lambda e, stm=stm: e.activation(stm[:, 6:7], stm[:, 5:6], AF.Ln),
                  [(s_dve, dvev)], None)
                E("scalar", lambda e, stm=stm: e.activation(stm[:, 7:8], stm[:, 6:7], AF.Exp,
                                                   scale=-0.5), [], s_act)
                actv = s_act.n
                E("vector", lambda e, stm=stm, xb=xb: e.tensor_copy(stm[:, 8:9], stm[:, 7:8]),
                  [(s_act, actv)], s_dve)
                rcv = s_dve.n
                if debug_taps:
                    E("vector", (lambda e, o=view(G9 + 20 * K, F32, 144)[:, ti * 12:(ti + 1) * 12]:
                                 e.tensor_copy(o, stm[:, 0:12])), [], None)
                xw = [(s_dve, rcv)]
                if ti >= 2:
                    xw.append((s_pe, ln_pe_done[ti - 2]))
                xnb = xn16[ti % 2]
                mb_ = stm[:, 2:3].to_broadcast([128, D])
                rb_ = stm[:, 8:9].to_broadcast([128, D])
                E("vector", (lambda e, mb_=mb_, xb=xb: e.tensor_tensor(
                    out=xb, in0=xb, in1=mb_, op=ALU.subtract)),
                  xw, None)
                E("vector", (lambda e, xnb=xnb, rb_=rb_, xb=xb: e.tensor_tensor(
                    out=xnb, in0=xb, in1=rb_, op=ALU.mult)),
                  [], s_dve)
                xnv = s_dve.n
                xn_nat3 = xn_nat.rearrange("p (c t) -> p c t", t=T_ALL)
                for h8 in range(2):
                    srcs = [xnb[:, c * 128:(c + 1) * 128] for c in range(8 * h8, 8 * h8 + 8)]
                    gdst = xn_nat3[:, 8 * h8:8 * h8 + 8, ti * 128:(ti + 1) * 128]
                    pe_transpose_group(srcs, gdst,
                                       extra_waits=([(s_dve, xnv)] if h8 == 0 else ()))
                if ti >= 4:
                    to = ti - 4
                    xn_pi3 = xn_pi.rearrange("p (c t) -> p c t", t=T_OWN)
                    for h8 in range(2):
                        srcs = []
                        for c in range(8 * h8, 8 * h8 + 8):
                            mb, j = c // 4, c % 4
                            srcs.append(xnb[:, 512 * mb + j: 512 * mb + j + 509: 4])
                        gdst = xn_pi3[:, 8 * h8:8 * h8 + 8, to * 128:(to + 1) * 128]
                        pe_transpose_group(srcs, gdst)
                xread[ti % 2] = s_dve.n
                ln_pe_done.append(s_pe.n)

            ln_end = prog.snap(SEMS)
            if debug_taps:
                DMA("sync", taps["t_xnnat"], xn_nat, s_out, [(s_dve, s_dve.n)])
                DMA("sync", taps["t_xnpi"], xn_pi, s_out, [(s_dve, s_dve.n)])
                DMA("sync", taps["t_stats"], view(G9 + 20 * K, F32, 144), s_out, [])
            _cut("ln")

            ogs = [("q", WQ0 + 512 * g, True, g) for g in range(4)]
            ogs.append(("k", WK0, True, 0))
            ogs.append(("v", WV0, False, 0))

            wt_loaded = []
            wt_lastuse = []

            def load_wT(i, row0):
                cast_gate = gld_wv if row0 >= WV0 else (gld_wk if row0 >= WK0 else gld_wq)
                waits = [(s_gld, cast_gate)]
                if i == 0:
                    pass
                if i >= 2:
                    waits.append((s_pe, wt_lastuse[i - 2]))
                buf = wT[i % 2]
                for c in range(16):
                    DMA("sync", buf[:, c * 512:(c + 1) * 512],
                        whbm[row0:row0 + 512, c * 128:(c + 1) * 128], s_ld,
                        waits if c == 0 else [], transpose=True)
                wt_loaded.append((s_ld.n, s_ld2.n))

            load_wT(0, ogs[0][1])
            mm_banks = (2, 3)
            bank_free = {2: ln_end[s_dve], 3: ln_end[s_dve]}
            rout_lastread = [0, 0]

            pend_tr = [None]

            def flush_pend_tr():
                if pend_tr[0] is None:
                    return
                srcs, gdst, rv, par = pend_tr[0]
                pend_tr[0] = None
                pe_transpose_group(srcs, gdst, extra_waits=[(s_dve, rv)])
                rout_lastread[par] = s_pe.n

            for i, (kind, row0, dorope, gidx) in enumerate(ogs):
                if i + 1 < len(ogs):
                    load_wT(i + 1, ogs[i + 1][1])
                buf = wT[i % 2]
                tlist = list(range(4, 12)) if kind == "q" else list(range(NT))
                for tj, ti in enumerate(tlist):
                    bank = mm_banks[tj % 2]
                    bw = [(s_ld, wt_loaded[i][0]), (s_ld2, wt_loaded[i][1]),
                          (s_dve, bank_free[bank]),
                          (s_dve, ln_end[s_dve]), (s_act, ln_end[s_act])]
                    for c in range(16):
                        lhs = xn_nat[:, c * T_ALL + ti * 128: c * T_ALL + (ti + 1) * 128]
                        rhs = buf[:, c * 512:(c + 1) * 512]
                        E("tensor", (lambda e, o=pb[bank][:], l=lhs, r=rhs,
                                     st=(c == 0), sp=(c == 15):
                                     e.matmul(o, lhsT=l, rhs=r, start=st, stop=sp)),
                          bw if c == 0 else [], s_pe if c == 15 else None)
                    pev = s_pe.n
                    flush_pend_tr()
                    if kind == "v":
                        dst = v_tm[:, ti * 512:(ti + 1) * 512]
                        E("vector", (lambda e, o=dst, s=pb[bank][:]: e.tensor_copy(o, s)),
                          [(s_pe, pev)], s_dve)
                        bank_free[bank] = s_dve.n
                    else:
                        qt = qtmp[tj % 2]
                        E("vector", (lambda e, o=qt, s=pb[bank][:]: e.tensor_copy(o, s)),
                          [(s_pe, pev)], s_dve)
                        bank_free[bank] = s_dve.n
                        cosb = cs_cos[:, ti * 256:(ti + 1) * 256].rearrange("p (h c) -> p h c", h=4)
                        sinb = cs_sin[:, ti * 256:(ti + 1) * 256].rearrange("p (h c) -> p h c", h=4)
                        q3 = qt.rearrange("p (h c two) -> p h c two", h=4, two=2)
                        ro = rout[tj % 2]
                        r3 = ro.rearrange("p (h c two) -> p h c two", h=4, two=2)
                        s3a = rsc_a.rearrange("p (h c) -> p h c", h=4)
                        s3b = rsc_b.rearrange("p (h c) -> p h c", h=4)
                        rw = [(s_pe, rout_lastread[tj % 2])]
                        E("vector", (lambda e, a=q3, b=cosb, o=s3a: e.tensor_tensor(
                            out=o, in0=a[:, :, :, 0], in1=b, op=ALU.mult)), rw, None)
                        E("vector", (lambda e, a=q3, b=sinb, o=s3b: e.tensor_tensor(
                            out=o, in0=a[:, :, :, 1], in1=b, op=ALU.mult)), [], None)
                        E("vector", (lambda e, a=s3a, b=s3b, o=r3: e.tensor_tensor(
                            out=o[:, :, :, 0], in0=a, in1=b, op=ALU.subtract)), [], None)
                        E("vector", (lambda e, a=q3, b=sinb, o=s3a: e.tensor_tensor(
                            out=o, in0=a[:, :, :, 0], in1=b, op=ALU.mult)), [], None)
                        E("vector", (lambda e, a=q3, b=cosb, o=s3b: e.tensor_tensor(
                            out=o, in0=a[:, :, :, 1], in1=b, op=ALU.mult)), [], None)
                        E("vector", (lambda e, a=s3a, b=s3b, o=r3: e.tensor_tensor(
                            out=o[:, :, :, 1], in0=a, in1=b, op=ALU.add)), [], s_dve)
                        rv = s_dve.n
                        srcs = [ro[:, hh * 128:(hh + 1) * 128] for hh in range(4)]
                        if kind == "q":
                            q_fm3 = q_fm.rearrange("p (c t) -> p c t", t=T_OWN)
                            gdst = q_fm3[:, 4 * gidx:4 * gidx + 4,
                                         (ti - 4) * 128:(ti - 3) * 128]
                        else:
                            k_fm3 = k_fm.rearrange("p (c t) -> p c t", t=T_ALL)
                            gdst = k_fm3[:, 0:4, ti * 128:(ti + 1) * 128]
                        pend_tr[0] = (srcs, gdst, rv, tj % 2)
                wt_lastuse.append(s_pe.n)
            flush_pend_tr()

            qkv_end = prog.snap(SEMS)
            if debug_taps:
                DMA("sync", taps["t_qfm"], q_fm, s_out, [(s_dve, s_dve.n)])
                DMA("sync", taps["t_kfm"], k_fm, s_out, [(s_dve, s_dve.n)])
                DMA("sync", taps["t_vtm"], v_tm, s_out, [(s_dve, s_dve.n)])
            _cut("qkv")

            p_T2 = [view(G4, BF16, 8 * 512), view(G4 + 8 * K, BF16, 8 * 512)]
            stmp2 = [view(G4 + 16 * K, F32, 512), view(G4 + 18 * K, F32, 512)]
            ttmp2 = [view(G4 + 20 * K, F32, 512), view(G4 + 22 * K, F32, 512)]
            ssum2 = [view(G4 + 24 * K, F32, 512), view(G4 + 26 * K, F32, 512)]
            recip2 = [view(G4 + 28 * K, F32, 512), view(G4 + 30 * K, F32, 512)]
            SC_SCALE = 1.0 / (CAP * float(np.sqrt(DH)))
            sb_free = {b: qkv_end[s_dve] for b in range(8)}
            pvread = [qkv_end[s_pe], qkv_end[s_pe]]
            sums_free = [qkv_end[s_dve], qkv_end[s_dve]]
            pv_free = [qkv_end[s_dve], qkv_end[s_dve]]
            last_tanh = [0, 0]
            mask_v = {}
            pv_stop = {}
            it_list = [(qb, hh) for qb in range(2) for hh in range(H)]

            def chunks_of(qb):
                return ([(qb * 4 + ci, 0, ci) for ci in (3, 2, 1, 0)]
                        + [(qb * 4 + 4 + ci, 1, ci) for ci in range(4)])

            def emit_front(it):
                qb, hh = it_list[it]
                par = it % 2
                kvh = hh // 4
                qbase = hh * T_OWN + qb * 512
                for cidx, (tkt, kind, ci) in enumerate(chunks_of(qb)):
                    if kind == 0:
                        c0, nq = 0, 128 * (ci + 1)
                    else:
                        c0, nq = 128 * ci, 512 - 128 * ci
                    sbk = 4 * par + cidx % 2
                    lhs = k_fm[:, kvh * T_ALL + tkt * 128: kvh * T_ALL + (tkt + 1) * 128]
                    rhs = q_fm[:, qbase + c0: qbase + c0 + nq]
                    E("tensor", (lambda e, o=pb[sbk][:, 0:nq], l=lhs, r=rhs:
                                 e.matmul(o, lhsT=l, rhs=r, start=True, stop=True,
                                          skip_group_check=True)),
                      [(s_dve, sb_free[sbk])], s_pe)
                    pev = s_pe.n
                    E("vector", (lambda e, o=stmp2[par][:, 0:nq], s=pb[sbk][:, 0:nq]:
                                 e.tensor_copy(o, s)),
                      [(s_pe, pev), (s_act, last_tanh[par])], s_dve)
                    sb_free[sbk] = s_dve.n
                    dvev = s_dve.n
                    tw = [(s_dve, dvev)]
                    if cidx == 0:
                        tw.append((s_pe, pvread[par]))
                    E("scalar", (lambda e, n=nq, par=par: e.activation(
                        ttmp2[par][:, 0:n], stmp2[par][:, 0:n], AF.Tanh, scale=SC_SCALE)),
                      tw, s_act)
                    last_tanh[par] = s_act.n
                    pslice = p_T2[par][:, cidx * 512: cidx * 512 + nq]
                    E("scalar", (lambda e, o=pslice, n=nq, par=par: e.activation(
                        o, ttmp2[par][:, 0:n], AF.Exp, scale=CAP)), [], s_act)
                    actv = s_act.n
                    if kind == 0 and qb == 0:
                        mseg = MQ0[:, 64 * ci * (ci + 1): 64 * ci * (ci + 1) + nq]
                        E("vector", (lambda e, o=pslice, m=mseg: e.tensor_tensor(
                            out=o, in0=o, in1=m, op=ALU.mult)), [(s_act, actv)], s_dve)
                    elif kind == 0:
                        dseg = pslice[:, nq - 128:nq]
                        E("vector", (lambda e, o=dseg: e.tensor_tensor(
                            out=o, in0=o, in1=MDP, op=ALU.mult)), [(s_act, actv)], s_dve)
                    else:
                        dseg = pslice[:, 0:128]
                        E("vector", (lambda e, o=dseg: e.tensor_tensor(
                            out=o, in0=o, in1=MDC, op=ALU.mult)), [(s_act, actv)], s_dve)
                    mask_v[(it, cidx)] = s_dve.n

            def emit_back(it):
                qb, hh = it_list[it]
                par = it % 2
                kvh = hh // 4
                qbase = hh * T_OWN + qb * 512
                bsum, bpv = 2 + 4 * par, 3 + 4 * par
                for cidx, (tkt, kind, ci) in enumerate(chunks_of(qb)):
                    if kind == 0:
                        c0, nq = 0, 128 * (ci + 1)
                    else:
                        c0, nq = 128 * ci, 512 - 128 * ci
                    pslice = p_T2[par][:, cidx * 512: cidx * 512 + nq]
                    stw = [(s_dve, mask_v[(it, cidx)])]
                    if cidx == 0:
                        stw += [(s_dve, sums_free[par]), (s_dve, pv_free[par])]
                    E("tensor", (lambda e, r=pslice, st=(cidx == 0), c0=c0, nq=nq, b=bsum:
                                 e.matmul(pb[b][:, c0:c0 + nq], lhsT=v_one, rhs=r,
                                          start=st, stop=False, skip_group_check=True)),
                      stw, None)
                    vsl = v_tm[:, tkt * 512 + kvh * 128: tkt * 512 + (kvh + 1) * 128]
                    E("tensor", (lambda e, r=pslice, l=vsl, st=(cidx == 0), c0=c0, nq=nq, b=bpv:
                                 e.matmul(pb[b][:, c0:c0 + nq], lhsT=l, rhs=r,
                                          start=st, stop=(cidx == 7), skip_group_check=True)),
                      [], s_pe if cidx == 7 else None)
                pv_stop[it] = s_pe.n
                pvread[par] = s_pe.n
                E("vector", (lambda e, par=par, b=bsum: e.tensor_copy(ssum2[par], pb[b][:])),
                  [(s_pe, pv_stop[it])], s_dve)
                sums_free[par] = s_dve.n
                dvev = s_dve.n
                E("scalar", (lambda e, par=par: e.activation(recip2[par], ssum2[par], AF.Ln)),
                  [(s_dve, dvev)], None)
                E("scalar", (lambda e, par=par: e.activation(recip2[par], recip2[par],
                                                             AF.Exp, scale=-1.0)), [], s_act)
                actv = s_act.n
                dst = attn_fm[:, qbase: qbase + 512]
                E("vector", (lambda e, o=dst, par=par, b=bpv: e.tensor_tensor(
                    out=o, in0=pb[b][:], in1=recip2[par], op=ALU.mult)),
                  [(s_act, actv)], s_dve)
                pv_free[par] = s_dve.n

            for it in range(len(it_list) + 1):
                if it < len(it_list):
                    emit_front(it)
                if it >= 1:
                    emit_back(it - 1)

            attn_end = prog.snap(SEMS)
            if debug_taps:
                DMA("sync", taps["t_attn"], attn_fm, s_out, [(s_dve, s_dve.n)])
            _cut("attn")

            wt_loaded2 = []
            wt_lastuse2 = []

            def load_wT2(i):
                waits = [(s_gld, gld_wo)]
                if i == 0:
                    waits += [(s_dve, attn_end[s_dve]), (s_act, attn_end[s_act]),
                              (s_pe, attn_end[s_pe])]
                if i >= 2:
                    waits.append((s_pe, wt_lastuse2[i - 2]))
                buf = wT[i % 2]
                for c in range(16):
                    DMA("sync", buf[:, c * 512:(c + 1) * 512],
                        whbm[WO0 + 512 * i:WO0 + 512 * i + 512, c * 128:(c + 1) * 128],
                        s_ld, waits if c == 0 else [], transpose=True)
                wt_loaded2.append((s_ld.n, s_ld2.n))

            load_wT2(0)
            bank_free2 = {2: attn_end[s_dve], 3: attn_end[s_dve]}
            obuf_ofree = [0, 0]

            for og in range(4):
                if og + 1 < 4:
                    load_wT2(og + 1)
                buf = wT[og % 2]
                for ti in range(8):
                    DMA("sync", xbuf[:, 0:512],
                        x_in[HALO + ti * 128: HALO + (ti + 1) * 128, og * 512:(og + 1) * 512],
                        s_ld, [(s_dve, s_dve.n)])
                    xldv = s_ld.n
                    bank = mm_banks[ti % 2]
                    bw = [(s_ld, wt_loaded2[og][0]), (s_ld2, wt_loaded2[og][1]),
                          (s_dve, bank_free2[bank])]
                    for c in range(16):
                        lhs = attn_fm[:, c * T_OWN + ti * 128: c * T_OWN + (ti + 1) * 128]
                        rhs = buf[:, c * 512:(c + 1) * 512]
                        E("tensor", (lambda e, o=pb[bank][:], l=lhs, r=rhs,
                                     st=(c == 0), sp=(c == 15):
                                     e.matmul(o, lhsT=l, rhs=r, start=st, stop=sp)),
                          bw if c == 0 else [], s_pe if c == 15 else None)
                    pev = s_pe.n
                    ob = obuf[ti % 2]
                    E("vector", (lambda e, o=ob, b=pb[bank][:]: e.tensor_tensor(
                        out=o, in0=b, in1=xbuf[:, 0:512], op=ALU.add)),
                      [(s_pe, pev), (s_ld, xldv), (s_out, obuf_ofree[ti % 2])], s_dve)
                    bank_free2[bank] = s_dve.n
                    dvev = s_dve.n
                    DMA("sync", y_out[ti * 128:(ti + 1) * 128, og * 512:(og + 1) * 512],
                        ob, s_out, [(s_dve, dvev)])
                    obuf_ofree[ti % 2] = s_out.n
                wt_lastuse2.append(s_pe.n)
            out_plain_all = s_out.n
            wo_end = prog.snap(SEMS)
            _cut("wo")

            def emit_decode_planes(items):
                done = [None] * len(items)
                orv = [0] * len(items)

                def front(i):
                    src32, sc, dst, dtype, j, w0 = items[i]
                    par = i % 2
                    s1, s2 = dk1[par], dk2[par]
                    if dtype == F16:
                        sh_m, msk_m = 9 - 4 * j, 0x0E000E00
                    else:
                        sh_m, msk_m = 6 - 4 * j, 0x01C001C0
                    op_m = ALU.logical_shift_left if sh_m >= 0 else ALU.logical_shift_right
                    E("vector", (lambda e, s=src32, o=s1, sh=abs(sh_m), m=msk_m, op=op_m:
                                 e.tensor_scalar(out=o, in0=s, scalar1=sh, scalar2=m,
                                                 op0=op, op1=ALU.bitwise_and)), list(w0), None)
                    E("vector", (lambda e, s=src32, o=s2, sh=12 - 4 * j:
                                 e.tensor_scalar(out=o, in0=s, scalar1=sh,
                                                 scalar2=-2147450880,
                                                 op0=ALU.logical_shift_left,
                                                 op1=ALU.bitwise_and)),
                      [], s_dve)
                    t2v = s_dve.n
                    if OR_ON_POOL:
                        E("gpsimd", (lambda e, a=s1, b=s2: e.tensor_tensor(
                            out=a, in0=a, in1=b, op=ALU.add)), [(s_dve, t2v)], s_pool)
                        orv[i] = s_pool.n
                    else:
                        E("vector", (lambda e, a=s1, b=s2: e.tensor_tensor(
                            out=a, in0=a, in1=b, op=ALU.bitwise_or)), [], None)

                def back(i):
                    src32, sc, dst, dtype, j, w0 = items[i]
                    par = i % 2
                    s1 = dk1[par]
                    w = [(s_pool, orv[i])] if OR_ON_POOL else []
                    E("vector", (lambda e, o=dst, a=s1.bitcast(dtype), b=sc:
                                 e.tensor_tensor(out=o, in0=a, in1=b, op=ALU.mult)),
                      w, s_dve)
                    done[i] = s_dve.n

                for i in range(len(items)):
                    front(i)
                    if i >= 1:
                        back(i - 1)
                back(len(items) - 1)
                return done

            def emit_decode(pkbuf, sclbuf, decbuf, nmc, dtype, waits_first):
                items = []
                for mc in range(nmc):
                    src32 = pkbuf[:, mc * 512:(mc + 1) * 512].bitcast(I32)
                    sc = sclbuf[:, mc * 512:(mc + 1) * 512]
                    for j in range(4):
                        pl = 4 * mc + j
                        dst = decbuf[:, pl * 512:(pl + 1) * 512]
                        items.append((src32, sc, dst, dtype, j,
                                      list(waits_first) if pl == 0 else []))
                return emit_decode_planes(items)

            h_mult_done = wo_end[s_dve]
            pk_war_dve = wo_end[s_dve]
            g5_war_dve = wo_end[s_dve]
            prev_up_read = wo_end[s_dve]

            for og in range(11):
                pkw = [(s_dve, pk_war_dve)]
                if og == 0:
                    pkw += [(s_out, out_plain_all), (s_pe, attn_end[s_pe])]
                for mc in range(4):
                    DMA("sync", pkT_g[:, mc * 512:(mc + 1) * 512],
                        gup[512 * og:512 * (og + 1), mc * 128:(mc + 1) * 128], s_ld,
                        pkw if mc == 0 else [], transpose=True)
                for mc in range(4):
                    DMA("sync", pkT_u[:, mc * 512:(mc + 1) * 512],
                        gup[IFF + 512 * og:IFF + 512 * (og + 1), mc * 128:(mc + 1) * 128],
                        s_ld, [], transpose=True)
                pk_ld = s_ld.n
                pk_ld2 = s_ld2.n
                for half, (sbuf_, col0) in enumerate([(scl_g, 512 * og), (scl_u, IFF + 512 * og)]):
                    srcs = gus_rep.rearrange("p (mb o) -> p mb o", mb=4)[:, :, col0:col0 + 512]
                    DMA("sync", sbuf_.rearrange("p (mb o) -> p mb o", mb=4), srcs, s_ld,
                        ([(s_dve, pk_war_dve), (s_gld, gld_gs)] if half == 0 else []))
                scl_ld = s_ld.n
                E("vector", lambda e: e.tensor_scalar(out=scl_g, in0=scl_g, scalar1=16384.0,
                                                      scalar2=None, op0=ALU.mult),
                  [(s_ld, scl_ld)], s_dve)
                E("vector", lambda e: e.tensor_scalar(out=scl_u, in0=scl_u, scalar1=16384.0,
                                                      scalar2=None, op0=ALU.mult), [], s_dve)
                scl_rdy = s_dve.n
                dg = emit_decode(pkT_g, scl_g, dec_g, 4, F16,
                                 [(s_ld, pk_ld), (s_dve, scl_rdy), (s_dve, g5_war_dve)])
                du = emit_decode(pkT_u, scl_u, dec_u, 4, F16, [(s_ld2, pk_ld2)])
                pk_war_dve = du[-1]
                gate_stops = [0] * 8
                for jh in range(4):
                    for th in range(2):
                        b8 = jh * 2 + th
                        bk = pb[b8]
                        for kc in range(16):
                            lhs = dec_g[:, kc * 512 + jh: kc * 512 + jh + 509: 4]
                            rhs = xn_pi[:, kc * T_OWN + th * 512: kc * T_OWN + (th + 1) * 512]
                            wts = [(s_dve, dg[kc])]
                            if jh == 0 and th == 0 and kc == 0:
                                wts.append((s_dve, h_mult_done))
                            E("tensor", (lambda e, o=bk[:], l=lhs, r=rhs,
                                         st=(kc == 0), sp=(kc == 15):
                                         e.matmul(o, lhsT=l, rhs=r, start=st, stop=sp)),
                              wts, s_pe if kc == 15 else None)
                        gate_stops[b8] = s_pe.n
                gpe_b = gate_stops
                gev_b = [0] * 8
                silv_b = [0] * 8
                for t8 in range(8):
                    E("vector", (lambda e, o=gtmp[:, t8 * 512:(t8 + 1) * 512], s=pb[t8][:]:
                                 e.tensor_copy(o, s)), [(s_pe, gpe_b[t8])], s_dve)
                    gev_b[t8] = s_dve.n
                    E("scalar", (lambda e, o=gtmp[:, t8 * 512:(t8 + 1) * 512]:
                                 e.activation(o, o, AF.Silu)), [(s_dve, gev_b[t8])], s_act)
                    silv_b[t8] = s_act.n
                g5_war_dve = gev_b[7]
                upe_b = [0] * 8
                for jh in range(4):
                    for th in range(2):
                        b8 = jh * 2 + th
                        bk = pb[b8]
                        for kc in range(16):
                            lhs = dec_u[:, kc * 512 + jh: kc * 512 + jh + 509: 4]
                            rhs = xn_pi[:, kc * T_OWN + th * 512: kc * T_OWN + (th + 1) * 512]
                            wts = [(s_dve, du[kc])]
                            if kc == 0:
                                wts.append((s_dve, gev_b[b8]))
                            E("tensor", (lambda e, o=bk[:], l=lhs, r=rhs,
                                         st=(kc == 0), sp=(kc == 15):
                                         e.matmul(o, lhsT=l, rhs=r, start=st, stop=sp)),
                              wts, s_pe if kc == 15 else None)
                        upe_b[b8] = s_pe.n
                for jh in range(4):
                    for th in range(2):
                        b8 = jh * 2 + th
                        hc = h_chunk(4 * og + jh)
                        dst = hc[:, th * 512:(th + 1) * 512]
                        sil = gtmp[:, b8 * 512:(b8 + 1) * 512]
                        E("vector", (lambda e, o=dst, a=sil, b=pb[b8][:]:
                                     e.tensor_tensor(out=o, in0=a, in1=b, op=ALU.mult)),
                          [(s_pe, upe_b[b8]), (s_act, silv_b[b8])], s_dve)
                h_mult_done = s_dve.n

            h_done = s_dve.n
            if debug_taps:
                for c in range(44):
                    DMA("sync", taps["t_h"][:, c * T_OWN:(c + 1) * T_OWN], h_chunk(c),
                        s_out, [(s_dve, h_done)] if c == 0 else [])
            _cut("gate")

            dgroups = [(0, 4), (4, 4), (8, 3)]
            wdec_lastmm = [wo_end[s_pe], wo_end[s_pe]]
            down_bank_free = [h_done] * 8
            pk_war_d = h_done
            ydone = 0

            for nck in range(4):
                pw = [(s_dve, pk_war_d)]
                if nck == 0:
                    pw.append((s_act, s_act.n))
                for mc in range(11):
                    DMA("sync", pkT_d[:, mc * 512:(mc + 1) * 512],
                        dnp[512 * nck:512 * (nck + 1), mc * 128:(mc + 1) * 128],
                        s_ld, pw if mc == 0 else [], transpose=True)
                pk_ld = s_ld.n
                pk_ld2d = s_ld2.n
                srcs = dns_rep.rearrange("p (mb o) -> p mb o", mb=11)[:, :, 512 * nck:512 * (nck + 1)]
                DMA("sync", dscl_f16.rearrange("p (mb o) -> p mb o", mb=11), srcs, s_ld,
                    [(s_dve, pk_war_d), (s_gld, gld_ds)])
                dsl = s_ld.n
                E("vector", lambda e: e.tensor_scalar(out=dscl, in0=dscl_f16,
                                                      scalar1=float(2.0 ** 126),
                                                      scalar2=None, op0=ALU.mult),
                  [(s_ld, dsl)], s_dve)
                dscl_rdy = s_dve.n
                grp_done_pe = []
                for gi, (mb0, nmb) in enumerate(dgroups):
                    gbuf = wdec[gi % 2]
                    ditems = []
                    for mci in range(nmb):
                        mc = mb0 + mci
                        src32 = pkT_d[:, mc * 512:(mc + 1) * 512].bitcast(I32)
                        sc = dscl[:, mc * 512:(mc + 1) * 512]
                        for j in range(4):
                            pl = 4 * mci + j
                            dst = gbuf[:, pl * 512:(pl + 1) * 512]
                            w0 = []
                            if pl == 0:
                                w0 = [(s_ld, pk_ld), (s_ld2, pk_ld2d), (s_dve, dscl_rdy),
                                      (s_pe, wdec_lastmm[gi % 2])]
                            ditems.append((src32, sc, dst, BF16, j, w0))
                    done = emit_decode_planes(ditems)
                    pk_war_d = done[-1]
                    for ti in range(8):
                        for pli in range(4 * nmb):
                            mc = mb0 + pli // 4
                            j = pli % 4
                            hc = h_chunk(4 * mc + j)
                            lhs = hc[:, ti * 128:(ti + 1) * 128]
                            rhs = gbuf[:, pli * 512:(pli + 1) * 512]
                            st = (gi == 0 and pli == 0)
                            sp = (gi == 2 and pli == 4 * nmb - 1)
                            last_of_grp = (ti == 7 and pli == 4 * nmb - 1)
                            wts = []
                            if pli == 0:
                                wts = [(s_dve, done[-1]), (s_dve, h_done)]
                                if gi == 0 and ti < 8:
                                    wts.append((s_dve, down_bank_free[ti]))
                            E("tensor", (lambda e, o=pb[ti][:], l=lhs, r=rhs, st=st, sp=sp:
                                         e.matmul(o, lhsT=l, rhs=r, start=st, stop=sp,
                                                  skip_group_check=True)),
                              wts, s_pe if (sp or last_of_grp) else None)
                    wdec_lastmm[gi % 2] = s_pe.n
                dpe = s_pe.n
                for ti in range(8):
                    dst = y_bf[:, ti * D + nck * 512: ti * D + (nck + 1) * 512]
                    E("vector", (lambda e, o=dst, s=pb[ti][:]: e.tensor_copy(o, s)),
                      [(s_pe, dpe)], s_dve)
                    down_bank_free[ti] = s_dve.n
                ydone = s_dve.n

            if debug_taps:
                DMA("sync", taps["t_y"], y_bf, s_out, [(s_dve, ydone)])

            for ti in range(8):
                E("vector", (lambda e, i=y_bf[:, ti * D:(ti + 1) * D]:
                             e.tensor_tensor(out=xbuf, in0=i, in1=i, op=ALU.mult)),
                  [(s_dve, ydone)], None)
                E("vector", (lambda e, o=st_nat[:, 40 + ti:41 + ti]:
                             e.reduce_sum(o, xbuf, axis=mybir.AxisListType.X)),
                  [], s_dve)
            red_done = s_dve.n
            E("vector", lambda e: e.tensor_scalar(out=st_nat[:, 48:56], in0=st_nat[:, 40:48],
                                                  scalar1=1.0 / D, scalar2=EPS_RMS,
                                                  op0=ALU.mult, op1=ALU.add),
              [(s_dve, red_done)], s_dve)
            ts_done = s_dve.n
            E("scalar", lambda e: e.activation(st_nat[:, 56:64], st_nat[:, 48:56], AF.Ln),
              [(s_dve, ts_done)], None)
            E("scalar", lambda e: e.activation(st_nat[:, 48:56], st_nat[:, 56:64], AF.Exp,
                                               scale=-0.5), [], s_act)
            rsv = s_act.n
            E("vector", lambda e: e.tensor_copy(st_nat[:, 40:48], st_nat[:, 48:56]),
              [(s_act, rsv)], s_dve)
            rstd_done = s_dve.n
            ob4 = [obuf[0], obuf[1],
                   view(G9 + 8 * K, F32, 512), view(G9 + 12 * K, F32, 512)]
            ob4_free = [0, 0, 0, 0]
            for ti in range(8):
                rb2 = st_nat[:, 40 + ti:41 + ti].to_broadcast([128, 512])
                for ncx in range(4):
                    ob = ob4[ncx]
                    E("vector", (lambda e, o=ob, rb2=rb2, i=y_bf[:, ti * D + ncx * 512:
                                                        ti * D + (ncx + 1) * 512]:
                                 e.tensor_tensor(out=o, in0=i, in1=rb2, op=ALU.mult)),
                      [(s_gout, ob4_free[ncx]), (s_dve, rstd_done)], s_dve)
                    dv2 = s_dve.n
                    DMA("gpsimd", y_out[ti * 128:(ti + 1) * 128, ncx * 512:(ncx + 1) * 512],
                        ob, s_gout, [(s_dve, dv2), (s_out, out_plain_all)],
                        accum_op=ALU.add)
                    ob4_free[ncx] = s_gout.n

            prog.emit("sync", lambda e: None,
                      [(s_out, s_out.n), (s_gout, s_gout.n),
                       (s_ld, s_ld.n), (s_gld, s_gld.n), (s_ld2, s_ld2.n)])

        except _StopBuild:
            pass

        def replay(engobj, oplist):
            for ws, fn, inc, incn in oplist:
                for s, v in ws:
                    engobj.wait_ge(s.h, v)
                ins = fn(engobj)
                if inc is not None:
                    ins.then_inc(inc.h, incn)

        @blk.sync
        def _(e):
            replay(e, prog.ops["sync"])

        @blk.gpsimd
        def _(e):
            replay(e, prog.ops["gpsimd"])

        @blk.vector
        def _(e):
            replay(e, prog.ops["vector"])

        @blk.scalar
        def _(e):
            replay(e, prog.ops["scalar"])

        @blk.tensor
        def _(e):
            replay(e, prog.ops["tensor"])

    nins = {e: len(prog.ops[e]) for e in ENGS}
    return nc, taps, nins


def _aux_arrays(core):
    import ml_dtypes
    pos = np.arange(core * T_OWN - HALO, (core + 1) * T_OWN, dtype=np.float64)
    pos = np.maximum(pos, 0.0)
    inv = 1.0 / (10000.0 ** (np.arange(0, DH, 2, dtype=np.float64) / DH))
    ang = pos[:, None] * inv[None, :]
    cos4 = np.tile(np.cos(ang), (1, 4)).astype(np.float16)
    sin4 = np.tile(np.sin(ang), (1, 4)).astype(np.float16)
    p = np.arange(128)[:, None]
    a = np.arange(128)[None, :]
    mdp = (p >= a).astype(np.float32)
    mdc = (p <= a).astype(np.float32)
    mq0 = np.zeros((128, 1280), dtype=np.float32)
    if core > 0:
        col = 0
        for ci in range(4):
            nq = 128 * (ci + 1)
            seg = np.ones((128, nq), dtype=np.float32)
            seg[:, nq - 128:nq] = mdp
            mq0[:, col:col + nq] = seg
            col += nq
    msk = np.zeros((128, 1792), dtype=np.float32)
    msk[:, 0:128] = mdp
    msk[:, 128:256] = mdc
    msk[:, 256:1536] = mq0
    return (cos4, sin4, msk.astype(ml_dtypes.bfloat16),
            np.eye(128, dtype=np.float16),
            np.ones((128, 128), dtype=ml_dtypes.bfloat16))


def make_in_maps(inputs):
    x = np.asarray(inputs["L0_attn_x"], dtype=np.float32).reshape(SEQ, D)
    wq_ = np.ascontiguousarray(np.asarray(inputs["L0_attn_wq"], dtype=np.float32))
    wk_ = np.ascontiguousarray(np.asarray(inputs["L0_attn_wk"], dtype=np.float32))
    wv_ = np.ascontiguousarray(np.asarray(inputs["L0_attn_wv"], dtype=np.float32))
    wo_ = np.ascontiguousarray(np.asarray(inputs["L0_attn_wo"], dtype=np.float32))
    gup_ = np.ascontiguousarray(np.asarray(inputs["L0_ffn_gate_up_packed"],
                                           dtype=np.int32)).view(np.uint16)
    dnp_ = np.ascontiguousarray(np.asarray(inputs["L0_ffn_down_packed"],
                                           dtype=np.int32)).view(np.uint16)
    gus_ = np.ascontiguousarray(np.asarray(inputs["L0_ffn_gate_up_scales"],
                                           dtype=np.float16))
    dns_ = np.ascontiguousarray(np.asarray(inputs["L0_ffn_down_scales"],
                                           dtype=np.float16))
    in_maps = []
    for c in range(NC):
        xs = np.zeros((T_ALL, D), dtype=np.float32)
        if c == 0:
            xs[HALO:] = x[0:T_OWN]
        else:
            lo = c * T_OWN - HALO
            xs[:] = x[lo:lo + T_ALL]
        cos4, sin4, msk, ide, one = _aux_arrays(c)
        eps = np.zeros((128, 2), dtype=np.float32)
        eps[:, 0] = EPS_LN
        eps[:, 1] = EPS_RMS
        in_maps.append(dict(x=xs, wq=wq_, wk=wk_, wv=wv_, wo=wo_,
                            gup=gup_, gus=gus_, dnp=dnp_, dns=dns_,
                            a_cos=cos4, a_sin=sin4, a_msk=msk, a_id=ide, a_one=one,
                            a_eps=eps))
    return in_maps


def kernel(**inputs):
    nc, _, _ = build_program(debug_taps=False)
    in_maps = make_in_maps(inputs)
    res = run_bass_kernel_spmd(nc, in_maps, core_ids=list(range(NC)))
    out = np.concatenate([r["y"] for r in res.results], axis=0)
    return out.reshape(1, SEQ, D).astype(np.float32)


if __name__ == "__main__":
    nc, taps, nins = build_program(debug_taps=False)
    print("instruction counts:", nins, "total:", sum(nins.values()))


# version 15 (final)
# speedup vs baseline: 1.1938x; 1.0106x over previous
import contextlib
import numpy as np

import concourse.bass as bass
import concourse.mybir as mybir
from concourse.bass_utils import run_bass_kernel_spmd

F32 = mybir.dt.float32
F16 = mybir.dt.float16
BF16 = mybir.dt.bfloat16
U16 = mybir.dt.uint16
I32 = mybir.dt.int32
U8 = mybir.dt.uint8
AF = mybir.ActivationFunctionType
ALU = mybir.AluOpType

NC = 8
T_OWN = 1024
HALO = 512
T_ALL = T_OWN + HALO
D = 2048
NT = T_ALL // 128
H, HK, DH = 16, 4, 128
IFF = 5632
CAP = 20.0
EPS_LN = 1e-5
EPS_RMS = 1e-6
SEQ = 8192

ENGS = ("sync", "gpsimd", "vector", "scalar", "tensor")
OR_ON_POOL = True


class Sem:
    def __init__(self, name):
        self.name = name
        self.h = None
        self.n = 0


class Prog:
    def __init__(self):
        self.ops = {e: [] for e in ENGS}
        self.lastw = {e: {} for e in ENGS}

    def emit(self, eng, fn, waits=(), inc=None, incn=1):
        ws = []
        for s, v in waits:
            if v <= 0:
                continue
            assert v <= s.n, f"[{eng}] waits {s.name}>={v} but only {s.n} emitted"
            if self.lastw[eng].get(s.name, -1) < v:
                ws.append((s, v))
                self.lastw[eng][s.name] = v
        if inc is not None:
            inc.n += incn
        self.ops[eng].append((ws, fn, inc, incn))

    def dma(self, eng, out, in_, sem, waits=(), **kw):
        def fn(e, o=out, i=in_, kw=kw):
            return e.dma_start(out=o, in_=i, **kw)
        self.emit(eng, fn, waits, sem, 16)

    def snap(self, sems):
        return {s: s.n for s in sems}


class _StopBuild(Exception):
    pass


def build_program(debug_taps=False, stop_after=None):
    nc = bass.Bass("TRN2", target_bir_lowering=False)

    x_in = nc.dram_tensor("x", [T_ALL, D], F32, kind="ExternalInput").ap()
    wq = nc.dram_tensor("wq", [D, D], F32, kind="ExternalInput").ap()
    wk = nc.dram_tensor("wk", [HK * DH, D], F32, kind="ExternalInput").ap()
    wv = nc.dram_tensor("wv", [HK * DH, D], F32, kind="ExternalInput").ap()
    wo = nc.dram_tensor("wo", [D, D], F32, kind="ExternalInput").ap()
    gup = nc.dram_tensor("gup", [2 * IFF, D // 4], U16, kind="ExternalInput").ap()
    gus = nc.dram_tensor("gus", [16, 2 * IFF], F16, kind="ExternalInput").ap()
    dnp = nc.dram_tensor("dnp", [D, IFF // 4], U16, kind="ExternalInput").ap()
    dns = nc.dram_tensor("dns", [44, D], F16, kind="ExternalInput").ap()
    a_cos = nc.dram_tensor("a_cos", [T_ALL, 256], F16, kind="ExternalInput").ap()
    a_sin = nc.dram_tensor("a_sin", [T_ALL, 256], F16, kind="ExternalInput").ap()
    a_msk = nc.dram_tensor("a_msk", [128, 1792], BF16, kind="ExternalInput").ap()
    a_id = nc.dram_tensor("a_id", [128, 128], F16, kind="ExternalInput").ap()
    a_one = nc.dram_tensor("a_one", [128, 128], BF16, kind="ExternalInput").ap()
    a_eps = nc.dram_tensor("a_eps", [128, 2], F32, kind="ExternalInput").ap()

    y_out = nc.dram_tensor("y", [T_OWN, D], F32, kind="ExternalOutput").ap()

    whbm = nc.dram_tensor("whbm", [2 * D + 2 * HK * DH, D], F16, kind="Internal").ap()
    gus_rep = nc.dram_tensor("gus_rep", [128, 4 * 2 * IFF], F16, kind="Internal").ap()
    dns_rep = nc.dram_tensor("dns_rep", [128, 11 * D], F16, kind="Internal").ap()
    WQ0, WK0, WV0, WO0 = 0, D, D + HK * DH, D + 2 * HK * DH

    taps = {}
    if debug_taps:
        def T(name, shape, dt):
            taps[name] = nc.dram_tensor(name, shape, dt, kind="ExternalOutput").ap()
        T("t_xnnat", [128, 16 * T_ALL], F16)
        T("t_xnpi", [128, 16 * T_OWN], F16)
        T("t_qfm", [128, H * T_OWN], F16)
        T("t_kfm", [128, HK * T_ALL], F16)
        T("t_vtm", [128, NT * 512], BF16)
        T("t_attn", [128, H * T_OWN], F16)
        T("t_h", [128, 44 * T_OWN], BF16)
        T("t_y", [128, 8 * D], BF16)
        T("t_stats", [128, 12 * 12], F32)

    K = 1024
    PERM0 = 0
    G1 = 4 * K
    G2 = 52 * K
    G3 = 84 * K
    G4 = 96 * K
    G5 = 128 * K
    G6 = 160 * K
    G7 = 172 * K
    G9 = 184 * K
    ATOT = 205 * K

    prog = Prog()
    s_ld = Sem("s_ld")
    s_gld = Sem("s_gld")
    s_pe = Sem("s_pe")
    s_dve = Sem("s_dve")
    s_act = Sem("s_act")
    s_pool = Sem("s_pool")
    s_out = Sem("s_out")
    s_gout = Sem("s_gout")
    s_ld2 = Sem("s_ld2")
    SEMS = [s_ld, s_gld, s_pe, s_dve, s_act, s_pool, s_out, s_gout, s_ld2]

    es = contextlib.ExitStack()
    with es:
        arena_t = es.enter_context(nc.sbuf_tensor("arena", [128, ATOT], U8))
        arena = arena_t[:]
        st_nat = es.enter_context(nc.sbuf_tensor("st_nat", [128, 64], F32))[:]

        def view(off, dtype, ncols):
            esz = {F32: 4, F16: 2, BF16: 2, U16: 2, I32: 4, U8: 1}[dtype]
            return arena[:, off:off + ncols * esz].bitcast(dtype)

        v_mask = view(PERM0, BF16, 1792)
        v_id = view(PERM0 + 3584, F16, 128)
        v_one = view(PERM0 + 3840, BF16, 128)
        MDP = v_mask[:, 0:128]
        MDC = v_mask[:, 128:256]
        MQ0 = v_mask[:, 256:1536]

        xn_nat = view(G1, F16, 16 * T_ALL)
        attn_fm = view(G1, F16, H * T_OWN)
        xn_pi = view(G2, F16, 16 * T_OWN)
        y_bf = view(G2, BF16, 8 * D)
        cs_cos = view(G3, F16, NT * 256)
        cs_sin = view(G3 + 6 * K, F16, NT * 256)
        wT = [view(G4, F16, 16 * 512), view(G4 + 16 * K, F16, 16 * 512)]
        p_T = view(G4, BF16, 8 * 512)
        stmp = view(G4 + 8 * K, F32, 512)
        ttmp = view(G4 + 10 * K, F32, 512)
        ssum = view(G4 + 12 * K, F32, 512)
        recip = view(G4 + 14 * K, F32, 512)
        dec_g = view(G4, F16, 16 * 512)
        dec_u = view(G4 + 16 * K, F16, 16 * 512)
        wdec = [view(G4, BF16, 16 * 512), view(G4 + 16 * K, BF16, 16 * 512)]
        ln_scratch = view(G5, F32, 2048)
        xn16 = [view(G5 + 8 * K, F16, 2048), view(G5 + 12 * K, F16, 2048)]
        q_fm = view(G5, F16, H * T_OWN)
        gtmp = view(G5, F16, 8 * 512)
        scl_g = view(G5 + 8 * K, F16, 2048)
        scl_u = view(G5 + 12 * K, F16, 2048)
        pkT_g = view(G5 + 16 * K, U16, 2048)
        pkT_u = view(G5 + 20 * K, U16, 2048)
        pkT_d = view(G5, U16, 11 * 512)
        dscl = view(G5 + 11 * K, BF16, 11 * 512)
        dscl_f16 = view(G5 + 11 * K, F16, 11 * 512)
        k_fm = view(G6, F16, HK * T_ALL)
        v_tm = view(G7, BF16, NT * 512)
        xbuf = view(G9, F32, 2048)
        qtmp = [view(G9 + 8 * K, F16, 512), view(G9 + 9 * K, F16, 512)]
        rsc_a = view(G9 + 10 * K, F16, 256)
        rsc_b = view(G9 + 10 * K + 512, F16, 256)
        rout = [view(G9 + 12 * K, F16, 512), view(G9 + 13 * K, F16, 512)]
        obuf = [view(G9 + 14 * K, F32, 512), view(G9 + 16 * K, F32, 512)]
        st_m = st_nat[:, 0:16]
        v_eps = view(G9 + 19 * K, F32, 2)
        st_sq = view(G9 + 18 * K + 64, F32, 16)
        st_rms = st_nat[:, 32:40]
        dk1 = [view(G9 + 14 * K, I32, 256), view(G9 + 15 * K, I32, 256)]
        dk2 = [view(G9 + 16 * K, I32, 256), view(G9 + 17 * K, I32, 256)]

        h_offs = ([G1 + i * 2 * K for i in range(24)]
                  + [G3 + i * 2 * K for i in range(6)]
                  + [G6 + i * 2 * K for i in range(6)]
                  + [G7 + i * 2 * K for i in range(6)]
                  + [G5 + 24 * K, G5 + 26 * K])
        assert len(h_offs) == 44

        def h_chunk(c):
            return view(h_offs[c], BF16, T_OWN)

        pb = [es.enter_context(nc.psum_tensor(f"pb{i}", [128, 512], F32)) for i in range(8)]
        tsl = []
        for b in range(2):
            v16 = pb[b][:].bitcast(F16)
            for s in range(4):
                tsl.append(v16[:, s * 128:(s + 1) * 128])

        for s in SEMS:
            s.h = es.enter_context(nc.semaphore(s.name))
        blk = es.enter_context(nc.Block())

        E = prog.emit
        DMA = prog.dma

        def _cut(name):
            if stop_after == name:
                prog.emit("sync", lambda e: None,
                          [(s_out, s_out.n), (s_gout, s_gout.n),
                           (s_ld, s_ld.n), (s_gld, s_gld.n),
                           (s_pe, s_pe.n), (s_dve, s_dve.n),
                           (s_act, s_act.n), (s_pool, s_pool.n)])
                raise _StopBuild()

        try:
            DMA("sync", v_mask, a_msk, s_ld)
            DMA("sync", v_id, a_id, s_ld)
            DMA("sync", v_one, a_one, s_ld)
            DMA("sync", v_eps, a_eps, s_ld)
            DMA("sync", cs_cos.rearrange("p (t c) -> p t c", c=256),
                a_cos.rearrange("(t p) c -> p t c", p=128), s_ld)
            DMA("sync", cs_sin.rearrange("p (t c) -> p t c", c=256),
                a_sin.rearrange("(t p) c -> p t c", p=128), s_ld)
            DMA("gpsimd", whbm[WQ0:WQ0 + D, :], wq, s_gld)
            DMA("gpsimd", whbm[WK0:WK0 + HK * DH, :], wk, s_gld)
            DMA("gpsimd", whbm[WV0:WV0 + HK * DH, :], wv, s_gld)
            DMA("gpsimd", whbm[WO0:WO0 + D, :], wo, s_gld)
            gld_wq, gld_wk, gld_wv, gld_wo = 16, 32, 48, 64
            src_g = gus.rearrange("(mb g) o -> mb g o", g=4)
            src_g = src_g.unsqueeze(2).broadcast_to([4, 4, 32, 2 * IFF])
            DMA("gpsimd", gus_rep.rearrange("(g r) (mb o) -> mb g r o", r=32, mb=4),
                src_g, s_gld)
            gld_gs = s_gld.n
            src_d = dns.rearrange("(mb g) o -> mb g o", g=4)
            src_d = src_d.unsqueeze(2).broadcast_to([11, 4, 32, D])
            DMA("gpsimd", dns_rep.rearrange("(g r) (mb o) -> mb g r o", r=32, mb=11),
                src_d, s_gld)
            gld_ds = s_gld.n

            E("vector", lambda e: e.memset(st_nat, 0.0), [], s_dve)
            memset_v = s_dve.n

            tb_pend = []
            tb_gdst = [None]
            tb_group = [0]
            tb_evacd = []

            def tb_flush():
                if not tb_pend:
                    return
                g = tb_group[0]
                tb_group[0] += 1
                bank = g % 2
                n = len(tb_pend)
                grp_waits = []
                if g >= 2:
                    grp_waits.append((s_dve, tb_evacd[g - 2]))
                for idx, (src_ap, ew) in enumerate(tb_pend):
                    slot = pb[bank][:].bitcast(F16)[:, idx * 128:(idx + 1) * 128]
                    w = list(ew) + (grp_waits if idx == 0 else [])
                    E("tensor", (lambda e, s=src_ap, o=slot: e.transpose(o, s, v_id)),
                      w, s_pe if idx == n - 1 else None)
                pev = s_pe.n
                psrc = pb[bank][:].bitcast(F16)[:, 0:n * 128].rearrange(
                    "p (n c) -> p n c", c=128)
                E("vector", (lambda e, o=tb_gdst[0], s=psrc: e.tensor_copy(o, s)),
                  [(s_pe, pev)], s_dve)
                tb_evacd.append(s_dve.n)
                tb_pend.clear()
                tb_gdst[0] = None

            def pe_transpose_group(srcs, gdst, extra_waits=()):
                assert not tb_pend
                for i, s in enumerate(srcs):
                    tb_pend.append((s, extra_waits if i == 0 else ()))
                tb_gdst[0] = gdst
                tb_flush()

            ln_pe_done = []
            xbufs = [xbuf, xbuf]
            stp = [st_nat[:, 0:16], st_nat[:, 0:16]]
            xn_last = [0]
            for ti in range(NT):
                xb = xbufs[ti % 2]
                stm = stp[ti % 2]
                waits = [(s_dve, xn_last[0])] if ti > 0 else []
                DMA("sync", xb, x_in[ti * 128:(ti + 1) * 128, :], s_ld, waits)
                ldv = s_ld.n
                E("vector", lambda e, stm=stm, xb=xb: e.reduce_sum(stm[:, 0:1], xb, axis=mybir.AxisListType.X),
                  [(s_ld, ldv)] + ([(s_dve, memset_v)] if ti == 0 else []), s_dve)
                redv = s_dve.n
                E("vector", (lambda e, xb=xb: e.tensor_tensor(out=ln_scratch, in0=xb,
                                                      in1=xb, op=ALU.mult)),
                  [(s_ld, ldv)], None)
                E("vector", lambda e, stm=stm, xb=xb: e.reduce_sum(stm[:, 12:13], ln_scratch,
                                                   axis=mybir.AxisListType.X),
                  [], s_dve)
                sqv = s_dve.n
                E("vector", lambda e, stm=stm, xb=xb: e.tensor_scalar(out=stm[:, 2:3], in0=stm[:, 0:1],
                                                      scalar1=1.0 / D, scalar2=None, op0=ALU.mult),
                  [(s_dve, redv)], s_dve)
                v1 = s_dve.n
                E("vector", lambda e, stm=stm, xb=xb: e.tensor_tensor(out=stm[:, 3:4], in0=stm[:, 2:3],
                                                      in1=stm[:, 2:3], op=ALU.mult),
                  [(s_dve, v1)], s_dve)
                v2 = s_dve.n
                E("vector", lambda e, stm=stm, xb=xb: e.tensor_scalar(out=stm[:, 4:5], in0=stm[:, 12:13],
                                                      scalar1=1.0 / D, scalar2=EPS_LN,
                                                      op0=ALU.mult, op1=ALU.add),
                  [(s_dve, sqv)], s_dve)
                v3 = s_dve.n
                E("vector", lambda e, stm=stm, xb=xb: e.tensor_tensor(out=stm[:, 5:6], in0=stm[:, 4:5],
                                                      in1=stm[:, 3:4], op=ALU.subtract),
                  [(s_dve, v2), (s_dve, v3)], s_dve)
                dvev = s_dve.n
                E("scalar", lambda e, stm=stm: e.activation(stm[:, 6:7], stm[:, 5:6], AF.Ln),
                  [(s_dve, dvev)], None)
                E("scalar", lambda e, stm=stm: e.activation(stm[:, 7:8], stm[:, 6:7], AF.Exp,
                                                   scale=-0.5), [], s_act)
                actv = s_act.n
                E("vector", lambda e, stm=stm, xb=xb: e.tensor_copy(stm[:, 8:9], stm[:, 7:8]),
                  [(s_act, actv)], s_dve)
                rcv = s_dve.n
                if debug_taps:
                    E("vector", (lambda e, o=view(G9 + 20 * K, F32, 144)[:, ti * 12:(ti + 1) * 12]:
                                 e.tensor_copy(o, stm[:, 0:12])), [], None)
                xw = [(s_dve, rcv)]
                if ti >= 2:
                    xw.append((s_pe, ln_pe_done[ti - 2]))
                xnb = xn16[ti % 2]
                mb_ = stm[:, 2:3].to_broadcast([128, D])
                rb_ = stm[:, 8:9].to_broadcast([128, D])
                E("vector", (lambda e, mb_=mb_, xb=xb: e.tensor_tensor(
                    out=xb, in0=xb, in1=mb_, op=ALU.subtract)),
                  xw, None)
                E("vector", (lambda e, xnb=xnb, rb_=rb_, xb=xb: e.tensor_tensor(
                    out=xnb, in0=xb, in1=rb_, op=ALU.mult)),
                  [], s_dve)
                xnv = s_dve.n
                xn_nat3 = xn_nat.rearrange("p (c t) -> p c t", t=T_ALL)
                for h8 in range(2):
                    srcs = [xnb[:, c * 128:(c + 1) * 128] for c in range(8 * h8, 8 * h8 + 8)]
                    gdst = xn_nat3[:, 8 * h8:8 * h8 + 8, ti * 128:(ti + 1) * 128]
                    pe_transpose_group(srcs, gdst,
                                       extra_waits=([(s_dve, xnv)] if h8 == 0 else ()))
                if ti >= 4:
                    to = ti - 4
                    xn_pi3 = xn_pi.rearrange("p (c t) -> p c t", t=T_OWN)
                    for h8 in range(2):
                        srcs = []
                        for c in range(8 * h8, 8 * h8 + 8):
                            mb, j = c // 4, c % 4
                            srcs.append(xnb[:, 512 * mb + j: 512 * mb + j + 509: 4])
                        gdst = xn_pi3[:, 8 * h8:8 * h8 + 8, to * 128:(to + 1) * 128]
                        pe_transpose_group(srcs, gdst)
                xn_last[0] = xnv
                ln_pe_done.append(s_pe.n)

            ln_end = prog.snap(SEMS)
            if debug_taps:
                DMA("sync", taps["t_xnnat"], xn_nat, s_out, [(s_dve, s_dve.n)])
                DMA("sync", taps["t_xnpi"], xn_pi, s_out, [(s_dve, s_dve.n)])
                DMA("sync", taps["t_stats"], view(G9 + 20 * K, F32, 144), s_out, [])
            _cut("ln")

            ogs = [("q", WQ0 + 512 * g, True, g) for g in range(4)]
            ogs.append(("k", WK0, True, 0))
            ogs.append(("v", WV0, False, 0))

            wt_loaded = []
            wt_lastuse = []

            def load_wT(i, row0):
                cast_gate = gld_wv if row0 >= WV0 else (gld_wk if row0 >= WK0 else gld_wq)
                waits = [(s_gld, cast_gate)]
                if i == 0:
                    pass
                if i >= 2:
                    waits.append((s_pe, wt_lastuse[i - 2]))
                buf = wT[i % 2]
                for c in range(16):
                    DMA("sync", buf[:, c * 512:(c + 1) * 512],
                        whbm[row0:row0 + 512, c * 128:(c + 1) * 128], s_ld,
                        waits if c == 0 else [], transpose=True)
                wt_loaded.append((s_ld.n, s_ld2.n))

            load_wT(0, ogs[0][1])
            mm_banks = (2, 3)
            bank_free = {2: ln_end[s_dve], 3: ln_end[s_dve]}
            rout_lastread = [0, 0]

            pend_tr = [None]

            def flush_pend_tr():
                if pend_tr[0] is None:
                    return
                srcs, gdst, rv, par = pend_tr[0]
                pend_tr[0] = None
                pe_transpose_group(srcs, gdst, extra_waits=[(s_dve, rv)])
                rout_lastread[par] = s_pe.n

            for i, (kind, row0, dorope, gidx) in enumerate(ogs):
                if i + 1 < len(ogs):
                    load_wT(i + 1, ogs[i + 1][1])
                buf = wT[i % 2]
                tlist = list(range(4, 12)) if kind == "q" else list(range(NT))
                for tj, ti in enumerate(tlist):
                    bank = mm_banks[tj % 2]
                    bw = [(s_ld, wt_loaded[i][0]), (s_ld2, wt_loaded[i][1]),
                          (s_dve, bank_free[bank]),
                          (s_dve, ln_end[s_dve]), (s_act, ln_end[s_act])]
                    for c in range(16):
                        lhs = xn_nat[:, c * T_ALL + ti * 128: c * T_ALL + (ti + 1) * 128]
                        rhs = buf[:, c * 512:(c + 1) * 512]
                        E("tensor", (lambda e, o=pb[bank][:], l=lhs, r=rhs,
                                     st=(c == 0), sp=(c == 15):
                                     e.matmul(o, lhsT=l, rhs=r, start=st, stop=sp)),
                          bw if c == 0 else [], s_pe if c == 15 else None)
                    pev = s_pe.n
                    flush_pend_tr()
                    if kind == "v":
                        dst = v_tm[:, ti * 512:(ti + 1) * 512]
                        E("vector", (lambda e, o=dst, s=pb[bank][:]: e.tensor_copy(o, s)),
                          [(s_pe, pev)], s_dve)
                        bank_free[bank] = s_dve.n
                    else:
                        qt = qtmp[tj % 2]
                        E("vector", (lambda e, o=qt, s=pb[bank][:]: e.tensor_copy(o, s)),
                          [(s_pe, pev)], s_dve)
                        bank_free[bank] = s_dve.n
                        cosb = cs_cos[:, ti * 256:(ti + 1) * 256].rearrange("p (h c) -> p h c", h=4)
                        sinb = cs_sin[:, ti * 256:(ti + 1) * 256].rearrange("p (h c) -> p h c", h=4)
                        q3 = qt.rearrange("p (h c two) -> p h c two", h=4, two=2)
                        ro = rout[tj % 2]
                        r3 = ro.rearrange("p (h c two) -> p h c two", h=4, two=2)
                        s3a = rsc_a.rearrange("p (h c) -> p h c", h=4)
                        s3b = rsc_b.rearrange("p (h c) -> p h c", h=4)
                        rw = [(s_pe, rout_lastread[tj % 2])]
                        E("vector", (lambda e, a=q3, b=cosb, o=s3a: e.tensor_tensor(
                            out=o, in0=a[:, :, :, 0], in1=b, op=ALU.mult)), rw, None)
                        E("vector", (lambda e, a=q3, b=sinb, o=s3b: e.tensor_tensor(
                            out=o, in0=a[:, :, :, 1], in1=b, op=ALU.mult)), [], None)
                        E("vector", (lambda e, a=s3a, b=s3b, o=r3: e.tensor_tensor(
                            out=o[:, :, :, 0], in0=a, in1=b, op=ALU.subtract)), [], None)
                        E("vector", (lambda e, a=q3, b=sinb, o=s3a: e.tensor_tensor(
                            out=o, in0=a[:, :, :, 0], in1=b, op=ALU.mult)), [], None)
                        E("vector", (lambda e, a=q3, b=cosb, o=s3b: e.tensor_tensor(
                            out=o, in0=a[:, :, :, 1], in1=b, op=ALU.mult)), [], None)
                        E("vector", (lambda e, a=s3a, b=s3b, o=r3: e.tensor_tensor(
                            out=o[:, :, :, 1], in0=a, in1=b, op=ALU.add)), [], s_dve)
                        rv = s_dve.n
                        srcs = [ro[:, hh * 128:(hh + 1) * 128] for hh in range(4)]
                        if kind == "q":
                            q_fm3 = q_fm.rearrange("p (c t) -> p c t", t=T_OWN)
                            gdst = q_fm3[:, 4 * gidx:4 * gidx + 4,
                                         (ti - 4) * 128:(ti - 3) * 128]
                        else:
                            k_fm3 = k_fm.rearrange("p (c t) -> p c t", t=T_ALL)
                            gdst = k_fm3[:, 0:4, ti * 128:(ti + 1) * 128]
                        pend_tr[0] = (srcs, gdst, rv, tj % 2)
                wt_lastuse.append(s_pe.n)
            flush_pend_tr()

            qkv_end = prog.snap(SEMS)
            if debug_taps:
                DMA("sync", taps["t_qfm"], q_fm, s_out, [(s_dve, s_dve.n)])
                DMA("sync", taps["t_kfm"], k_fm, s_out, [(s_dve, s_dve.n)])
                DMA("sync", taps["t_vtm"], v_tm, s_out, [(s_dve, s_dve.n)])
            _cut("qkv")

            p_T2 = [view(G4, BF16, 8 * 512), view(G4 + 8 * K, BF16, 8 * 512)]
            stmp2 = [view(G4 + 16 * K, F32, 512), view(G4 + 18 * K, F32, 512)]
            ttmp2 = [view(G4 + 20 * K, F32, 512), view(G4 + 22 * K, F32, 512)]
            ssum2 = [view(G4 + 24 * K, F32, 512), view(G4 + 26 * K, F32, 512)]
            recip2 = [view(G4 + 28 * K, F32, 512), view(G4 + 30 * K, F32, 512)]
            SC_SCALE = 1.0 / (CAP * float(np.sqrt(DH)))
            sb_free = {b: qkv_end[s_dve] for b in range(8)}
            pvread = [qkv_end[s_pe], qkv_end[s_pe]]
            sums_free = [qkv_end[s_dve], qkv_end[s_dve]]
            pv_free = [qkv_end[s_dve], qkv_end[s_dve]]
            last_tanh = [0, 0]
            mask_v = {}
            pv_stop = {}
            it_list = [(qb, hh) for qb in range(2) for hh in range(H)]

            def chunks_of(qb):
                return ([(qb * 4 + ci, 0, ci) for ci in (3, 2, 1, 0)]
                        + [(qb * 4 + 4 + ci, 1, ci) for ci in range(4)])

            def emit_front(it):
                qb, hh = it_list[it]
                par = it % 2
                kvh = hh // 4
                qbase = hh * T_OWN + qb * 512
                for cidx, (tkt, kind, ci) in enumerate(chunks_of(qb)):
                    if kind == 0:
                        c0, nq = 0, 128 * (ci + 1)
                    else:
                        c0, nq = 128 * ci, 512 - 128 * ci
                    sbk = 4 * par + cidx % 2
                    lhs = k_fm[:, kvh * T_ALL + tkt * 128: kvh * T_ALL + (tkt + 1) * 128]
                    rhs = q_fm[:, qbase + c0: qbase + c0 + nq]
                    E("tensor", (lambda e, o=pb[sbk][:, 0:nq], l=lhs, r=rhs:
                                 e.matmul(o, lhsT=l, rhs=r, start=True, stop=True,
                                          skip_group_check=True)),
                      [(s_dve, sb_free[sbk])], s_pe)
                    pev = s_pe.n
                    E("vector", (lambda e, o=stmp2[par][:, 0:nq], s=pb[sbk][:, 0:nq]:
                                 e.tensor_copy(o, s)),
                      [(s_pe, pev), (s_act, last_tanh[par])], s_dve)
                    sb_free[sbk] = s_dve.n
                    dvev = s_dve.n
                    tw = [(s_dve, dvev)]
                    if cidx == 0:
                        tw.append((s_pe, pvread[par]))
                    E("scalar", (lambda e, n=nq, par=par: e.activation(
                        ttmp2[par][:, 0:n], stmp2[par][:, 0:n], AF.Tanh, scale=SC_SCALE)),
                      tw, s_act)
                    last_tanh[par] = s_act.n
                    pslice = p_T2[par][:, cidx * 512: cidx * 512 + nq]
                    E("scalar", (lambda e, o=pslice, n=nq, par=par: e.activation(
                        o, ttmp2[par][:, 0:n], AF.Exp, scale=CAP)), [], s_act)
                    actv = s_act.n
                    if kind == 0 and qb == 0:
                        mseg = MQ0[:, 64 * ci * (ci + 1): 64 * ci * (ci + 1) + nq]
                        E("vector", (lambda e, o=pslice, m=mseg: e.tensor_tensor(
                            out=o, in0=o, in1=m, op=ALU.mult)), [(s_act, actv)], s_dve)
                    elif kind == 0:
                        dseg = pslice[:, nq - 128:nq]
                        E("vector", (lambda e, o=dseg: e.tensor_tensor(
                            out=o, in0=o, in1=MDP, op=ALU.mult)), [(s_act, actv)], s_dve)
                    else:
                        dseg = pslice[:, 0:128]
                        E("vector", (lambda e, o=dseg: e.tensor_tensor(
                            out=o, in0=o, in1=MDC, op=ALU.mult)), [(s_act, actv)], s_dve)
                    mask_v[(it, cidx)] = s_dve.n

            def emit_back(it):
                qb, hh = it_list[it]
                par = it % 2
                kvh = hh // 4
                qbase = hh * T_OWN + qb * 512
                bsum, bpv = 2 + 4 * par, 3 + 4 * par
                for cidx, (tkt, kind, ci) in enumerate(chunks_of(qb)):
                    if kind == 0:
                        c0, nq = 0, 128 * (ci + 1)
                    else:
                        c0, nq = 128 * ci, 512 - 128 * ci
                    pslice = p_T2[par][:, cidx * 512: cidx * 512 + nq]
                    stw = [(s_dve, mask_v[(it, cidx)])]
                    if cidx == 0:
                        stw += [(s_dve, sums_free[par]), (s_dve, pv_free[par])]
                    E("tensor", (lambda e, r=pslice, st=(cidx == 0), c0=c0, nq=nq, b=bsum:
                                 e.matmul(pb[b][:, c0:c0 + nq], lhsT=v_one, rhs=r,
                                          start=st, stop=False, skip_group_check=True)),
                      stw, None)
                    vsl = v_tm[:, tkt * 512 + kvh * 128: tkt * 512 + (kvh + 1) * 128]
                    E("tensor", (lambda e, r=pslice, l=vsl, st=(cidx == 0), c0=c0, nq=nq, b=bpv:
                                 e.matmul(pb[b][:, c0:c0 + nq], lhsT=l, rhs=r,
                                          start=st, stop=(cidx == 7), skip_group_check=True)),
                      [], s_pe if cidx == 7 else None)
                pv_stop[it] = s_pe.n
                pvread[par] = s_pe.n
                E("vector", (lambda e, par=par, b=bsum: e.tensor_copy(ssum2[par], pb[b][:])),
                  [(s_pe, pv_stop[it])], s_dve)
                sums_free[par] = s_dve.n
                dvev = s_dve.n
                E("scalar", (lambda e, par=par: e.activation(recip2[par], ssum2[par], AF.Ln)),
                  [(s_dve, dvev)], None)
                E("scalar", (lambda e, par=par: e.activation(recip2[par], recip2[par],
                                                             AF.Exp, scale=-1.0)), [], s_act)
                actv = s_act.n
                dst = attn_fm[:, qbase: qbase + 512]
                E("vector", (lambda e, o=dst, par=par, b=bpv: e.tensor_tensor(
                    out=o, in0=pb[b][:], in1=recip2[par], op=ALU.mult)),
                  [(s_act, actv)], s_dve)
                pv_free[par] = s_dve.n

            for it in range(len(it_list) + 1):
                if it < len(it_list):
                    emit_front(it)
                if it >= 1:
                    emit_back(it - 1)

            attn_end = prog.snap(SEMS)
            if debug_taps:
                DMA("sync", taps["t_attn"], attn_fm, s_out, [(s_dve, s_dve.n)])
            _cut("attn")

            wt_loaded2 = []
            wt_lastuse2 = []

            def load_wT2(i):
                waits = [(s_gld, gld_wo)]
                if i == 0:
                    waits += [(s_dve, attn_end[s_dve]), (s_act, attn_end[s_act]),
                              (s_pe, attn_end[s_pe])]
                if i >= 2:
                    waits.append((s_pe, wt_lastuse2[i - 2]))
                buf = wT[i % 2]
                for c in range(16):
                    DMA("sync", buf[:, c * 512:(c + 1) * 512],
                        whbm[WO0 + 512 * i:WO0 + 512 * i + 512, c * 128:(c + 1) * 128],
                        s_ld, waits if c == 0 else [], transpose=True)
                wt_loaded2.append((s_ld.n, s_ld2.n))

            load_wT2(0)
            bank_free2 = {2: attn_end[s_dve], 3: attn_end[s_dve]}
            obuf_ofree = [0, 0]

            for og in range(4):
                if og + 1 < 4:
                    load_wT2(og + 1)
                buf = wT[og % 2]
                for ti in range(8):
                    DMA("sync", xbuf[:, 0:512],
                        x_in[HALO + ti * 128: HALO + (ti + 1) * 128, og * 512:(og + 1) * 512],
                        s_ld, [(s_dve, s_dve.n)])
                    xldv = s_ld.n
                    bank = mm_banks[ti % 2]
                    bw = [(s_ld, wt_loaded2[og][0]), (s_ld2, wt_loaded2[og][1]),
                          (s_dve, bank_free2[bank])]
                    for c in range(16):
                        lhs = attn_fm[:, c * T_OWN + ti * 128: c * T_OWN + (ti + 1) * 128]
                        rhs = buf[:, c * 512:(c + 1) * 512]
                        E("tensor", (lambda e, o=pb[bank][:], l=lhs, r=rhs,
                                     st=(c == 0), sp=(c == 15):
                                     e.matmul(o, lhsT=l, rhs=r, start=st, stop=sp)),
                          bw if c == 0 else [], s_pe if c == 15 else None)
                    pev = s_pe.n
                    ob = obuf[ti % 2]
                    E("vector", (lambda e, o=ob, b=pb[bank][:]: e.tensor_tensor(
                        out=o, in0=b, in1=xbuf[:, 0:512], op=ALU.add)),
                      [(s_pe, pev), (s_ld, xldv), (s_out, obuf_ofree[ti % 2])], s_dve)
                    bank_free2[bank] = s_dve.n
                    dvev = s_dve.n
                    DMA("sync", y_out[ti * 128:(ti + 1) * 128, og * 512:(og + 1) * 512],
                        ob, s_out, [(s_dve, dvev)])
                    obuf_ofree[ti % 2] = s_out.n
                wt_lastuse2.append(s_pe.n)
            out_plain_all = s_out.n
            wo_end = prog.snap(SEMS)
            _cut("wo")

            def emit_decode_planes(items):
                done = [None] * len(items)
                orv = [0] * len(items)

                def front(i):
                    src32, sc, dst, dtype, j, w0 = items[i]
                    par = i % 2
                    s1, s2 = dk1[par], dk2[par]
                    if dtype == F16:
                        sh_m, msk_m = 9 - 4 * j, 0x0E000E00
                    else:
                        sh_m, msk_m = 6 - 4 * j, 0x01C001C0
                    op_m = ALU.logical_shift_left if sh_m >= 0 else ALU.logical_shift_right
                    E("vector", (lambda e, s=src32, o=s1, sh=abs(sh_m), m=msk_m, op=op_m:
                                 e.tensor_scalar(out=o, in0=s, scalar1=sh, scalar2=m,
                                                 op0=op, op1=ALU.bitwise_and)), list(w0), None)
                    E("vector", (lambda e, s=src32, o=s2, sh=12 - 4 * j:
                                 e.tensor_scalar(out=o, in0=s, scalar1=sh,
                                                 scalar2=-2147450880,
                                                 op0=ALU.logical_shift_left,
                                                 op1=ALU.bitwise_and)),
                      [], s_dve)
                    t2v = s_dve.n
                    if OR_ON_POOL:
                        E("gpsimd", (lambda e, a=s1, b=s2: e.tensor_tensor(
                            out=a, in0=a, in1=b, op=ALU.add)), [(s_dve, t2v)], s_pool)
                        orv[i] = s_pool.n
                    else:
                        E("vector", (lambda e, a=s1, b=s2: e.tensor_tensor(
                            out=a, in0=a, in1=b, op=ALU.bitwise_or)), [], None)

                def back(i):
                    src32, sc, dst, dtype, j, w0 = items[i]
                    par = i % 2
                    s1 = dk1[par]
                    w = [(s_pool, orv[i])] if OR_ON_POOL else []
                    E("vector", (lambda e, o=dst, a=s1.bitcast(dtype), b=sc:
                                 e.tensor_tensor(out=o, in0=a, in1=b, op=ALU.mult)),
                      w, s_dve)
                    done[i] = s_dve.n

                for i in range(len(items)):
                    front(i)
                    if i >= 1:
                        back(i - 1)
                back(len(items) - 1)
                return done

            def emit_decode(pkbuf, sclbuf, decbuf, nmc, dtype, waits_first):
                items = []
                for mc in range(nmc):
                    src32 = pkbuf[:, mc * 512:(mc + 1) * 512].bitcast(I32)
                    sc = sclbuf[:, mc * 512:(mc + 1) * 512]
                    for j in range(4):
                        pl = 4 * mc + j
                        dst = decbuf[:, pl * 512:(pl + 1) * 512]
                        items.append((src32, sc, dst, dtype, j,
                                      list(waits_first) if pl == 0 else []))
                return emit_decode_planes(items)

            h_mult_done = wo_end[s_dve]
            pk_war_dve = wo_end[s_dve]
            g5_war_dve = wo_end[s_dve]
            prev_up_read = wo_end[s_dve]

            for og in range(11):
                pkw = [(s_dve, pk_war_dve)]
                if og == 0:
                    pkw += [(s_out, out_plain_all), (s_pe, attn_end[s_pe])]
                for mc in range(4):
                    DMA("sync", pkT_g[:, mc * 512:(mc + 1) * 512],
                        gup[512 * og:512 * (og + 1), mc * 128:(mc + 1) * 128], s_ld,
                        pkw if mc == 0 else [], transpose=True)
                for mc in range(4):
                    DMA("sync", pkT_u[:, mc * 512:(mc + 1) * 512],
                        gup[IFF + 512 * og:IFF + 512 * (og + 1), mc * 128:(mc + 1) * 128],
                        s_ld, [], transpose=True)
                pk_ld = s_ld.n
                pk_ld2 = s_ld2.n
                for half, (sbuf_, col0) in enumerate([(scl_g, 512 * og), (scl_u, IFF + 512 * og)]):
                    srcs = gus_rep.rearrange("p (mb o) -> p mb o", mb=4)[:, :, col0:col0 + 512]
                    DMA("sync", sbuf_.rearrange("p (mb o) -> p mb o", mb=4), srcs, s_ld,
                        ([(s_dve, pk_war_dve), (s_gld, gld_gs)] if half == 0 else []))
                scl_ld = s_ld.n
                E("vector", lambda e: e.tensor_scalar(out=scl_g, in0=scl_g, scalar1=16384.0,
                                                      scalar2=None, op0=ALU.mult),
                  [(s_ld, scl_ld)], s_dve)
                E("vector", lambda e: e.tensor_scalar(out=scl_u, in0=scl_u, scalar1=16384.0,
                                                      scalar2=None, op0=ALU.mult), [], s_dve)
                scl_rdy = s_dve.n
                dg = emit_decode(pkT_g, scl_g, dec_g, 4, F16,
                                 [(s_ld, pk_ld), (s_dve, scl_rdy), (s_dve, g5_war_dve)])
                du = emit_decode(pkT_u, scl_u, dec_u, 4, F16, [(s_ld2, pk_ld2)])
                pk_war_dve = du[-1]
                gate_stops = [0] * 8
                for jh in range(4):
                    for th in range(2):
                        b8 = jh * 2 + th
                        bk = pb[b8]
                        for kc in range(16):
                            lhs = dec_g[:, kc * 512 + jh: kc * 512 + jh + 509: 4]
                            rhs = xn_pi[:, kc * T_OWN + th * 512: kc * T_OWN + (th + 1) * 512]
                            wts = [(s_dve, dg[kc])]
                            if jh == 0 and th == 0 and kc == 0:
                                wts.append((s_dve, h_mult_done))
                            E("tensor", (lambda e, o=bk[:], l=lhs, r=rhs,
                                         st=(kc == 0), sp=(kc == 15):
                                         e.matmul(o, lhsT=l, rhs=r, start=st, stop=sp)),
                              wts, s_pe if kc == 15 else None)
                        gate_stops[b8] = s_pe.n
                gpe_b = gate_stops
                gev_b = [0] * 8
                silv_b = [0] * 8
                for t8 in range(8):
                    E("vector", (lambda e, o=gtmp[:, t8 * 512:(t8 + 1) * 512], s=pb[t8][:]:
                                 e.tensor_copy(o, s)), [(s_pe, gpe_b[t8])], s_dve)
                    gev_b[t8] = s_dve.n
                    E("scalar", (lambda e, o=gtmp[:, t8 * 512:(t8 + 1) * 512]:
                                 e.activation(o, o, AF.Silu)), [(s_dve, gev_b[t8])], s_act)
                    silv_b[t8] = s_act.n
                g5_war_dve = gev_b[7]
                upe_b = [0] * 8
                for jh in range(4):
                    for th in range(2):
                        b8 = jh * 2 + th
                        bk = pb[b8]
                        for kc in range(16):
                            lhs = dec_u[:, kc * 512 + jh: kc * 512 + jh + 509: 4]
                            rhs = xn_pi[:, kc * T_OWN + th * 512: kc * T_OWN + (th + 1) * 512]
                            wts = [(s_dve, du[kc])]
                            if kc == 0:
                                wts.append((s_dve, gev_b[b8]))
                            E("tensor", (lambda e, o=bk[:], l=lhs, r=rhs,
                                         st=(kc == 0), sp=(kc == 15):
                                         e.matmul(o, lhsT=l, rhs=r, start=st, stop=sp)),
                              wts, s_pe if kc == 15 else None)
                        upe_b[b8] = s_pe.n
                for jh in range(4):
                    for th in range(2):
                        b8 = jh * 2 + th
                        hc = h_chunk(4 * og + jh)
                        dst = hc[:, th * 512:(th + 1) * 512]
                        sil = gtmp[:, b8 * 512:(b8 + 1) * 512]
                        E("vector", (lambda e, o=dst, a=sil, b=pb[b8][:]:
                                     e.tensor_tensor(out=o, in0=a, in1=b, op=ALU.mult)),
                          [(s_pe, upe_b[b8]), (s_act, silv_b[b8])], s_dve)
                h_mult_done = s_dve.n

            h_done = s_dve.n
            if debug_taps:
                for c in range(44):
                    DMA("sync", taps["t_h"][:, c * T_OWN:(c + 1) * T_OWN], h_chunk(c),
                        s_out, [(s_dve, h_done)] if c == 0 else [])
            _cut("gate")

            dgroups = [(0, 4), (4, 4), (8, 3)]
            wdec_lastmm = [wo_end[s_pe], wo_end[s_pe]]
            down_bank_free = [h_done] * 8
            pk_war_d = h_done
            ydone = 0

            for nck in range(4):
                pw = [(s_dve, pk_war_d)]
                if nck == 0:
                    pw.append((s_act, s_act.n))
                for mc in range(11):
                    DMA("sync", pkT_d[:, mc * 512:(mc + 1) * 512],
                        dnp[512 * nck:512 * (nck + 1), mc * 128:(mc + 1) * 128],
                        s_ld, pw if mc == 0 else [], transpose=True)
                pk_ld = s_ld.n
                pk_ld2d = s_ld2.n
                srcs = dns_rep.rearrange("p (mb o) -> p mb o", mb=11)[:, :, 512 * nck:512 * (nck + 1)]
                DMA("sync", dscl_f16.rearrange("p (mb o) -> p mb o", mb=11), srcs, s_ld,
                    [(s_dve, pk_war_d), (s_gld, gld_ds)])
                dsl = s_ld.n
                E("vector", lambda e: e.tensor_scalar(out=dscl, in0=dscl_f16,
                                                      scalar1=float(2.0 ** 126),
                                                      scalar2=None, op0=ALU.mult),
                  [(s_ld, dsl)], s_dve)
                dscl_rdy = s_dve.n
                grp_done_pe = []
                for gi, (mb0, nmb) in enumerate(dgroups):
                    gbuf = wdec[gi % 2]
                    ditems = []
                    for mci in range(nmb):
                        mc = mb0 + mci
                        src32 = pkT_d[:, mc * 512:(mc + 1) * 512].bitcast(I32)
                        sc = dscl[:, mc * 512:(mc + 1) * 512]
                        for j in range(4):
                            pl = 4 * mci + j
                            dst = gbuf[:, pl * 512:(pl + 1) * 512]
                            w0 = []
                            if pl == 0:
                                w0 = [(s_ld, pk_ld), (s_ld2, pk_ld2d), (s_dve, dscl_rdy),
                                      (s_pe, wdec_lastmm[gi % 2])]
                            ditems.append((src32, sc, dst, BF16, j, w0))
                    done = emit_decode_planes(ditems)
                    pk_war_d = done[-1]
                    for ti in range(8):
                        for pli in range(4 * nmb):
                            mc = mb0 + pli // 4
                            j = pli % 4
                            hc = h_chunk(4 * mc + j)
                            lhs = hc[:, ti * 128:(ti + 1) * 128]
                            rhs = gbuf[:, pli * 512:(pli + 1) * 512]
                            st = (gi == 0 and pli == 0)
                            sp = (gi == 2 and pli == 4 * nmb - 1)
                            last_of_grp = (ti == 7 and pli == 4 * nmb - 1)
                            wts = []
                            if pli == 0:
                                wts = [(s_dve, done[-1]), (s_dve, h_done)]
                                if gi == 0 and ti < 8:
                                    wts.append((s_dve, down_bank_free[ti]))
                            E("tensor", (lambda e, o=pb[ti][:], l=lhs, r=rhs, st=st, sp=sp:
                                         e.matmul(o, lhsT=l, rhs=r, start=st, stop=sp,
                                                  skip_group_check=True)),
                              wts, s_pe if (sp or last_of_grp) else None)
                    wdec_lastmm[gi % 2] = s_pe.n
                dpe = s_pe.n
                for ti in range(8):
                    dst = y_bf[:, ti * D + nck * 512: ti * D + (nck + 1) * 512]
                    E("vector", (lambda e, o=dst, s=pb[ti][:]: e.tensor_copy(o, s)),
                      [(s_pe, dpe)], s_dve)
                    down_bank_free[ti] = s_dve.n
                ydone = s_dve.n

            if debug_taps:
                DMA("sync", taps["t_y"], y_bf, s_out, [(s_dve, ydone)])

            obuf_gfree = [0, 0]
            rms_read = 0
            for ti in range(8):
                E("vector", (lambda e, i=y_bf[:, ti * D:(ti + 1) * D]:
                             e.tensor_tensor(out=xbuf, in0=i, in1=i, op=ALU.mult)),
                  [(s_dve, ydone), (s_dve, rms_read)], None)
                E("vector", lambda e: e.reduce_sum(st_rms[:, 7:8], xbuf,
                                                   axis=mybir.AxisListType.X),
                  [], s_dve)
                sqv = s_dve.n
                E("vector", lambda e: e.tensor_scalar(out=st_rms[:, 1:2], in0=st_rms[:, 7:8],
                                                      scalar1=1.0 / D, scalar2=EPS_RMS,
                                                      op0=ALU.mult, op1=ALU.add),
                  [(s_dve, sqv)], s_dve)
                rms_read = s_dve.n
                E("scalar", lambda e: e.activation(st_rms[:, 2:3], st_rms[:, 1:2], AF.Ln),
                  [(s_dve, rms_read)], None)
                E("scalar", lambda e: e.activation(st_rms[:, 3:4], st_rms[:, 2:3], AF.Exp,
                                                   scale=-0.5), [], s_act)
                rsv = s_act.n
                E("vector", lambda e: e.tensor_copy(st_rms[:, 4:5], st_rms[:, 3:4]),
                  [(s_act, rsv)], s_dve)
                rcv2 = s_dve.n
                rb2 = st_rms[:, 4:5].to_broadcast([128, 512])
                for ncx in range(4):
                    par = ncx % 2
                    ob = obuf[par]
                    E("vector", (lambda e, o=ob, rb2=rb2, i=y_bf[:, ti * D + ncx * 512:
                                                        ti * D + (ncx + 1) * 512]:
                                 e.tensor_tensor(out=o, in0=i, in1=rb2, op=ALU.mult)),
                      [(s_gout, obuf_gfree[par]), (s_dve, rcv2)], s_dve)
                    dv2 = s_dve.n
                    DMA("gpsimd", y_out[ti * 128:(ti + 1) * 128, ncx * 512:(ncx + 1) * 512],
                        ob, s_gout, [(s_dve, dv2), (s_out, out_plain_all)],
                        accum_op=ALU.add)
                    obuf_gfree[par] = s_gout.n

            prog.emit("sync", lambda e: None,
                      [(s_out, s_out.n), (s_gout, s_gout.n),
                       (s_ld, s_ld.n), (s_gld, s_gld.n), (s_ld2, s_ld2.n)])

        except _StopBuild:
            pass

        def replay(engobj, oplist):
            for ws, fn, inc, incn in oplist:
                for s, v in ws:
                    engobj.wait_ge(s.h, v)
                ins = fn(engobj)
                if inc is not None:
                    ins.then_inc(inc.h, incn)

        @blk.sync
        def _(e):
            replay(e, prog.ops["sync"])

        @blk.gpsimd
        def _(e):
            replay(e, prog.ops["gpsimd"])

        @blk.vector
        def _(e):
            replay(e, prog.ops["vector"])

        @blk.scalar
        def _(e):
            replay(e, prog.ops["scalar"])

        @blk.tensor
        def _(e):
            replay(e, prog.ops["tensor"])

    nins = {e: len(prog.ops[e]) for e in ENGS}
    return nc, taps, nins


def _aux_arrays(core):
    import ml_dtypes
    pos = np.arange(core * T_OWN - HALO, (core + 1) * T_OWN, dtype=np.float64)
    pos = np.maximum(pos, 0.0)
    inv = 1.0 / (10000.0 ** (np.arange(0, DH, 2, dtype=np.float64) / DH))
    ang = pos[:, None] * inv[None, :]
    cos4 = np.tile(np.cos(ang), (1, 4)).astype(np.float16)
    sin4 = np.tile(np.sin(ang), (1, 4)).astype(np.float16)
    p = np.arange(128)[:, None]
    a = np.arange(128)[None, :]
    mdp = (p >= a).astype(np.float32)
    mdc = (p <= a).astype(np.float32)
    mq0 = np.zeros((128, 1280), dtype=np.float32)
    if core > 0:
        col = 0
        for ci in range(4):
            nq = 128 * (ci + 1)
            seg = np.ones((128, nq), dtype=np.float32)
            seg[:, nq - 128:nq] = mdp
            mq0[:, col:col + nq] = seg
            col += nq
    msk = np.zeros((128, 1792), dtype=np.float32)
    msk[:, 0:128] = mdp
    msk[:, 128:256] = mdc
    msk[:, 256:1536] = mq0
    return (cos4, sin4, msk.astype(ml_dtypes.bfloat16),
            np.eye(128, dtype=np.float16),
            np.ones((128, 128), dtype=ml_dtypes.bfloat16))


def make_in_maps(inputs):
    x = np.asarray(inputs["L0_attn_x"], dtype=np.float32).reshape(SEQ, D)
    wq_ = np.ascontiguousarray(np.asarray(inputs["L0_attn_wq"], dtype=np.float32))
    wk_ = np.ascontiguousarray(np.asarray(inputs["L0_attn_wk"], dtype=np.float32))
    wv_ = np.ascontiguousarray(np.asarray(inputs["L0_attn_wv"], dtype=np.float32))
    wo_ = np.ascontiguousarray(np.asarray(inputs["L0_attn_wo"], dtype=np.float32))
    gup_ = np.ascontiguousarray(np.asarray(inputs["L0_ffn_gate_up_packed"],
                                           dtype=np.int32)).view(np.uint16)
    dnp_ = np.ascontiguousarray(np.asarray(inputs["L0_ffn_down_packed"],
                                           dtype=np.int32)).view(np.uint16)
    gus_ = np.ascontiguousarray(np.asarray(inputs["L0_ffn_gate_up_scales"],
                                           dtype=np.float16))
    dns_ = np.ascontiguousarray(np.asarray(inputs["L0_ffn_down_scales"],
                                           dtype=np.float16))
    in_maps = []
    for c in range(NC):
        xs = np.zeros((T_ALL, D), dtype=np.float32)
        if c == 0:
            xs[HALO:] = x[0:T_OWN]
        else:
            lo = c * T_OWN - HALO
            xs[:] = x[lo:lo + T_ALL]
        cos4, sin4, msk, ide, one = _aux_arrays(c)
        eps = np.zeros((128, 2), dtype=np.float32)
        eps[:, 0] = EPS_LN
        eps[:, 1] = EPS_RMS
        in_maps.append(dict(x=xs, wq=wq_, wk=wk_, wv=wv_, wo=wo_,
                            gup=gup_, gus=gus_, dnp=dnp_, dns=dns_,
                            a_cos=cos4, a_sin=sin4, a_msk=msk, a_id=ide, a_one=one,
                            a_eps=eps))
    return in_maps


def kernel(**inputs):
    nc, _, _ = build_program(debug_taps=False)
    in_maps = make_in_maps(inputs)
    res = run_bass_kernel_spmd(nc, in_maps, core_ids=list(range(NC)))
    out = np.concatenate([r["y"] for r in res.results], axis=0)
    return out.reshape(1, SEQ, D).astype(np.float32)


if __name__ == "__main__":
    nc, taps, nins = build_program(debug_taps=False)
    print("instruction counts:", nins, "total:", sum(nins.values()))
